# version 09
# baseline (speedup 1.0000x reference)
import numpy as np
import ml_dtypes
import concourse.bass as bass
import concourse.mybir as mybir
from concourse.bass_utils import run_bass_kernel_spmd

F32 = mybir.dt.float32
BF16 = mybir.dt.bfloat16
I32 = mybir.dt.int32
AF = mybir.ActivationFunctionType
ALU = mybir.AluOpType
AX = mybir.AxisListType.X

B = 2
TG = 8192
T = 2048
CH = 1024
D = 1024
DC = 8
H = 16
DH = 64
DI = 2048
DHI = 16
DS = 16
DTR = 64
NL = 4
E = 8
DF = 4096
EPS = 1e-5
QT = 512
bf = ml_dtypes.bfloat16

ACC_ENG = "gpsimd"
BUILD_UPTO = 4
DETECT_RACES = True



def _weight_layout(inp):
    parts, off, pos = [], {}, 0

    def add(name, arr):
        nonlocal pos
        a = np.ascontiguousarray(arr, dtype=np.float32)
        off[name] = (pos, a.shape)
        parts.append(a.ravel())
        pos += a.size

    def addt(name, arr, mw=128):
        nonlocal pos
        a = np.ascontiguousarray(arr, dtype=np.float32)
        K, M = a.shape
        nm = M // mw
        t = a.reshape(K // 128, 128, nm, mw).transpose(2, 1, 0, 3)
        off[name] = (pos, (K, M, mw, "t"))
        parts.append(np.ascontiguousarray(t).ravel())
        pos += a.size

    addt("W1T", inp["L0_attn_a_W1"].T)
    addt("W2T", inp["L0_attn_a_W2"].T)
    addt("gate", inp["L0_ffn_gate_w"], mw=8)
    for e in range(E):
        addt(f"fc{e}", inp["L0_ffn_fc_w"][e])
    for e in range(E):
        addt(f"pj{e}", inp["L0_ffn_proj_w"][e])
    for l in range(NL):
        addt(f"in{l}", inp["L0_attn_b_in_proj_w"][l].T)
    for l in range(NL):
        addt(f"xp{l}", inp["L0_attn_b_x_proj_w"][l].T, mw=96)
    for l in range(NL):
        add(f"dtp{l}", inp["L0_attn_b_dt_proj_w"][l].T)
    for l in range(NL):
        addt(f"op{l}", inp["L0_attn_b_out_proj_w"][l].T)
    flat = np.concatenate(parts)
    pad = (-flat.size) % 8192
    if pad:
        flat = np.concatenate([flat, np.zeros(pad, np.float32)])
    return flat, off


def _smalls_layout(inp):
    cols, off = [], {}

    def addc(name, arr2d):
        off[name] = (sum(c.shape[1] for c in cols), arr2d.shape[1])
        cols.append(np.ascontiguousarray(arr2d, np.float32))

    for l in range(NL):
        addc(f"nw{l}", inp["L0_attn_b_norm_w"][l].reshape(DC, 128).T)
        addc(f"nb{l}", inp["L0_attn_b_norm_b"][l].reshape(DC, 128).T)
    addc("nfw", inp["L0_attn_b_normf_w"].reshape(DC, 128).T)
    addc("nfb", inp["L0_attn_b_normf_b"].reshape(DC, 128).T)
    for l in range(NL):
        for k in range(4):
            addc(f"cw{l}_{k}", inp["L0_attn_b_conv_w"][l, :, k].reshape(DHI, 128).T)
        addc(f"cb{l}", inp["L0_attn_b_conv_b"][l].reshape(DHI, 128).T)
        addc(f"dtb{l}", inp["L0_attn_b_dt_proj_b"][l].reshape(DHI, 128).T)
        addc(f"dsk{l}", inp["L0_attn_b_D_skip"][l].reshape(DHI, 128).T)
        al = inp["L0_attn_b_A_log"][l].reshape(DHI, 128, DS)
        addc(f"alog{l}", np.transpose(al, (1, 0, 2)).reshape(128, DHI * DS))
    addc("ceps", np.full((128, 1), EPS, np.float32))
    addc("cm05", np.full((128, 1), -0.5, np.float32))
    addc("c0125", np.full((128, 1), 0.125, np.float32))
    addc("cm1", np.full((128, 1), -1.0, np.float32))
    addc("cone", np.full((128, 1), 1.0, np.float32))
    for nm in ("ohA", "ohB", "ohPA", "ohPB"):
        addc(nm, np.zeros((128, 8), np.float32))
    packed = np.concatenate(cols, axis=1)
    return np.ascontiguousarray(packed), off


def _core_smalls(smalls, soff, jj):
    s = smalls.copy()

    def setcol(name, k):
        if 0 <= k < 8:
            s[:, soff[name][0] + k] = 1.0
    setcol("ohA", jj)
    setcol("ohB", 7 - jj)
    setcol("ohPA", jj - 1)
    setcol("ohPB", 6 - jj)
    return s


def _rope_dup(chunks):
    hs = DH
    inv = 1.0 / (10000.0 ** (np.arange(0, hs, 2, dtype=np.float64) / hs))
    pos = np.concatenate([np.arange(c * CH, (c + 1) * CH) for c in chunks])
    ang = pos[:, None].astype(np.float64) * inv[None, :]
    cos = np.concatenate([np.cos(ang), np.cos(ang)], axis=1).T
    sin = np.concatenate([np.sin(ang), np.sin(ang)], axis=1).T
    out = np.zeros((128, 2 * T), np.float32)
    out[0:64, 0:T] = cos
    out[64:128, 0:T] = cos
    out[0:64, T:] = sin
    out[64:128, T:] = sin
    return out


def _tri_masks():
    m = np.zeros((128, 4 * QT), np.float32)
    kk = np.arange(128)[:, None]
    qq = np.arange(QT)[None, :]
    for i in range(4):
        m[:, i * QT:(i + 1) * QT] = (i * 128 + kk <= qq)
    return m.astype(bf)


def _rot_perm():
    p = np.zeros((64, 64), np.float32)
    for jj in range(32):
        p[jj + 32, jj] = -1.0
        p[jj, jj + 32] = 1.0
    return np.vstack([p, p]).astype(bf)


def _sel16():
    s = np.zeros((16, 16 * 128), np.float32)
    for n in range(16):
        s[n, n * 128:(n + 1) * 128] = 1.0
    return s.astype(bf)



class Prog:
    ENGS = ("sync", "vector", "scalar", "tensor", "gpsimd")

    def __init__(self):
        self.steps = []
        self.done_at = []
        self.counts = {e: [0, 0] for e in self.ENGS}
        self.phase = "init"
        self.phases = []

    def step(self, eng, deps=None, n_inst=1, n_dma=0):
        sid = len(self.steps)
        if deps is None:
            deps = [sid - 1] if sid > 0 else []
        deps = [d for d in deps if d is not None and d >= 0]

        def reg(fn):
            self.steps.append((eng, fn, list(deps), n_inst, n_dma))
            self.phases.append(self.phase)
            c = self.counts[eng]
            c[0] += n_inst
            c[1] += n_dma
            da = {}
            if n_inst:
                da[eng + "_i"] = c[0]
            if n_dma:
                da[eng + "_d"] = 16 * c[1]
            if not da:
                da[eng + "_i"] = c[0]
            self.done_at.append(da)
            return len(self.steps) - 1
        return reg

    def last(self):
        return len(self.steps) - 1

    def emit(self, block, sems, nc=None):
        per_eng = {e: [] for e in self.ENGS}
        for sid, st in enumerate(self.steps):
            per_eng[st[0]].append((sid,) + st[1:])
        from contextlib import nullcontext

        def make(eng):
            items = per_eng[eng]

            def body(handle):
                for sid, fn, deps, n_inst, n_dma in items:
                    need = {}
                    for d in deps:
                        for sk, dv in self.done_at[d].items():
                            need[sk] = max(need.get(sk, 0), dv)
                    for sk in sorted(need):
                        handle.wait_ge(sems[sk], need[sk])
                    with (nc.named_scope(self.phases[sid]) if nc is not None
                          else nullcontext()):
                        marks = fn(handle)
                    if marks is None:
                        marks = []
                    elif isinstance(marks, tuple) and len(marks) == 2 and marks[0] == "dma":
                        marks = [marks]
                    elif not isinstance(marks, (list, tuple)):
                        marks = [marks]
                    gi = gd = 0
                    for m in marks:
                        if isinstance(m, tuple) and m[0] == "dma":
                            m[1].then_inc(sems[eng + "_d"], 16)
                            gd += 1
                        else:
                            m.then_inc(sems[eng + "_i"], 1)
                            gi += 1
                    assert gi == n_inst and gd == n_dma, \
                        f"step {sid} {eng}: declared ({n_inst},{n_dma}) got ({gi},{gd})"
            return body

        for eng in self.ENGS:
            getattr(block, eng)(make(eng))



def build_program(meta):
    nc = bass.Bass("TRN2", target_bir_lowering=False, debug=False,
                   detect_race_conditions=DETECT_RACES)
    S = meta["S"]; S8 = S // 8
    NSM = meta["NSM"]
    woff = meta["woff"]; soff = meta["soff"]

    PC = DH * T
    KVN = H * 3 * PC
    AOB = 4 * DH * T
    AON = 4 * AOB

    xin = nc.dram_tensor("xin", [D, T], F32, kind="ExternalInput").ap()
    wsl = nc.dram_tensor("wsl", [1, S8], F32, kind="ExternalInput").ap()
    smalls_in = nc.dram_tensor("smalls", [128, NSM], F32, kind="ExternalInput").ap()
    rope_in = nc.dram_tensor("rope", [128, 2 * T], F32, kind="ExternalInput").ap()
    dyn_in = nc.dram_tensor("dyn", [1, 8], I32, kind="ExternalInput").ap()
    out_ext = nc.dram_tensor("out", [DC * 128, T], F32, kind="ExternalOutput").ap()

    wbf_sl = nc.dram_tensor("wbf_sl", [1, S8], BF16)
    wbf = nc.dram_tensor("wbf", [8, S8], BF16)
    kv_loc = nc.dram_tensor("kv_loc", [1, KVN], BF16)
    kv_all_t = nc.dram_tensor("kv_all", [4, KVN], BF16)
    ao_loc = nc.dram_tensor("ao_loc", [1, AON], BF16)
    ao_all_t = nc.dram_tensor("ao_all", [4, AON], BF16)
    kvstage = nc.dram_tensor("kvstage", [4, 12 * PC], BF16)
    w8_dram = nc.dram_tensor("w8_dram", [8, T], BF16)
    aostage = nc.dram_tensor("aostage", [4, 4 * PC], BF16)
    park_r = nc.dram_tensor("park_r", [128, DC * T], F32)
    park_zs = nc.dram_tensor("park_zs", [128, DHI * T], BF16)
    aga_loc = nc.dram_tensor("aga_loc", [1, 2 * 48 * 128], F32)
    aga_all = [nc.dram_tensor(f"aga_all{l}", [4, 2 * 48 * 128], F32)
               for l in range(NL)]
    agb_loc = nc.dram_tensor("agb_loc", [1, 2 * 2 * 256 * 128], F32)
    agb_all = [nc.dram_tensor(f"agb_all{l}", [4, 2 * 2 * 256 * 128], F32)
               for l in range(NL)]
    bc_dram = nc.dram_tensor("bc_dram", [16, 4 * 1024], BF16)
    park_dt = nc.dram_tensor("park_dt", [128, DHI * T], BF16)

    wflat = wbf.ap().rearrange("r s -> (r s)")

    def wv(name):
        pos, shape = woff[name]
        K, M = shape
        return wflat[pos:pos + K * M].rearrange("(k m) -> k m", m=M)

    def wvt(name, mb):
        pos, (K, M, mw, _) = woff[name]
        blk = K * mw
        return wflat[pos + mb * blk: pos + (mb + 1) * blk].rearrange(
            "(p x) -> p x", x=(K // 128) * mw)

    def wvt2(name, mb2):
        pos, (K, M, mw, _) = woff[name]
        blk = K * mw
        return wflat[pos + 2 * mb2 * blk: pos + (2 * mb2 + 2) * blk].rearrange(
            "(b p x) -> p b x", p=128, x=(K // 128) * mw)

    CEPS = soff["ceps"][0]
    CM05 = soff["cm05"][0]
    C0125 = soff["c0125"][0]
    CM1 = soff["cm1"][0]

    ident = nc.inline_tensor(np.eye(128, dtype=bf), name="ident128").ap()
    ident32 = nc.inline_tensor(np.eye(128, dtype=np.float32), name="ident128f").ap()
    rotp = nc.inline_tensor(_rot_perm(), name="rotperm").ap()
    trimask = nc.inline_tensor(_tri_masks(), name="trimask").ap()
    sel16c = nc.inline_tensor(_sel16(), name="sel16").ap()
    ones1x128 = nc.inline_tensor(np.ones((1, 128), np.float32), name="o1x128c").ap()
    ones128c = nc.inline_tensor(np.ones((128, 1), np.float32), name="o128cc").ap()

    from contextlib import ExitStack
    stack = ExitStack()
    sbuf = lambda n, s, d=F32: stack.enter_context(nc.sbuf_tensor(n, s, d))
    psum = lambda n, s, d=F32: stack.enter_context(nc.psum_tensor(n, s, d))
    semf = lambda n: stack.enter_context(nc.semaphore(n))

    with stack:
        r_sb = sbuf("r_sb", [128, DC * T])
        xn_sb = sbuf("xn_sb", [128, DC * T], BF16)
        bufA = sbuf("bufA", [128, 8192], BF16)
        bufB = sbuf("bufB", [128, 8192], BF16)
        bufC = sbuf("bufC", [128, 8192], BF16)
        wstr = sbuf("wstr", [128, 2048], BF16)
        abuf = sbuf("abuf", [128, 1024])
        scrA = sbuf("scrA", [128, 2048])
        scrB = sbuf("scrB", [128, 512])
        scrD = sbuf("scrD", [128, 2048], BF16)
        scrE = sbuf("scrE", [128, 2048], BF16)
        scrF = sbuf("scrF", [128, 2048], BF16)
        cst = sbuf("cst", [128, 2048], BF16)
        smalls = sbuf("smalls_sb", [128, NSM])
        rows_a = sbuf("rows_a", [1, 512])
        rows_b = sbuf("rows_b", [1, 512])
        rows_c = sbuf("rows_c", [1, 512])
        id_sb = sbuf("id_sb", [128, 128], BF16)
        id32_sb = sbuf("id32_sb", [128, 128])
        gsb = sbuf("gsb", [128, 64], BF16)
        rot_sb = sbuf("rot_sb", [128, 64], BF16)
        o1x = sbuf("o1x", [1, 128])
        o128 = sbuf("o128", [128, 1])
        mxc = sbuf("mxc", [128, 8])
        mxv = sbuf("mxv", [1, 16])
        dyn_sb = sbuf("dyn_sb", [1, 8], I32)
        carry = sbuf("carry_sb", [128, 1296])
        state = sbuf("state_sb", [128, 2 * 256])
        hin = state
        scrG = sbuf("scrG", [128, 1024], BF16)
        ones_bf = sbuf("ones_bf", [128, 512], BF16)
        negA = scrB[:, 256:512]
        cumST = carry[:, 1264:1296]
        kfull = bufA[0:64, :]
        qfull = bufC[0:64, :]
        vstb = bufB[:, 0:64 * 65]
        ropeC = bufC
        tri_sb = cst

        ps_big = psum("ps_big", [128, 2048])
        ps_o = psum("ps_o", [128, 512])
        ps_p = psum("ps_p", [128, 512])
        ps_q = psum("ps_q", [128, 512])
        ps_t = psum("ps_t", [128, 1024], BF16)

        sems = {}
        for e in Prog.ENGS:
            sems[e + "_i"] = semf("sem_" + e + "_i")
            sems[e + "_d"] = semf("sem_" + e + "_d")
        P = Prog()
        regs = {}
        P.phase = "load"

        @P.step("sync", deps=[], n_inst=0, n_dma=9)
        def LOADS(sync):
            return [
                ("dma", sync.dma_start(out=smalls[:], in_=smalls_in[:])),
                ("dma", sync.dma_start(out=id_sb[:], in_=ident[:])),
                ("dma", sync.dma_start(out=id32_sb[:], in_=ident32[:])),
                ("dma", sync.dma_start(out=rot_sb[:], in_=rotp[:])),
                ("dma", sync.dma_start(out=cst[:], in_=trimask[:])),
                                ("dma", sync.dma_start(out=o1x[:], in_=ones1x128[:])),
                ("dma", sync.dma_start(out=o128[:], in_=ones128c[:])),
                ("dma", sync.dma_start(out=dyn_sb[:], in_=dyn_in[:])),
                ("dma", sync.dma_start(
                    out=r_sb[:].rearrange("p (dc t) -> p dc t", t=T),
                    in_=xin.rearrange("(dc p) t -> p dc t", p=128))),
            ]
        LOADS_ID = P.last()

        prev_rl = LOADS_ID
        for rr in range(4):
            @P.step("sync", deps=[prev_rl], n_inst=0, n_dma=1)
            def ROPELD(sync, rr=rr):
                return ("dma", sync.dma_start(
                    out=abuf[:, 0:1024], in_=rope_in[:, rr * 1024:(rr + 1) * 1024]))
            rl = P.last()

            @P.step("vector", deps=[rl])
            def ROPECAST(vector, rr=rr):
                return vector.tensor_copy(out=bufC[:, rr * 1024:(rr + 1) * 1024],
                                          in_=abuf[:, 0:1024])
            prev_rl = P.last()
        ROPE_ID = P.last()

        P.phase = "castag"
        CPC = S8 // 128
        ncast = (CPC + 1023) // 1024
        wsl2 = wsl.rearrange("a (p c) -> a p c", p=128)
        wbf2 = wbf_sl.ap().rearrange("a (p c) -> a p c", p=128)
        prev_store = ROPE_ID
        for i in range(ncast):
            c0 = i * 1024
            c1 = min(CPC, c0 + 1024)

            @P.step("sync", deps=[prev_store], n_inst=0, n_dma=1)
            def CLD(sync, c0=c0, c1=c1):
                return ("dma", sync.dma_start(out=abuf[:, 0:c1 - c0],
                                              in_=wsl2[0, :, c0:c1]))
            ld = P.last()

            @P.step("vector", deps=[ld, prev_store])
            def CCAST(vector, c0=c0, c1=c1):
                return vector.tensor_copy(out=wstr[:, 0:c1 - c0],
                                          in_=abuf[:, 0:c1 - c0])
            cast_id = P.last()

            @P.step("scalar", deps=[cast_id], n_inst=0, n_dma=1)
            def CST(scalar, c0=c0, c1=c1):
                return ("dma", scalar.dma_start(out=wbf2[0, :, c0:c1],
                                                in_=wstr[:, 0:c1 - c0]))
            prev_store = P.last()

        @P.step("gpsimd", deps=[prev_store])
        def AGW(gpsimd):
            return gpsimd.collective_compute(
                "AllGather", ALU.bypass,
                replica_groups=[[0, 1, 2, 3, 4, 5, 6, 7]],
                ins=[wbf_sl.ap().opt()], outs=[wbf.ap().opt()])
        AGW_ID = P.last()

        def ln_stream(src, first_deps, apply_cb):
            out_ids = []
            prev = None
            for tt in range(4):
                t0 = tt * 512
                mmlast = None
                for rnd in range(2):
                    deps = list(first_deps) if (tt == 0 and rnd == 0) else None

                    @P.step("scalar", deps=deps, n_inst=4)
                    def SQ(scalar, t0=t0, rnd=rnd):
                        ms = []
                        for k in range(4):
                            dc = rnd * 4 + k
                            ms.append(scalar.activation(
                                out=scrA[:, k * 512:(k + 1) * 512],
                                in_=src[:, dc * T + t0: dc * T + t0 + 512],
                                func=AF.Square))
                        return ms
                    sq = P.last()

                    @P.step("tensor", deps=[sq])
                    def MM(tensor, t0=t0, rnd=rnd):
                        m = None
                        for k in range(4):
                            dc = rnd * 4 + k
                            tensor.matmul(ps_q[0:1, :], o128[:],
                                          src[:, dc * T + t0: dc * T + t0 + 512],
                                          start=(dc == 0), stop=(dc == DC - 1))
                            m = tensor.matmul(ps_q[32:33, :], o128[:],
                                              scrA[:, k * 512:(k + 1) * 512],
                                              start=(dc == 0), stop=(dc == DC - 1))
                        return m
                    mmlast = P.last()

                @P.step("vector", deps=[mmlast])
                def STAT1(vector):
                    vector.tensor_scalar(out=rows_a[:], in0=ps_q[0:1, :],
                                         scalar1=1.0 / D, scalar2=None, op0=ALU.mult)
                    return vector.tensor_scalar(out=rows_b[:], in0=ps_q[32:33, :],
                                                scalar1=1.0 / D, scalar2=None,
                                                op0=ALU.mult)
                v1 = P.last()

                @P.step("scalar", deps=[v1])
                def MUSQ(scalar):
                    return scalar.activation(out=rows_c[:], in_=rows_a[:],
                                             func=AF.Square)
                s1 = P.last()

                @P.step("vector", deps=[s1])
                def VARV(vector):
                    return vector.tensor_tensor(out=rows_b[:], in0=rows_b[:],
                                                in1=rows_c[:], op=ALU.subtract)
                v2 = P.last()

                @P.step("scalar", deps=[v2])
                def RSTD1(scalar):
                    return scalar.activation(out=rows_c[:], in_=rows_b[:],
                                             func=AF.Ln,
                                             bias=smalls[0:1, CEPS:CEPS + 1])
                s2a = P.last()

                @P.step("scalar", deps=[s2a])
                def RSTD2(scalar):
                    return scalar.activation(out=rows_b[:], in_=rows_c[:],
                                             func=AF.Exp,
                                             scale=smalls[0:1, CM05:CM05 + 1])
                s2 = P.last()

                deps_b = [s2] if prev is None else [s2, prev]

                @P.step("tensor", deps=deps_b)
                def BCAST(tensor):
                    tensor.matmul(ps_o[:, :], o1x[:], rows_a[:],
                                  start=True, stop=True)
                    return tensor.matmul(ps_p[:, :], o1x[:], rows_b[:],
                                         start=True, stop=True)
                bc = P.last()
                prev = apply_cb(tt, bc)
                out_ids.append(prev)
            return out_ids

        def ln_apply(src, dst_bf, gamma=None, beta=None):
            def cb(tt, dep):
                t0 = tt * 512
                last = dep
                for grp in range(2):
                    @P.step("vector", deps=[last], n_inst=4)
                    def SUBS(vector, grp=grp):
                        ms = []
                        for k in range(4):
                            dc = grp * 4 + k
                            ms.append(vector.tensor_tensor(
                                out=scrA[:, k * 512:(k + 1) * 512],
                                in0=src[:, dc * T + t0: dc * T + t0 + 512],
                                in1=ps_o[:, :], op=ALU.subtract))
                        return ms
                    subs = P.last()

                    @P.step("vector", deps=[subs], n_inst=4)
                    def MULS(vector, grp=grp):
                        ms = []
                        for k in range(4):
                            dc = grp * 4 + k
                            tgt = (scrA[:, k * 512:(k + 1) * 512] if gamma
                                   else dst_bf[:, dc * T + t0: dc * T + t0 + 512])
                            ms.append(vector.tensor_tensor(
                                out=tgt,
                                in0=scrA[:, k * 512:(k + 1) * 512],
                                in1=ps_p[:, :], op=ALU.mult))
                        return ms
                    muls = P.last()
                    last = muls
                    if gamma is not None:
                        gc = soff[gamma][0]
                        bcc = soff[beta][0]

                        @P.step("scalar", deps=[muls], n_inst=4)
                        def GB(scalar, grp=grp):
                            ms = []
                            for k in range(4):
                                dc = grp * 4 + k
                                ms.append(scalar.activation(
                                    out=dst_bf[:, dc * T + t0: dc * T + t0 + 512],
                                    in_=scrA[:, k * 512:(k + 1) * 512],
                                    func=AF.Identity,
                                    scale=smalls[:, gc + dc:gc + dc + 1],
                                    bias=smalls[:, bcc + dc:bcc + dc + 1]))
                            return ms
                        last = P.last()
                return last
            return cb

        P.phase = "ln1"
        ln1_ids = ln_stream(r_sb, [LOADS_ID], ln_apply(r_sb, xn_sb))
        LN1_DONE = ln1_ids[-1]

        kvl = kv_loc.ap().rearrange("a n -> (a n)")

        P.phase = "qkv"
        prev_rope_consumer = None
        prev_half_consumer = [None, None]
        prev_psbig_consumer = [None, None]
        prev_scrD_consumer = [None, None]
        prev_scrG_consumer = [None, None]
        last_pub = []

        for mc in range(24):
            wdeps = [AGW_ID, LN1_DONE]
            if prev_half_consumer[mc % 2] is not None:
                wdeps.append(prev_half_consumer[mc % 2])

            @P.step("sync", deps=wdeps, n_inst=0, n_dma=1)
            def WLD(sync, mc=mc):
                return ("dma", sync.dma_start(
                    out=wstr[:, (mc % 2) * 1024:(mc % 2) * 1024 + 1024],
                    in_=wvt("W1T", mc)))
            wld_id = P.last()
            mm_ids = []

            for tt in range(4):
                t0 = tt * 512
                pb = (tt % 2) * 1024
                mmdeps = [wld_id]
                if prev_psbig_consumer[tt % 2] is not None:
                    mmdeps.append(prev_psbig_consumer[tt % 2])

                @P.step("tensor", deps=mmdeps)
                def QKVMM(tensor, mc=mc, t0=t0, pb=pb):
                    m = None
                    for kc in range(DC):
                        m = tensor.matmul(
                            ps_big[:, pb:pb + 512],
                            wstr[:, (mc % 2) * 1024 + kc * 128:
                                 (mc % 2) * 1024 + kc * 128 + 128],
                            xn_sb[:, kc * T + t0: kc * T + t0 + 512],
                            start=(kc == 0), stop=(kc == DC - 1))
                    return m
                mmid = P.last()
                mm_ids.append(mmid)

                evdeps = [mmid]
                if prev_scrD_consumer[tt % 2] is not None:
                    evdeps.append(prev_scrD_consumer[tt % 2])

                @P.step("scalar", deps=evdeps)
                def EVQKV(scalar, pb=pb, tt=tt):
                    return scalar.activation(
                        out=scrD[:, (tt % 2) * 512:(tt % 2) * 512 + 512],
                        in_=ps_big[:, pb:pb + 512], func=AF.Copy)
                ev = P.last()
                prev_psbig_consumer[tt % 2] = ev

                if mc < 16:
                    is_q = mc < 8
                    h0 = (mc % 8) * 2
                    rotdeps = [ev]
                    if prev_rope_consumer is not None:
                        rotdeps.append(prev_rope_consumer)

                    @P.step("tensor", deps=rotdeps)
                    def ROTMM(tensor, tt=tt):
                        c0 = (tt % 2) * 512
                        tensor.matmul(ps_q[0:64, :], rot_sb[0:64, :],
                                      scrD[0:64, c0:c0 + 512],
                                      start=True, stop=True)
                        return tensor.matmul(ps_q[64:128, :], rot_sb[64:128, :],
                                             scrD[64:128, c0:c0 + 512],
                                             start=True, stop=True)
                    rot = P.last()

                    rpdeps = [rot, ROPE_ID]
                    if prev_scrG_consumer[tt % 2] is not None:
                        rpdeps.append(prev_scrG_consumer[tt % 2])

                    @P.step("vector", deps=rpdeps, n_inst=2)
                    def ROPEAP(vector, t0=t0, tt=tt):
                        c0 = (tt % 2) * 512
                        m1 = vector.tensor_tensor(
                            out=scrE[:, c0:c0 + 512],
                            in0=scrD[:, c0:c0 + 512],
                            in1=bufC[:, t0:t0 + 512], op=ALU.mult)
                        m2 = vector.tensor_tensor(
                            out=scrF[:, c0:c0 + 512],
                            in0=ps_q[:, :],
                            in1=bufC[:, T + t0:T + t0 + 512], op=ALU.mult)
                        return [m1, m2]
                    rpa = P.last()

                    @P.step("vector", deps=[rpa])
                    def ROPEADD(vector, tt=tt):
                        c0 = (tt % 2) * 512
                        return vector.tensor_tensor(
                            out=scrE[:, c0:c0 + 512],
                            in0=scrE[:, c0:c0 + 512],
                            in1=scrF[:, c0:c0 + 512], op=ALU.add)
                    rp = P.last()
                    prev_rope_consumer = rp
                    prev_scrD_consumer[tt % 2] = rp

                    @P.step("scalar", deps=[rp], n_inst=0, n_dma=2)
                    def QKPUB(scalar, t0=t0, tt=tt, h0=h0, is_q=is_q):
                        c0 = (tt % 2) * 512
                        kind = 1 if is_q else 0
                        ms = []
                        for hh in range(2):
                            base = (h0 + hh) * 3 * PC + kind * PC
                            dst = kvl[base:base + PC]
                            dst = dst.rearrange("(dh t) -> dh t", t=T)[:, t0:t0 + 512]
                            ms.append(("dma", scalar.dma_start(
                                out=dst,
                                in_=scrE[hh * 64:hh * 64 + 64, c0:c0 + 512])))
                        return ms
                    pub = P.last()
                    last_pub.append(pub)
                    prev_scrG_consumer[tt % 2] = pub
                else:
                    vtdeps = [ev]
                    if prev_rope_consumer is not None:
                        vtdeps.append(prev_rope_consumer)

                    @P.step("tensor", deps=vtdeps)
                    def VT(tensor, tt=tt):
                        c0 = (tt % 2) * 512
                        m = None
                        for blk in range(4):
                            m = tensor.transpose(
                                ps_t[:, blk * 128:blk * 128 + 128],
                                scrD[:, c0 + blk * 128:c0 + blk * 128 + 128],
                                id_sb[:])
                        return m
                    vt = P.last()

                    vtedeps = [vt]
                    if prev_scrG_consumer[tt % 2] is not None:
                        vtedeps.append(prev_scrG_consumer[tt % 2])

                    @P.step("vector", deps=vtedeps)
                    def VTEV(vector, tt=tt):
                        c0 = (tt % 2) * 512
                        return vector.tensor_copy(out=scrE[:, c0:c0 + 512],
                                                  in_=ps_t[:, 0:512])
                    vte = P.last()
                    prev_rope_consumer = vte
                    prev_scrD_consumer[tt % 2] = vte

                    @P.step("scalar", deps=[vte], n_inst=0, n_dma=8)
                    def VPUB(scalar, mc=mc, t0=t0, tt=tt):
                        c0 = (tt % 2) * 512
                        h0 = (mc - 16) * 2
                        ms = []
                        for blk in range(4):
                            tg0 = t0 + blk * 128
                            for hh in range(2):
                                base = (h0 + hh) * 3 * PC + 2 * PC
                                vtv = kvl[base:base + PC].rearrange(
                                    "(t dh) -> t dh", dh=DH)
                                ms.append(("dma", scalar.dma_start(
                                    out=vtv[tg0:tg0 + 128, :],
                                    in_=scrE[:, c0 + blk * 128 + hh * 64:
                                             c0 + blk * 128 + hh * 64 + 64])))
                        return ms
                    pub = P.last()
                    last_pub.append(pub)
                    prev_scrG_consumer[tt % 2] = pub
            prev_half_consumer[mc % 2] = mm_ids[-1]

        P.phase = "attn"

        @P.step("gpsimd", deps=[last_pub[-1]])
        def AGKV(gpsimd):
            return gpsimd.collective_compute(
                "AllGather", ALU.bypass,
                replica_groups=[[0, 1, 2, 3], [4, 5, 6, 7]],
                ins=[kv_loc.ap().opt()], outs=[kv_all_t.ap().opt()])
        AGKV_ID = P.last()

        def rank_half_of_chunk(c):
            return (c, 0) if c < 4 else (7 - c, 1)

        ao_l = ao_loc.ap().rearrange("a n -> (a n)")
        aol_v = ao_l

        @P.step("sync", deps=[AGKV_ID], n_inst=0, n_dma=4)
        def HDYN(sync):
            if "g" not in regs:
                regs["g"] = sync.alloc_register("rg")
                regs["o"] = sync.alloc_register("ro")
            rg, ro = regs["g"], regs["o"]
            sync.reg_load(rg, dyn_sb[0:1, 0:1])
            ms = []
            for rk in range(4):
                sync.reg_mul(ro, rg, 12 * PC)
                sync.reg_add(ro, ro, rk * KVN)
                ms.append(("dma", sync.dma_start(
                    out=kvstage.ap()[rk:rk + 1, :],
                    in_=bass.AP(kv_all_t, ro, [[1, 1], [1, 12 * PC]]))))
            return ms
        HDYN_ID = P.last()

        prev_qt_chain = HDYN_ID
        exps_hist = [HDYN_ID, HDYN_ID]
        pvmm_hist = [HDYN_ID, HDYN_ID]
        gp = 0
        for hl in range(4):
            @P.step("sync", deps=[HDYN_ID, prev_qt_chain], n_inst=0, n_dma=24)
            def HLD(sync, hl=hl):
                ms = []
                for rk in range(4):
                    base = hl * 3 * PC
                    kpiece = kvstage.ap()[rk, base:base + PC].rearrange(
                        "(dh t) -> dh t", t=T)
                    qpiece = kvstage.ap()[rk, base + PC:base + 2 * PC].rearrange(
                        "(dh t) -> dh t", t=T)
                    vpiece = kvstage.ap()[rk, base + 2 * PC:base + 3 * PC].rearrange(
                        "(t dh) -> t dh", dh=DH)
                    for half in range(2):
                        c = rk if half == 0 else 7 - rk
                        ms.append(("dma", sync.dma_start(
                            out=kfull[:, c * CH:(c + 1) * CH],
                            in_=kpiece[:, half * CH:(half + 1) * CH])))
                        ms.append(("dma", sync.dma_start(
                            out=qfull[:, c * CH:(c + 1) * CH],
                            in_=qpiece[:, half * CH:(half + 1) * CH])))
                        dst = vstb[:, c * 8 * 65:(c + 1) * 8 * 65].rearrange(
                            "p (b c2) -> p b c2", c2=65)[:, :, 0:64]
                        ms.append(("dma", sync.dma_start(
                            out=dst,
                            in_=vpiece[half * CH:(half + 1) * CH, :].rearrange(
                                "(b p) dh -> p b dh", p=128))))
                return ms
            hld = P.last()

            @P.step("vector", deps=[hld])
            def VONES(vector):
                return vector.memset(
                    vstb[:].rearrange("p (b c2) -> p b c2", c2=65)[:, :, 64:65], 1.0)
            vones = P.last()

            nm_last = []
            prev_nmx = None
            for which, buf in (("q", qfull), ("k", kfull)):
                for span in range(8):
                    s0 = span * 1024

                    @P.step("scalar", deps=[vones] if span == 0 else None)
                    def NSQ(scalar, buf=buf, s0=s0):
                        return scalar.activation(out=abuf[0:64, 0:1024],
                                                 in_=buf[:, s0:s0 + 1024],
                                                 func=AF.Square)
                    nsq = P.last()

                    nmmdeps = [nsq]
                    if prev_nmx is not None:
                        nmmdeps.append(prev_nmx)

                    @P.step("tensor", deps=nmmdeps)
                    def NMM(tensor):
                        m = None
                        for i in range(2):
                            m = tensor.matmul(ps_big[0:1, i * 512:i * 512 + 512],
                                              o128[0:64, :],
                                              abuf[0:64, i * 512:i * 512 + 512],
                                              start=True, stop=True)
                        return m
                    nmm = P.last()

                    @P.step("vector", deps=[nmm])
                    def NMXa(vector):
                        return vector.reduce_max(out=rows_c[0:1, 0:1],
                                                 in_=ps_big[0:1, 0:1024], axis=AX)
                    nmxa = P.last()

                    @P.step("vector", deps=[nmxa])
                    def NMXb(vector, which=which, hl=hl, span=span):
                        col = hl if which == "q" else 8 + hl
                        if span == 0:
                            return vector.tensor_copy(out=mxv[0:1, col:col + 1],
                                                      in_=rows_c[0:1, 0:1])
                        return vector.tensor_tensor(
                            out=mxv[0:1, col:col + 1],
                            in0=mxv[0:1, col:col + 1],
                            in1=rows_c[0:1, 0:1], op=ALU.max)
                    prev_nmx = P.last()
                    nm_last.append(prev_nmx)

            @P.step("vector", deps=[nm_last[-1]])
            def MH(vector, hl=hl):
                return vector.tensor_tensor(out=rows_c[0:1, 0:1],
                                            in0=mxv[0:1, hl:hl + 1],
                                            in1=mxv[0:1, 8 + hl:8 + hl + 1],
                                            op=ALU.mult)
            mh1 = P.last()

            @P.step("scalar", deps=[mh1])
            def MH2(scalar):
                return scalar.activation(out=rows_c[0:1, 2:3],
                                         in_=rows_c[0:1, 0:1], func=AF.Sqrt)
            mh2 = P.last()

            @P.step("vector", deps=[mh2])
            def MH3(vector):
                return vector.tensor_scalar(out=rows_c[0:1, 3:4],
                                            in0=rows_c[0:1, 2:3],
                                            scalar1=-0.125, scalar2=None,
                                            op0=ALU.mult)
            mh3 = P.last()

            @P.step("tensor", deps=[mh3])
            def MHB(tensor):
                return tensor.matmul(ps_q[:, 0:1], o1x[:], rows_c[0:1, 3:4],
                                     start=True, stop=True)
            mhb = P.last()

            @P.step("vector", deps=[mhb])
            def MHC(vector, hl=hl):
                return vector.tensor_copy(out=mxc[:, hl:hl + 1],
                                          in_=ps_q[:, 0:1])
            mhc = P.last()

            prev_oev = mhc
            prev_onrm = mhc
            prev_aow = mhc
            for qt in range(16):
                nkb = 4 * qt + 4
                npr = (nkb + 1) // 2
                qt_pv = []
                for g in range(npr):
                    kbs = [k for k in (2 * g, 2 * g + 1) if k < nkb]
                    half = gp % 2
                    pb = half * 1024

                    @P.step("tensor", deps=[exps_hist[gp], mhc])
                    def STMM(tensor, qt=qt, kbs=kbs, pb=pb):
                        m = None
                        for i, kb in enumerate(kbs):
                            m = tensor.matmul(
                                ps_big[:, pb + i * 512:pb + i * 512 + 512],
                                kfull[:, kb * 128:(kb + 1) * 128],
                                qfull[:, qt * 512:qt * 512 + 512],
                                start=True, stop=True)
                        return m
                    stmm = P.last()

                    @P.step("scalar", deps=[stmm, pvmm_hist[gp]],
                            n_inst=len(kbs))
                    def EXPS(scalar, hl=hl, kbs=kbs, pb=pb, half=half):
                        ms = []
                        for i, kb in enumerate(kbs):
                            ms.append(scalar.activation(
                                out=scrD[:, half * 1024 + i * 512:
                                         half * 1024 + i * 512 + 512],
                                in_=ps_big[:, pb + i * 512:pb + i * 512 + 512],
                                func=AF.Exp, scale=smalls[:, C0125:C0125 + 1],
                                bias=mxc[:, hl:hl + 1]))
                        return ms
                    exps = P.last()
                    exps_hist.append(exps)

                    diag = [kb for kb in kbs if kb >= 4 * qt]
                    dep_mm2 = exps
                    if diag:
                        @P.step("vector", deps=[exps], n_inst=len(diag))
                        def MASK(vector, diag=diag, kbs=kbs, qt=qt, half=half):
                            ms = []
                            for kb in diag:
                                i = kb - kbs[0]
                                mi = kb - 4 * qt
                                ms.append(vector.tensor_tensor(
                                    out=scrD[:, half * 1024 + i * 512:
                                             half * 1024 + i * 512 + 512],
                                    in0=scrD[:, half * 1024 + i * 512:
                                             half * 1024 + i * 512 + 512],
                                    in1=tri_sb[:, mi * 512:(mi + 1) * 512],
                                    op=ALU.mult))
                            return ms
                        dep_mm2 = P.last()

                    pvdeps = [dep_mm2]
                    if g == 0:
                        pvdeps.append(prev_oev)

                    @P.step("tensor", deps=pvdeps)
                    def PVMM(tensor, kbs=kbs, half=half, first=(g == 0),
                             lastg=(g == npr - 1)):
                        m = None
                        for i, kb in enumerate(kbs):
                            m = tensor.matmul(
                                ps_o[0:65, :],
                                vstb[:, kb * 65:(kb + 1) * 65],
                                scrD[:, half * 1024 + i * 512:
                                     half * 1024 + i * 512 + 512],
                                start=(first and i == 0),
                                stop=(lastg and i == len(kbs) - 1))
                        return m
                    pv = P.last()
                    pvmm_hist.append(pv)
                    qt_pv.append(pv)
                    gp += 1

                @P.step("vector", deps=[qt_pv[-1], prev_onrm, prev_aow])
                def ZREC(vector):
                    return vector.reciprocal(out=rows_c[0:1, 0:512],
                                             in_=ps_o[64:65, :])
                zr = P.last()

                @P.step("tensor", deps=[zr])
                def ZB(tensor):
                    return tensor.matmul(ps_p[0:64, :], o1x[0:1, 0:64],
                                         rows_c[0:1, 0:512],
                                         start=True, stop=True)
                zb = P.last()

                @P.step("scalar", deps=[zb])
                def OEV(scalar):
                    return scalar.activation(out=scrE[0:64, 0:512],
                                             in_=ps_o[0:64, :], func=AF.Copy)
                prev_oev = P.last()

                @P.step("vector", deps=[prev_oev])
                def ONRM(vector):
                    return vector.tensor_tensor(out=scrF[0:64, 0:512],
                                                in0=scrE[0:64, 0:512],
                                                in1=ps_p[0:64, :], op=ALU.mult)
                prev_onrm = P.last()

                c = qt // 2
                rk, half2 = rank_half_of_chunk(c)
                lt = half2 * CH + (qt % 2) * 512

                @P.step("scalar", deps=[prev_onrm], n_inst=0, n_dma=1)
                def AOW(scalar, hl=hl, rk=rk, lt=lt):
                    dst = aol_v[rk * AOB + hl * DH * T:
                                rk * AOB + (hl + 1) * DH * T]
                    dst = dst.rearrange("(dh t) -> dh t", t=T)[:, lt:lt + 512]
                    return ("dma", scalar.dma_start(out=dst,
                                                    in_=scrF[0:64, 0:512]))
                prev_aow = P.last()
                prev_qt_chain = prev_aow

        @P.step("gpsimd", deps=[prev_qt_chain])
        def AGAO(gpsimd):
            return gpsimd.collective_compute(
                "AllGather", ALU.bypass,
                replica_groups=[[0, 1, 2, 3], [4, 5, 6, 7]],
                ins=[ao_loc.ap().opt()], outs=[ao_all_t.ap().opt()])
        AGAO_ID = P.last()

        @P.step("sync", deps=[AGAO_ID], n_inst=0, n_dma=4)
        def AODYN(sync):
            rg, ro = regs["g"], regs["o"]
            sync.reg_load(rg, dyn_sb[0:1, 0:1])
            ms = []
            for r in range(4):
                sync.reg_mul(ro, rg, AOB)
                sync.reg_add(ro, ro, r * AON)
                ms.append(("dma", sync.dma_start(
                    out=aostage.ap()[r:r + 1, :],
                    in_=bass.AP(ao_all_t, ro, [[1, 1], [1, AOB]]))))
            return ms
        aodyn = P.last()

        @P.step("sync", deps=[aodyn], n_inst=0, n_dma=16)
        def AORD(sync):
            ms = []
            for h in range(H):
                r, hlh = h // 4, h % 4
                dc, p0 = h // 2, (h % 2) * 64
                ms.append(("dma", sync.dma_start(
                    out=xn_sb[p0:p0 + 64, dc * T:(dc + 1) * T],
                    in_=aostage.ap()[r, hlh * PC:(hlh + 1) * PC]
                        .rearrange("(dh t) -> dh t", t=T))))
            return ms
        AORD_ID = P.last()

        P.phase = "w2"
        prev = AORD_ID
        prev_half = [AORD_ID, AORD_ID]
        prev_pb = [AORD_ID, AORD_ID]
        for dc in range(DC):
            w2deps = [prev_half[dc % 2]] if dc >= 2 else [AGKV_ID]

            @P.step("sync", deps=w2deps, n_inst=0, n_dma=1)
            def W2LD(sync, dc=dc):
                return ("dma", sync.dma_start(
                    out=wstr[:, (dc % 2) * 1024:(dc % 2) * 1024 + 1024],
                    in_=wvt("W2T", dc)))
            wld2 = P.last()
            mmlist = []
            for tt in range(4):
                t0 = tt * 512

                @P.step("tensor", deps=[wld2, prev_pb[tt % 2], AORD_ID])
                def W2MM(tensor, dc=dc, t0=t0, tt=tt):
                    pp = (tt % 2) * 1024
                    m = None
                    for kc in range(DC):
                        m = tensor.matmul(
                            ps_big[:, pp:pp + 512],
                            wstr[:, (dc % 2) * 1024 + kc * 128:
                                 (dc % 2) * 1024 + kc * 128 + 128],
                            xn_sb[:, kc * T + t0: kc * T + t0 + 512],
                            start=(kc == 0), stop=(kc == DC - 1))
                    return m
                w2mm = P.last()

                @P.step("vector", deps=[w2mm])
                def RADD(vector, dc=dc, t0=t0, tt=tt):
                    pp = (tt % 2) * 1024
                    return vector.tensor_tensor(
                        out=r_sb[:, dc * T + t0: dc * T + t0 + 512],
                        in0=r_sb[:, dc * T + t0: dc * T + t0 + 512],
                        in1=ps_big[:, pp:pp + 512], op=ALU.add)
                prev_pb[tt % 2] = P.last()
                mmlist.append(w2mm)
            prev_half[dc % 2] = mmlist[-1]
        SUBL1 = P.last()


        def ffn_block(prev_done, pre_dep=None):
            @P.step("sync", deps=[prev_done, AGW_ID], n_inst=0, n_dma=1)
            def GLD(sync):
                return ("dma", sync.dma_start(out=gsb[:], in_=wvt("gate", 0)))
            prev = P.last()

            for tt in range(4):
                t0 = tt * 512

                @P.step("tensor", deps=[prev])
                def GMM(tensor, t0=t0):
                    m = None
                    for kc in range(DC):
                        m = tensor.matmul(
                            ps_q[0:8, :], gsb[:, kc * 8:(kc + 1) * 8],
                            xn_sb[:, kc * T + t0: kc * T + t0 + 512],
                            start=(kc == 0), stop=(kc == DC - 1))
                    return m
                gmm = P.last()

                @P.step("scalar", deps=[gmm])
                def GEV(scalar):
                    return scalar.activation(out=scrB[0:8, 0:512],
                                             in_=ps_q[0:8, :], func=AF.Copy)
                gev = P.last()

                @P.step("tensor", deps=[gev])
                def GTR(tensor):
                    m = None
                    for blk in range(4):
                        m = tensor.transpose(ps_p[:, blk * 8:(blk + 1) * 8],
                                             scrB[0:8, blk * 128:(blk + 1) * 128],
                                             id32_sb[0:8, 0:8])
                    return m
                gtr = P.last()

                @P.step("vector", deps=[gtr])
                def GT2(vector):
                    return vector.tensor_copy(out=scrA[:, 1024:1056],
                                              in_=ps_p[:, 0:32])
                gt2 = P.last()

                @P.step("vector", deps=[gt2], n_inst=4)
                def TOPA1(vector):
                    ms = []
                    for blk in range(4):
                        g0 = 1024 + blk * 8
                        mx1 = 1056 + blk
                        ms.append(vector.reduce_max(
                            out=scrA[:, mx1:mx1 + 1],
                            in_=scrA[:, g0:g0 + 8], axis=AX))
                    return ms
                topa1 = P.last()

                @P.step("vector", deps=[topa1], n_inst=4)
                def TOPA2(vector):
                    ms = []
                    for blk in range(4):
                        g0 = 1024 + blk * 8
                        mx1 = 1056 + blk
                        d1 = 1064 + blk * 8
                        ms.append(vector.tensor_scalar(
                            out=scrA[:, d1:d1 + 8], in0=scrA[:, g0:g0 + 8],
                            scalar1=scrA[:, mx1:mx1 + 1], scalar2=None,
                            op0=ALU.subtract))
                    return ms
                topa = P.last()

                @P.step("scalar", deps=[topa], n_inst=4)
                def SGN1(scalar):
                    ms = []
                    for blk in range(4):
                        d1 = 1064 + blk * 8
                        s1 = 1096 + blk * 8
                        ms.append(scalar.activation(
                            out=scrA[:, s1:s1 + 8], in_=scrA[:, d1:d1 + 8],
                            func=AF.Sign))
                    return ms
                sgn1 = P.last()

                @P.step("vector", deps=[sgn1], n_inst=4)
                def S4a(vector):
                    ms = []
                    for blk in range(4):
                        s1 = 1096 + blk * 8
                        mk1 = 1128 + blk * 8
                        ms.append(vector.tensor_scalar(
                            out=scrA[:, mk1:mk1 + 8], in0=scrA[:, s1:s1 + 8],
                            scalar1=1.0, scalar2=None, op0=ALU.add))
                    return ms
                s4a = P.last()

                @P.step("vector", deps=[s4a], n_inst=4)
                def S4b(vector):
                    ms = []
                    for blk in range(4):
                        g0 = 1024 + blk * 8
                        mk1 = 1128 + blk * 8
                        g2 = 1160 + blk * 8
                        ms.append(vector.scalar_tensor_tensor(
                            out=scrA[:, g2:g2 + 8], in0=scrA[:, mk1:mk1 + 8],
                            scalar=-1e9, in1=scrA[:, g0:g0 + 8],
                            op0=ALU.mult, op1=ALU.add))
                    return ms
                s4 = P.last()

                @P.step("vector", deps=[s4], n_inst=4)
                def S5(vector):
                    ms = []
                    for blk in range(4):
                        g2 = 1160 + blk * 8
                        mx2 = 1192 + blk
                        ms.append(vector.reduce_max(
                            out=scrA[:, mx2:mx2 + 1],
                            in_=scrA[:, g2:g2 + 8], axis=AX))
                    return ms
                s5 = P.last()

                @P.step("vector", deps=[s5], n_inst=4 * 2)
                def S6(vector):
                    ms = []
                    for blk in range(4):
                        g2 = 1160 + blk * 8
                        mx1 = 1056 + blk
                        mx2 = 1192 + blk
                        d12 = 1196 + blk
                        d2 = 1208 + blk * 8
                        ms.append(vector.tensor_scalar(
                            out=scrA[:, d2:d2 + 8], in0=scrA[:, g2:g2 + 8],
                            scalar1=scrA[:, mx2:mx2 + 1], scalar2=None,
                            op0=ALU.subtract))
                        ms.append(vector.tensor_tensor(
                            out=scrA[:, d12:d12 + 1], in0=scrA[:, mx1:mx1 + 1],
                            in1=scrA[:, mx2:mx2 + 1], op=ALU.subtract))
                    return ms
                topb = P.last()

                @P.step("scalar", deps=[topb], n_inst=8)
                def S7(scalar):
                    ms = []
                    for blk in range(4):
                        d12 = 1196 + blk
                        w1 = 1200 + blk
                        d2 = 1208 + blk * 8
                        s2 = 1240 + blk * 8
                        ms.append(scalar.activation(
                            out=scrA[:, w1:w1 + 1], in_=scrA[:, d12:d12 + 1],
                            func=AF.Sigmoid))
                        ms.append(scalar.activation(
                            out=scrA[:, s2:s2 + 8], in_=scrA[:, d2:d2 + 8],
                            func=AF.Sign))
                    return ms
                sigw = P.last()

                @P.step("vector", deps=[sigw], n_inst=4 * 2)
                def S8(vector):
                    ms = []
                    for blk in range(4):
                        w1 = 1200 + blk
                        w2 = 1204 + blk
                        s2 = 1240 + blk * 8
                        mk2 = 1272 + blk * 8
                        ms.append(vector.tensor_scalar(
                            out=scrA[:, w2:w2 + 1], in0=scrA[:, w1:w1 + 1],
                            scalar1=-1.0, scalar2=1.0, op0=ALU.mult, op1=ALU.add))
                        ms.append(vector.tensor_scalar(
                            out=scrA[:, mk2:mk2 + 8], in0=scrA[:, s2:s2 + 8],
                            scalar1=1.0, scalar2=None, op0=ALU.add))
                    return ms
                s8 = P.last()

                @P.step("vector", deps=[s8], n_inst=4 * 2)
                def S9a(vector):
                    ms = []
                    for blk in range(4):
                        mk1 = 1128 + blk * 8
                        mk2 = 1272 + blk * 8
                        w1 = 1200 + blk
                        w2 = 1204 + blk
                        t2 = 1304 + blk * 8
                        wt = 1336 + blk * 8
                        ms.append(vector.tensor_scalar(
                            out=scrA[:, wt:wt + 8], in0=scrA[:, mk1:mk1 + 8],
                            scalar1=scrA[:, w1:w1 + 1], scalar2=None,
                            op0=ALU.mult))
                        ms.append(vector.tensor_scalar(
                            out=scrA[:, t2:t2 + 8], in0=scrA[:, mk2:mk2 + 8],
                            scalar1=scrA[:, w2:w2 + 1], scalar2=None,
                            op0=ALU.mult))
                    return ms
                s9a = P.last()

                @P.step("vector", deps=[s9a], n_inst=4)
                def S9b(vector):
                    ms = []
                    for blk in range(4):
                        t2 = 1304 + blk * 8
                        wt = 1336 + blk * 8
                        ms.append(vector.tensor_tensor(
                            out=scrA[:, wt:wt + 8], in0=scrA[:, wt:wt + 8],
                            in1=scrA[:, t2:t2 + 8], op=ALU.add))
                    return ms
                wbuild = P.last()

                @P.step("tensor", deps=[wbuild])
                def WTR(tensor):
                    m = None
                    for blk in range(4):
                        wt = 1336 + blk * 8
                        m = tensor.transpose(ps_q[0:8, blk * 128:(blk + 1) * 128]
                                             if False else
                                             ps_p[0:8, blk * 128:(blk + 1) * 128],
                                             scrA[:, wt:wt + 8], id32_sb[:])
                    return m
                wtr = P.last()

                @P.step("scalar", deps=[wtr])
                def W8EV(scalar):
                    return scalar.activation(out=scrD[0:8, 0:512],
                                             in_=ps_p[0:8, 0:512], func=AF.Copy)
                w8ev = P.last()

                @P.step("scalar", deps=[w8ev], n_inst=0, n_dma=1)
                def W8ST(scalar, t0=t0):
                    return ("dma", scalar.dma_start(
                        out=w8_dram.ap()[:, t0:t0 + 512], in_=scrD[0:8, 0:512]))
                prev = P.last()

            w8_done = prev
            ring_cons = []
            gel_hist = []
            prev_last_pmm = w8_done
            prev_cmbs = [w8_done, w8_done]
            last = w8_done
            rglob = 0
            for tc in range(4):
                t0 = tc * 512

                @P.step("sync", deps=[last, w8_done], n_inst=0, n_dma=8)
                def WBLD(sync, t0=t0):
                    ms = []
                    for e in range(E):
                        slot = scrD if e < 4 else scrE
                        ms.append(("dma", sync.dma_start(
                            out=slot[:, (e % 4) * 512:(e % 4) * 512 + 512],
                            in_=bass.AP(w8_dram, e * T + t0, [[0, 128], [1, 512]]))))
                    return ms
                wbld = P.last()

                for e in range(E):
                    gels_this = []
                    for fcp in range(16):
                        slot = rglob % 4
                        eng = ("sync", "gpsimd", "sync", "scalar")[rglob % 4]
                        if rglob >= 4:
                            ldeps = [ring_cons[rglob - 4]]
                        else:
                            ldeps = [wbld] if pre_dep is None else [pre_dep]

                        @P.step(eng, deps=ldeps, n_inst=0, n_dma=1)
                        def FLD(eh, e=e, fcp=fcp, slot=slot):
                            return ("dma", eh.dma_start(
                                out=bufC[:, slot * 2048:(slot + 1) * 2048]
                                    .rearrange("p (b x) -> p b x", b=2),
                                in_=wvt2(f"fc{e}", fcp)))
                        ld = P.last()

                        mdeps = [ld, wbld]
                        if len(gel_hist) >= 1:
                            mdeps.append(gel_hist[-1])

                        @P.step("tensor", deps=mdeps)
                        def FMM(tensor, t0=t0, fcp=fcp, slot=slot):
                            m = None
                            for i in range(2):
                                for kc in range(DC):
                                    m = tensor.matmul(
                                        ps_big[:, (fcp % 2) * 1024 + i * 512:
                                               (fcp % 2) * 1024 + i * 512 + 512],
                                        bufC[:, slot * 2048 + i * 1024 + kc * 128:
                                             slot * 2048 + i * 1024 + kc * 128 + 128],
                                        xn_sb[:, kc * T + t0: kc * T + t0 + 512],
                                        start=(kc == 0), stop=(kc == DC - 1))
                            return m
                        fmm = P.last()
                        ring_cons.append(fmm)
                        rglob += 1

                        @P.step("scalar", deps=[fmm, prev_last_pmm])
                        def GEL(scalar, fcp=fcp):
                            hb = bufA if fcp < 8 else bufB
                            return scalar.activation(
                                out=hb[:, (fcp % 8) * 1024:(fcp % 8) * 1024 + 1024],
                                in_=ps_big[:, (fcp % 2) * 1024:
                                           (fcp % 2) * 1024 + 1024],
                                func=AF.Gelu)
                        gel_hist.append(P.last())
                        gels_this.append(gel_hist[-1])

                    for dc in range(DC):
                        pp = ps_o if dc % 2 == 0 else ps_p
                        pmm = None
                        for php in range(2):
                            slot = rglob % 4
                            eng = ("sync", "gpsimd", "sync", "scalar")[rglob % 4]
                            ldeps = [ring_cons[rglob - 4]]

                            @P.step(eng, deps=ldeps, n_inst=0, n_dma=1)
                            def PLD(eh, e=e, dc=dc, php=php, slot=slot):
                                return ("dma", eh.dma_start(
                                    out=bufC[:, slot * 2048:(slot + 1) * 2048],
                                    in_=wvt(f"pj{e}", dc)
                                        [:, php * 2048:(php + 1) * 2048]))
                            ld = P.last()

                            mdeps = [ld, gels_this[php * 8 + 7]]
                            if php == 0:
                                mdeps.append(prev_cmbs[dc % 2])

                            @P.step("tensor", deps=mdeps)
                            def PMM(tensor, dc=dc, php=php, slot=slot, pp=pp):
                                m = None
                                for b2 in range(16):
                                    fc = php * 16 + b2
                                    hb = bufA if fc < 16 else bufB
                                    m = tensor.matmul(
                                        pp[:, :],
                                        bufC[:, slot * 2048 + b2 * 128:
                                             slot * 2048 + b2 * 128 + 128],
                                        hb[:, (fc % 16) * 512:(fc % 16) * 512 + 512],
                                        start=(php == 0 and b2 == 0),
                                        stop=(php == 1 and b2 == 15))
                                return m
                            pmm = P.last()
                            ring_cons.append(pmm)
                            rglob += 1

                        @P.step("vector", deps=[pmm])
                        def CMBa(vector, e=e, pp=pp):
                            slot = scrD if e < 4 else scrE
                            return vector.tensor_tensor(
                                out=scrF[:, 0:512], in0=pp[:, :],
                                in1=slot[:, (e % 4) * 512:(e % 4) * 512 + 512],
                                op=ALU.mult)
                        cmba = P.last()

                        @P.step("vector", deps=[cmba])
                        def CMBb(vector, dc=dc, t0=t0):
                            return vector.tensor_tensor(
                                out=r_sb[:, dc * T + t0: dc * T + t0 + 512],
                                in0=r_sb[:, dc * T + t0: dc * T + t0 + 512],
                                in1=scrF[:, 0:512], op=ALU.add)
                        prev_cmbs[dc % 2] = P.last()
                        last = prev_cmbs[dc % 2]
                    prev_last_pmm = ring_cons[-1]
            return last

        if BUILD_UPTO >= 2:
            P.phase = "ffn1"
            ln2_ids = ln_stream(r_sb, [SUBL1], ln_apply(r_sb, xn_sb))
            SUBL2 = ffn_block(ln2_ids[-1], pre_dep=AGAO_ID)
        else:
            SUBL2 = SUBL1

        def sel_view(n):
            return cst[0:16, n * 128:(n + 1) * 128]

        def mamba_block(prev_done):
            OHA = soff["ohA"][0]
            OHB = soff["ohB"][0]
            OHPA = soff["ohPA"][0]
            OHPB = soff["ohPB"][0]
            CONE = soff["cone"][0]

            P.phase = "mm_prep"

            @P.step("sync", deps=[prev_done], n_inst=0, n_dma=2)
            def PARKR(sync):
                return [("dma", sync.dma_start(out=park_r.ap()[:, :], in_=r_sb[:])),
                        ("dma", sync.dma_start(out=cst[0:16, :], in_=sel16c[:]))]
            parkr = P.last()

            @P.step("vector", deps=[parkr])
            def ONESB(vector):
                return vector.memset(ones_bf[:], 1.0)
            onesb = P.last()

            ln3 = ln_stream(r_sb, [onesb], ln_apply(r_sb, r_sb))
            prev = ln3[-1]

            dtv_l = {l: wv(f"dtp{l}") for l in range(NL)}

            for l in range(NL):
                alc = soff[f"alog{l}"][0]
                dtbc = soff[f"dtb{l}"][0]
                dskc = soff[f"dsk{l}"][0]
                cbc = soff[f"cb{l}"][0]

                P.phase = "mm_ln"

                @P.step("scalar", deps=[prev])
                def NEGA1(scalar, alc=alc):
                    return scalar.activation(out=negA[:, 0:256],
                                             in_=smalls[:, alc:alc + 256],
                                             func=AF.Exp)
                nega1 = P.last()

                @P.step("vector", deps=[nega1])
                def NEGA2(vector):
                    return vector.tensor_scalar(out=negA[:, 0:256],
                                                in0=negA[:, 0:256],
                                                scalar1=-1.0, scalar2=None,
                                                op0=ALU.mult)
                nega = P.last()

                lnl = ln_stream(r_sb, [nega],
                                ln_apply(r_sb, xn_sb, gamma=f"nw{l}", beta=f"nb{l}"))
                LNL = lnl[-1]

                prev_q = LNL

                for q in range(4):
                    t0q = q * 512
                    ci = q // 2

                    P.phase = "mm_inproj"
                    imms = []
                    ievs = []
                    for oc in range(32):
                        eng = ("sync", "gpsimd", "sync")[oc % 3]
                        wdeps = [imms[oc - 2]] if oc >= 2 else [prev_q]

                        @P.step(eng, deps=wdeps, n_inst=0, n_dma=1)
                        def IWLD(eh, l=l, oc=oc):
                            return ("dma", eh.dma_start(
                                out=wstr[:, (oc % 2) * 1024:(oc % 2) * 1024 + 1024],
                                in_=wvt(f"in{l}", oc)))
                        iwld = P.last()

                        mdeps = [iwld] + ([ievs[oc - 2]] if oc >= 2 else [prev_q])

                        @P.step("tensor", deps=mdeps)
                        def IMM(tensor, oc=oc, t0q=t0q, q=q):
                            m = None
                            for kc in range(DC):
                                m = tensor.matmul(
                                    ps_big[:, (oc % 2) * 512:(oc % 2) * 512 + 512],
                                    wstr[:, (oc % 2) * 1024 + kc * 128:
                                         (oc % 2) * 1024 + kc * 128 + 128],
                                    xn_sb[:, kc * T + t0q: kc * T + t0q + 512],
                                    start=(kc == 0), stop=(kc == DC - 1))
                            if q == 0 and oc < 16:
                                for ci2, toff in enumerate((CH - 3, T - 3)):
                                    for kc in range(DC):
                                        m = tensor.matmul(
                                            ps_q[:, (oc % 2) * 8 + ci2 * 4:
                                                 (oc % 2) * 8 + ci2 * 4 + 3],
                                            wstr[:, (oc % 2) * 1024 + kc * 128:
                                                 (oc % 2) * 1024 + kc * 128 + 128],
                                            xn_sb[:, kc * T + toff:
                                                  kc * T + toff + 3],
                                            start=(kc == 0), stop=(kc == DC - 1))
                            return m
                        imm = P.last()
                        imms.append(imm)

                        nin = 3 if (q == 0 and oc < 16) else 1

                        @P.step("scalar", deps=[imm], n_inst=nin)
                        def IEV(scalar, oc=oc, q=q):
                            ms = []
                            if oc < 16:
                                ms.append(scalar.activation(
                                    out=bufA[:, oc * 512:(oc + 1) * 512],
                                    in_=ps_big[:, (oc % 2) * 512:
                                               (oc % 2) * 512 + 512],
                                    func=AF.Copy))
                                if q == 0:
                                    for ci2 in range(2):
                                        ms.append(scalar.activation(
                                            out=carry[:, 1024 + ci2 * 48 + oc * 3:
                                                      1024 + ci2 * 48 + oc * 3 + 3],
                                            in_=ps_q[:, (oc % 2) * 8 + ci2 * 4:
                                                     (oc % 2) * 8 + ci2 * 4 + 3],
                                            func=AF.Copy))
                            else:
                                ms.append(scalar.activation(
                                    out=bufC[:, (oc - 16) * 512:(oc - 15) * 512],
                                    in_=ps_big[:, (oc % 2) * 512:
                                               (oc % 2) * 512 + 512],
                                    func=AF.Silu))
                            return ms
                        ievs.append(P.last())
                    INPJ = ievs[-1]

                    hexid = None
                    if q == 0:
                        @P.step("sync", deps=[ievs[15]], n_inst=0, n_dma=1)
                        def AGAW(sync):
                            return ("dma", sync.dma_start(
                                out=aga_loc.ap().rearrange("a (p c) -> (a p) c",
                                                           c=96),
                                in_=carry[:, 1024:1120]))
                        agaw = P.last()

                        @P.step("gpsimd", deps=[agaw])
                        def AGA(gpsimd, l=l):
                            return gpsimd.collective_compute(
                                "AllGather", ALU.bypass,
                                replica_groups=[[0, 1, 2, 3], [4, 5, 6, 7]],
                                ins=[aga_loc.ap().opt()],
                                outs=[aga_all[l].ap().opt()])
                        aga = P.last()

                        @P.step("sync", deps=[aga], n_inst=0, n_dma=8)
                        def HALD(sync, l=l):
                            ms = []
                            for g in range(8):
                                rk = g if g < 4 else 7 - g
                                sl = 0 if g < 4 else 1
                                src2 = aga_all[l].ap()[rk:rk + 1, :].rearrange(
                                    "r (p c) -> (r p) c", c=96)[:,
                                                                sl * 48:sl * 48 + 48]
                                ms.append(("dma", sync.dma_start(
                                    out=scrA[:, g * 48:(g + 1) * 48], in_=src2)))
                            return ms
                        hald = P.last()

                        @P.step("vector", deps=[hald])
                        def HZERO(vector):
                            return vector.memset(carry[:, 1168:1264], 0.0)
                        hz = P.last()

                        @P.step("vector", deps=[hz], n_inst=32)
                        def HEX(vector):
                            ms = []
                            for g in range(8):
                                ms.append(vector.tensor_scalar(
                                    out=scrA[:, 384:432],
                                    in0=scrA[:, g * 48:(g + 1) * 48],
                                    scalar1=smalls[:, OHPA + g:OHPA + g + 1],
                                    scalar2=None, op0=ALU.mult))
                                ms.append(vector.tensor_scalar(
                                    out=scrA[:, 432:480],
                                    in0=scrA[:, g * 48:(g + 1) * 48],
                                    scalar1=smalls[:, OHPB + g:OHPB + g + 1],
                                    scalar2=None, op0=ALU.mult))
                                ms.append(vector.tensor_tensor(
                                    out=carry[:, 1168:1216],
                                    in0=carry[:, 1168:1216],
                                    in1=scrA[:, 384:432], op=ALU.add))
                                ms.append(vector.tensor_tensor(
                                    out=carry[:, 1216:1264],
                                    in0=carry[:, 1216:1264],
                                    in1=scrA[:, 432:480], op=ALU.add))
                            return ms
                        hexid = P.last()

                    @P.step("sync", deps=[INPJ], n_inst=0, n_dma=1)
                    def ZPARK(scalar, q=q):
                        return ("dma", scalar.dma_start(
                            out=park_zs.ap()[:, q * 8192:(q + 1) * 8192],
                            in_=bufC[:]))
                    prev_c = P.last()

                    P.phase = "mm_conv"
                    csils = [prev_c, prev_c]
                    for dhi in range(DHI):
                        cdeps = [csils[dhi]]
                        if dhi == 0 and hexid is not None:
                            cdeps.append(hexid)

                        @P.step("vector", deps=cdeps, n_inst=9)
                        def CONVV(vector, l=l, dhi=dhi, q=q, ci=ci):
                            if q % 2 == 0:
                                lead = carry[:, 1168 + ci * 48 + dhi * 3:
                                             1168 + ci * 48 + dhi * 3 + 3]
                            else:
                                lead = carry[:, 1120 + dhi * 3:1120 + dhi * 3 + 3]
                            ms = [vector.tensor_copy(out=scrG[:, 0:3], in_=lead),
                                  vector.tensor_copy(
                                      out=scrG[:, 3:515],
                                      in_=bufA[:, dhi * 512:(dhi + 1) * 512])]
                            for k in range(4):
                                wc = soff[f"cw{l}_{k}"][0]
                                ms.append(vector.tensor_scalar(
                                    out=scrD[:, k * 512:(k + 1) * 512],
                                    in0=scrG[:, k:k + 512],
                                    scalar1=smalls[:, wc + dhi:wc + dhi + 1],
                                    scalar2=None, op0=ALU.mult))
                            ms.append(vector.tensor_tensor(
                                out=scrE[:, 0:512], in0=scrD[:, 0:512],
                                in1=scrD[:, 512:1024], op=ALU.add))
                            ms.append(vector.tensor_tensor(
                                out=scrE[:, 512:1024], in0=scrD[:, 1024:1536],
                                in1=scrD[:, 1536:2048], op=ALU.add))
                            ms.append(vector.tensor_tensor(
                                out=scrE[:, 1024 + (dhi % 2) * 512:
                                         1024 + (dhi % 2) * 512 + 512],
                                in0=scrE[:, 0:512],
                                in1=scrE[:, 512:1024], op=ALU.add))
                            return ms
                        convv = P.last()

                        @P.step("scalar", deps=[convv])
                        def CSIL(scalar, dhi=dhi, cbc=cbc):
                            return scalar.activation(
                                out=bufB[:, dhi * 512:(dhi + 1) * 512],
                                in_=scrE[:, 1024 + (dhi % 2) * 512:
                                         1024 + (dhi % 2) * 512 + 512],
                                func=AF.Silu,
                                bias=smalls[:, cbc + dhi:cbc + dhi + 1])
                        csils.append(P.last())
                        prev_c = csils[-1]
                    CONVD = prev_c

                    if q % 2 == 0:
                        @P.step("vector", deps=[CONVD])
                        def QTL(vector):
                            return vector.tensor_copy(
                                out=carry[:, 1120:1168].rearrange(
                                    "p (dhi t) -> p dhi t", t=3),
                                in_=bufA[:].rearrange("p (dhi t) -> p dhi t",
                                                      t=512)[:, :, 509:512])
                        CONVD = P.last()

                    P.phase = "mm_xdt"

                    @P.step("sync", deps=[CONVD], n_inst=0, n_dma=1)
                    def XWLD(sync, l=l):
                        return ("dma", sync.dma_start(
                            out=wstr[:, 0:1536], in_=wvt(f"xp{l}", 0)))
                    xwld = P.last()

                    @P.step("tensor", deps=[xwld], n_inst=3)
                    def XMM(tensor):
                        outs = []
                        for (o0, o1, pp) in ((0, 64, ps_o[0:64, :]),
                                             (64, 80, ps_p[0:16, :]),
                                             (80, 96, ps_q[0:16, :])):
                            m = None
                            for fc in range(16):
                                m = tensor.matmul(
                                    pp, wstr[:, fc * 96 + o0:fc * 96 + o1],
                                    bufB[:, fc * 512:(fc + 1) * 512],
                                    start=(fc == 0), stop=(fc == 15))
                            outs.append(m)
                        return outs
                    xmm = P.last()

                    @P.step("scalar", deps=[xmm], n_inst=3)
                    def XEV(scalar):
                        return [scalar.activation(out=scrF[0:64, 0:512],
                                                  in_=ps_o[0:64, :], func=AF.Copy),
                                scalar.activation(out=scrG[0:16, 0:512],
                                                  in_=ps_p[0:16, :], func=AF.Copy),
                                scalar.activation(out=scrG[0:16, 512:1024],
                                                  in_=ps_q[0:16, :], func=AF.Copy)]
                    xev = P.last()

                    @P.step("sync", deps=[xev], n_inst=0, n_dma=2)
                    def BCST(scalar, q=q):
                        return [("dma", scalar.dma_start(
                            out=bc_dram.ap()[:, q * 1024:q * 1024 + 512],
                            in_=scrG[0:16, 0:512])),
                            ("dma", scalar.dma_start(
                                out=bc_dram.ap()[:, q * 1024 + 512:q * 1024 + 1024],
                                in_=scrG[0:16, 512:1024]))]
                    bcst = P.last()

                    @P.step("sync", deps=[bcst], n_inst=0, n_dma=1)
                    def DWLD(sync, l=l):
                        return ("dma", sync.dma_start(
                            out=wstr[0:64, 0:2048], in_=dtv_l[l][:, :]))
                    prev_dt = P.last()
                    dvrels = []

                    for g4 in range(4):
                        @P.step("tensor", deps=[prev_dt])
                        def DMM(tensor, g4=g4):
                            m = None
                            for k in range(4):
                                dhi = g4 * 4 + k
                                m = tensor.matmul(
                                    ps_big[:, k * 512:(k + 1) * 512],
                                    wstr[0:64, dhi * 128:(dhi + 1) * 128],
                                    scrF[0:64, 0:512], start=True, stop=True)
                            return m
                        dmm = P.last()

                        @P.step("scalar", deps=[dmm], n_inst=4)
                        def DABS(scalar, g4=g4, dtbc=dtbc):
                            ms = []
                            for k in range(4):
                                dhi = g4 * 4 + k
                                ms.append(scalar.activation(
                                    out=scrA[:, k * 512:(k + 1) * 512],
                                    in_=ps_big[:, k * 512:(k + 1) * 512],
                                    func=AF.Abs,
                                    bias=smalls[:, dtbc + dhi:dtbc + dhi + 1]))
                            return ms
                        dabs = P.last()

                        @P.step("scalar", deps=[dabs], n_inst=4)
                        def DEXP(scalar):
                            return [scalar.activation(
                                out=scrA[:, k * 512:(k + 1) * 512],
                                in_=scrA[:, k * 512:(k + 1) * 512],
                                func=AF.Exp, scale=smalls[:, CM1:CM1 + 1])
                                for k in range(4)]
                        dexp = P.last()

                        @P.step("scalar", deps=[dexp], n_inst=4)
                        def DLN(scalar):
                            return [scalar.activation(
                                out=scrD[:, k * 512:(k + 1) * 512],
                                in_=scrA[:, k * 512:(k + 1) * 512],
                                func=AF.Ln, bias=smalls[:, CONE:CONE + 1])
                                for k in range(4)]
                        dln = P.last()

                        @P.step("vector", deps=[dln], n_inst=8)
                        def DVREL(vector, g4=g4, dtbc=dtbc):
                            ms = []
                            for k in range(4):
                                dhi = g4 * 4 + k
                                ms.append(vector.tensor_scalar(
                                    out=bufA[:, dhi * 512:(dhi + 1) * 512],
                                    in0=ps_big[:, k * 512:(k + 1) * 512],
                                    scalar1=smalls[:, dtbc + dhi:dtbc + dhi + 1],
                                    scalar2=0.0, op0=ALU.add, op1=ALU.max))
                            for k in range(4):
                                dhi = g4 * 4 + k
                                ms.append(vector.tensor_tensor(
                                    out=bufA[:, dhi * 512:(dhi + 1) * 512],
                                    in0=bufA[:, dhi * 512:(dhi + 1) * 512],
                                    in1=scrD[:, k * 512:(k + 1) * 512],
                                    op=ALU.add))
                            return ms
                        prev_dt = P.last()
                        dvrels.append(prev_dt)

                        @P.step("sync", deps=[prev_dt], n_inst=0, n_dma=1)
                        def DTPK(sync, q=q, g4=g4):
                            return ("dma", sync.dma_start(
                                out=park_dt.ap()[:, q * 8192 + g4 * 2048:
                                                 q * 8192 + (g4 + 1) * 2048],
                                in_=bufA[:, g4 * 2048:(g4 + 1) * 2048]))

                    prev_scan = dvrels[-1]

                    P.phase = "mm_scan"
                    for dhi in range(DHI):
                        sdeps = [prev_scan]
                        if dhi % 4 == 0:
                            sdeps.append(dvrels[dhi // 4])

                        @P.step("vector", deps=sdeps, n_inst=3)
                        def DTXC(vector, dhi=dhi, ci=ci, q=q):
                            init = (0.0 if q % 2 == 0 else
                                    cumST[:, ci * 16 + dhi:ci * 16 + dhi + 1])
                            return [
                                vector.tensor_tensor(
                                    out=scrF[:, 0:512],
                                    in0=bufA[:, dhi * 512:(dhi + 1) * 512],
                                    in1=bufB[:, dhi * 512:(dhi + 1) * 512],
                                    op=ALU.mult),
                                vector.tensor_tensor_scan(
                                    out=scrF[:, 512:1024], data0=ones_bf[:],
                                    data1=bufA[:, dhi * 512:(dhi + 1) * 512],
                                    initial=init, op0=ALU.mult, op1=ALU.add),
                                vector.tensor_copy(
                                    out=cumST[:, ci * 16 + dhi:ci * 16 + dhi + 1],
                                    in_=scrF[:, 1023:1024])]
                        dtxc = P.last()

                        vsteps = [dtxc, dtxc]
                        accs = [dtxc, dtxc]
                        for n in range(16):
                            half = n % 2
                            pb = half * 1024
                            fs = 1024 + half * 512

                            @P.step("tensor", deps=[vsteps[n], accs[n]])
                            def BCB(tensor, n=n, pb=pb):
                                tensor.matmul(ps_big[:, pb:pb + 512],
                                              sel_view(n), scrG[0:16, 0:512],
                                              start=True, stop=True)
                                return tensor.matmul(
                                    ps_big[:, pb + 512:pb + 1024],
                                    sel_view(n), scrG[0:16, 512:1024],
                                    start=True, stop=True)
                            bcb = P.last()

                            @P.step("scalar", deps=[vsteps[n]])
                            def AEXP(scalar, dhi=dhi, n=n, half=half):
                                nn = dhi * 16 + n
                                return scalar.activation(
                                    out=abuf[:, half * 512:half * 512 + 512],
                                    in_=bufA[:, dhi * 512:(dhi + 1) * 512],
                                    func=AF.Exp, scale=negA[:, nn:nn + 1])
                            aexp = P.last()

                            es = (n % 4) * 512
                            vdeps = [bcb, aexp]
                            if n >= 4:
                                vdeps.append(accs[n - 2])

                            @P.step("vector", deps=vdeps, n_inst=4)
                            def VSTEP(vector, dhi=dhi, ci=ci, n=n, q=q,
                                      half=half, pb=pb, fs=fs, es=es):
                                sc = ci * 256 + dhi * 16 + n
                                init = (0.0 if q % 2 == 0 else
                                        state[:, sc:sc + 1])
                                return [
                                    vector.tensor_tensor(
                                        out=scrF[:, fs:fs + 512],
                                        in0=scrF[:, 0:512],
                                        in1=ps_big[:, pb:pb + 512],
                                        op=ALU.mult),
                                    vector.tensor_tensor_scan(
                                        out=scrE[:, es:es + 512],
                                        data0=abuf[:, half * 512:
                                                   half * 512 + 512],
                                        data1=scrF[:, fs:fs + 512],
                                        initial=init, op0=ALU.mult,
                                        op1=ALU.add),
                                    vector.tensor_copy(
                                        out=state[:, sc:sc + 1],
                                        in_=scrE[:, es + 511:es + 512]),
                                    vector.tensor_tensor(
                                        out=scrE[:, es:es + 512],
                                        in0=scrE[:, es:es + 512],
                                        in1=ps_big[:, pb + 512:pb + 1024],
                                        op=ALU.mult)]
                            vstep = P.last()
                            vsteps.append(vstep)

                            @P.step("gpsimd", deps=[vstep])
                            def PACC(gpsimd, n=n, es=es):
                                slot = n % 4
                                dst = scrD[:, slot * 512:(slot + 1) * 512]
                                if n < 4:
                                    return gpsimd.tensor_copy(
                                        out=dst, in_=scrE[:, es:es + 512])
                                return gpsimd.tensor_tensor(
                                    out=dst, in0=dst, in1=scrE[:, es:es + 512],
                                    op=ALU.add)
                            accs.append(P.last())

                        @P.step("vector", deps=[accs[-1], accs[-2]], n_inst=6)
                        def YSG(vector, dhi=dhi, dskc=dskc):
                            return [
                                vector.tensor_tensor(
                                    out=scrF[:, 0:512], in0=scrD[:, 0:512],
                                    in1=scrD[:, 512:1024], op=ALU.add),
                                vector.tensor_tensor(
                                    out=scrF[:, 512:1024], in0=scrD[:, 1024:1536],
                                    in1=scrD[:, 1536:2048], op=ALU.add),
                                vector.tensor_tensor(
                                    out=scrF[:, 0:512], in0=scrF[:, 0:512],
                                    in1=scrF[:, 512:1024], op=ALU.add),
                                vector.tensor_scalar(
                                    out=scrF[:, 1536:2048],
                                    in0=bufB[:, dhi * 512:(dhi + 1) * 512],
                                    scalar1=smalls[:, dskc + dhi:dskc + dhi + 1],
                                    scalar2=None, op0=ALU.mult),
                                vector.tensor_tensor(
                                    out=scrF[:, 0:512], in0=scrF[:, 0:512],
                                    in1=scrF[:, 1536:2048], op=ALU.add),
                                vector.tensor_tensor(
                                    out=bufC[:, dhi * 512:(dhi + 1) * 512],
                                    in0=scrF[:, 0:512],
                                    in1=bufC[:, dhi * 512:(dhi + 1) * 512],
                                    op=ALU.mult)]
                        prev_scan = P.last()
                    SCAND = prev_scan

                    P.phase = "mm_oproj"
                    omms = []
                    oadds = []
                    for dc in range(DC):
                        wb = wstr[:, 0:2048] if dc % 2 == 0 else bufA[:, 0:2048]
                        pp = ps_o if dc % 2 == 0 else ps_p
                        eng = ("sync", "gpsimd")[dc % 2]
                        wdeps = [omms[dc - 2]] if dc >= 2 else [SCAND]

                        @P.step(eng, deps=wdeps, n_inst=0, n_dma=1)
                        def OWLD(eh, l=l, dc=dc, wb=wb):
                            return ("dma", eh.dma_start(out=wb,
                                                        in_=wvt(f"op{l}", dc)))
                        owld = P.last()

                        mdeps = [owld] + ([oadds[dc - 2]] if dc >= 2 else [SCAND])

                        @P.step("tensor", deps=mdeps)
                        def OMM(tensor, wb=wb, pp=pp):
                            m = None
                            for dhi in range(DHI):
                                m = tensor.matmul(
                                    pp[:, :], wb[:, dhi * 128:(dhi + 1) * 128],
                                    bufC[:, dhi * 512:(dhi + 1) * 512],
                                    start=(dhi == 0), stop=(dhi == DHI - 1))
                            return m
                        omms.append(P.last())

                        @P.step("vector", deps=[omms[-1]])
                        def OADD(vector, dc=dc, t0q=t0q, pp=pp):
                            return vector.tensor_tensor(
                                out=r_sb[:, dc * T + t0q: dc * T + t0q + 512],
                                in0=r_sb[:, dc * T + t0q: dc * T + t0q + 512],
                                in1=pp[:, :], op=ALU.add)
                        oadds.append(P.last())
                    prev_q = oadds[-1]

                    if q % 2 == 1:
                        @P.step("vector", deps=[prev_q], n_inst=17)
                        def SCPG(vector, ci=ci):
                            ms = [vector.tensor_copy(
                                out=carry[:, 512 + ci * 256:512 + (ci + 1) * 256],
                                in_=state[:, ci * 256:(ci + 1) * 256])]
                            for dhi in range(DHI):
                                ms.append(vector.tensor_scalar(
                                    out=scrA[:, dhi * 16:(dhi + 1) * 16],
                                    in0=negA[:, dhi * 16:(dhi + 1) * 16],
                                    scalar1=cumST[:, ci * 16 + dhi:
                                                  ci * 16 + dhi + 1],
                                    scalar2=None, op0=ALU.mult))
                            return ms
                        scpg = P.last()

                        @P.step("scalar", deps=[scpg], n_inst=16)
                        def PEXP(scalar, ci=ci):
                            return [scalar.activation(
                                out=carry[:, ci * 256 + dhi * 16:
                                          ci * 256 + dhi * 16 + 16],
                                in_=scrA[:, dhi * 16:(dhi + 1) * 16],
                                func=AF.Exp) for dhi in range(DHI)]
                        prev_q = P.last()

                P.phase = "mm_comb"

                @P.step("sync", deps=[prev_q], n_inst=0, n_dma=1)
                def AGBW(sync):
                    return ("dma", sync.dma_start(
                        out=agb_loc.ap().rearrange("a (p c) -> (a p) c", c=1024),
                        in_=carry[:, 0:1024]))
                agbw = P.last()

                @P.step("gpsimd", deps=[agbw])
                def AGB(gpsimd, l=l):
                    return gpsimd.collective_compute(
                        "AllGather", ALU.bypass,
                        replica_groups=[[0, 1, 2, 3], [4, 5, 6, 7]],
                        ins=[agb_loc.ap().opt()], outs=[agb_all[l].ap().opt()])
                agb = P.last()

                @P.step("vector", deps=[agb], n_inst=2)
                def CMB0(vector):
                    return [vector.memset(scrB[:, 0:256], 0.0),
                            vector.memset(hin[:, 0:512], 0.0)]
                cprev = P.last()

                for k in range(1, 8):
                    g = k - 1
                    rk = g if g < 4 else 7 - g
                    sl = 0 if g < 4 else 1

                    @P.step("sync", deps=[cprev], n_inst=0, n_dma=2)
                    def CLDPS(sync, l=l, rk=rk, sl=sl):
                        v = agb_all[l].ap()[rk:rk + 1, :].rearrange(
                            "r (p c) -> (r p) c", c=1024)
                        return [("dma", sync.dma_start(
                            out=scrA[:, 0:256],
                            in_=v[:, sl * 256:(sl + 1) * 256])),
                            ("dma", sync.dma_start(
                                out=scrA[:, 256:512],
                                in_=v[:, 512 + sl * 256:512 + (sl + 1) * 256]))]
                    cld = P.last()

                    @P.step("vector", deps=[cld], n_inst=6)
                    def CMB(vector, k=k):
                        return [
                            vector.tensor_tensor(out=scrB[:, 0:256],
                                                 in0=scrB[:, 0:256],
                                                 in1=scrA[:, 0:256], op=ALU.mult),
                            vector.tensor_tensor(out=scrB[:, 0:256],
                                                 in0=scrB[:, 0:256],
                                                 in1=scrA[:, 256:512], op=ALU.add),
                            vector.tensor_scalar(
                                out=scrA[:, 512:768], in0=scrB[:, 0:256],
                                scalar1=smalls[:, OHA + k:OHA + k + 1],
                                scalar2=None, op0=ALU.mult),
                            vector.tensor_scalar(
                                out=scrA[:, 768:1024], in0=scrB[:, 0:256],
                                scalar1=smalls[:, OHB + k:OHB + k + 1],
                                scalar2=None, op0=ALU.mult),
                            vector.tensor_tensor(out=hin[:, 0:256],
                                                 in0=hin[:, 0:256],
                                                 in1=scrA[:, 512:768], op=ALU.add),
                            vector.tensor_tensor(out=hin[:, 256:512],
                                                 in0=hin[:, 256:512],
                                                 in1=scrA[:, 768:1024],
                                                 op=ALU.add)]
                    cprev = P.last()

                cqprev = cprev

                P.phase = "mm_corr"
                for q in range(4):
                    t0q = q * 512
                    ci = q // 2

                    @P.step("sync", deps=[cqprev], n_inst=0, n_dma=4)
                    def CRLD(sync, q=q):
                        return [
                            ("dma", sync.dma_start(
                                out=bufA[:],
                                in_=park_dt.ap()[:, q * 8192:(q + 1) * 8192])),
                            ("dma", sync.dma_start(
                                out=bufC[:],
                                in_=park_zs.ap()[:, q * 8192:(q + 1) * 8192])),
                            ("dma", sync.dma_start(
                                out=scrG[0:16, 0:512],
                                in_=bc_dram.ap()[:, q * 1024:q * 1024 + 512])),
                            ("dma", sync.dma_start(
                                out=scrG[0:16, 512:1024],
                                in_=bc_dram.ap()[:, q * 1024 + 512:
                                                 q * 1024 + 1024]))]
                    prev_cd = P.last()

                    for dhi in range(DHI):
                        @P.step("vector", deps=[prev_cd], n_inst=2)
                        def CCUM(vector, dhi=dhi, ci=ci, q=q):
                            init = (0.0 if q % 2 == 0 else
                                    cumST[:, ci * 16 + dhi:ci * 16 + dhi + 1])
                            return [
                                vector.tensor_tensor_scan(
                                    out=scrF[:, 512:1024], data0=ones_bf[:],
                                    data1=bufA[:, dhi * 512:(dhi + 1) * 512],
                                    initial=init, op0=ALU.mult, op1=ALU.add),
                                vector.tensor_copy(
                                    out=cumST[:, ci * 16 + dhi:ci * 16 + dhi + 1],
                                    in_=scrF[:, 1023:1024])]
                        ccum = P.last()

                        cvs = [ccum, ccum]
                        cbhs = [ccum, ccum]
                        for n in range(16):
                            half = n % 2
                            pb = half * 1024
                            fs = 1024 + half * 512

                            @P.step("tensor", deps=[cvs[n], cbhs[n]])
                            def CCB(tensor, n=n, pb=pb):
                                return tensor.matmul(
                                    ps_big[:, pb:pb + 512], sel_view(n),
                                    scrG[0:16, 512:1024], start=True, stop=True)
                            ccb = P.last()

                            @P.step("scalar", deps=[cvs[n]])
                            def CEX(scalar, dhi=dhi, n=n, half=half):
                                nn = dhi * 16 + n
                                return scalar.activation(
                                    out=scrE[:, 1024 + half * 512:
                                             1024 + half * 512 + 512],
                                    in_=scrF[:, 512:1024], func=AF.Exp,
                                    scale=negA[:, nn:nn + 1])
                            cex = P.last()

                            @P.step("scalar", deps=[ccb, cvs[n]])
                            def CBH(scalar, n=n, ci=ci, dhi=dhi, half=half,
                                    pb=pb):
                                hc = ci * 256 + dhi * 16 + n
                                return scalar.activation(
                                    out=scrE[:, half * 512:half * 512 + 512],
                                    in_=ps_big[:, pb:pb + 512],
                                    func=AF.Identity,
                                    scale=hin[:, hc:hc + 1])
                            cbhs.append(P.last())

                            @P.step("vector", deps=[cex, cbhs[-1]], n_inst=2)
                            def CV(vector, n=n, half=half, fs=fs):
                                m1 = vector.tensor_tensor(
                                    out=scrF[:, fs:fs + 512],
                                    in0=scrE[:, fs:fs + 512],
                                    in1=scrE[:, half * 512:half * 512 + 512],
                                    op=ALU.mult)
                                if n == 0:
                                    m2 = vector.tensor_copy(
                                        out=scrD[:, 0:512],
                                        in_=scrF[:, fs:fs + 512])
                                else:
                                    m2 = vector.tensor_tensor(
                                        out=scrD[:, 0:512], in0=scrD[:, 0:512],
                                        in1=scrF[:, fs:fs + 512], op=ALU.add)
                                return [m1, m2]
                            cvs.append(P.last())
                        cps = [cvs[-1]]

                        @P.step("vector", deps=[cps[-1]])
                        def CGATE(vector, dhi=dhi):
                            return vector.tensor_tensor(
                                out=bufC[:, dhi * 512:(dhi + 1) * 512],
                                in0=scrD[:, 0:512],
                                in1=bufC[:, dhi * 512:(dhi + 1) * 512],
                                op=ALU.mult)
                        prev_cd = P.last()

                    P.phase = "mm_coproj"
                    omms2 = []
                    oadds2 = []
                    for dc in range(DC):
                        wb = wstr[:, 0:2048] if dc % 2 == 0 else bufB[:, 0:2048]
                        pp = ps_o if dc % 2 == 0 else ps_p
                        eng = ("sync", "gpsimd")[dc % 2]
                        wdeps = [omms2[dc - 2]] if dc >= 2 else [prev_cd]

                        @P.step(eng, deps=wdeps, n_inst=0, n_dma=1)
                        def OWLD2(eh, l=l, dc=dc, wb=wb):
                            return ("dma", eh.dma_start(out=wb,
                                                        in_=wvt(f"op{l}", dc)))
                        owld2 = P.last()

                        mdeps = [owld2] + ([oadds2[dc - 2]] if dc >= 2
                                           else [prev_cd])

                        @P.step("tensor", deps=mdeps)
                        def OMM2(tensor, wb=wb, pp=pp):
                            m = None
                            for dhi in range(DHI):
                                m = tensor.matmul(
                                    pp[:, :], wb[:, dhi * 128:(dhi + 1) * 128],
                                    bufC[:, dhi * 512:(dhi + 1) * 512],
                                    start=(dhi == 0), stop=(dhi == DHI - 1))
                            return m
                        omms2.append(P.last())

                        @P.step("vector", deps=[omms2[-1]])
                        def OADD2(vector, dc=dc, t0q=t0q, pp=pp):
                            return vector.tensor_tensor(
                                out=r_sb[:, dc * T + t0q: dc * T + t0q + 512],
                                in0=r_sb[:, dc * T + t0q: dc * T + t0q + 512],
                                in1=pp[:, :], op=ALU.add)
                        oadds2.append(P.last())
                    prev_omm2 = oadds2[-1]
                    cqprev = prev_omm2
                prev = cqprev

            P.phase = "mm_fin"
            lnf = ln_stream(r_sb, [prev],
                            ln_apply(r_sb, r_sb, gamma="nfw", beta="nfb"))
            radd = lnf[-1]
            for tt in range(4):
                for grp in range(2):
                    @P.step("sync", deps=[radd], n_inst=0, n_dma=4)
                    def RLD(sync, tt=tt, grp=grp):
                        ms = []
                        for k in range(4):
                            dc = grp * 4 + k
                            ms.append(("dma", sync.dma_start(
                                out=scrA[:, k * 512:(k + 1) * 512],
                                in_=park_r.ap()[:, dc * T + tt * 512:
                                                dc * T + tt * 512 + 512])))
                        return ms
                    rld = P.last()

                    @P.step("vector", deps=[rld], n_inst=4)
                    def RADDF(vector, tt=tt, grp=grp):
                        ms = []
                        for k in range(4):
                            dc = grp * 4 + k
                            ms.append(vector.tensor_tensor(
                                out=r_sb[:, dc * T + tt * 512:
                                         dc * T + tt * 512 + 512],
                                in0=r_sb[:, dc * T + tt * 512:
                                         dc * T + tt * 512 + 512],
                                in1=scrA[:, k * 512:(k + 1) * 512], op=ALU.add))
                        return ms
                    radd = P.last()
            return radd

        if BUILD_UPTO >= 3:
            SUBL3 = mamba_block(SUBL2)
        else:
            SUBL3 = SUBL2

        if BUILD_UPTO >= 4:
            P.phase = "ffn2"
            ln4_ids = ln_stream(r_sb, [SUBL3], ln_apply(r_sb, xn_sb))
            SUBL4 = ffn_block(ln4_ids[-1], pre_dep=SUBL3)
        else:
            SUBL4 = SUBL3

        FINAL = SUBL4

        P.phase = "out"

        @P.step("sync", deps=[FINAL], n_inst=0, n_dma=1)
        def OUT(sync):
            return ("dma", sync.dma_start(
                out=out_ext[:].rearrange("(dc p) t -> p dc t", p=128),
                in_=r_sb[:].rearrange("p (dc t) -> p dc t", t=T)))
        outid = P.last()

        @P.step("sync", deps=[outid])
        def FIN(sync):
            return sync.nop()

        with nc.Block() as block:
            P.emit(block, sems, nc)

    return nc



def kernel(**inputs):
    flat, woff = _weight_layout(inputs)
    smalls, soff = _smalls_layout(inputs)
    S = flat.size
    meta = {"S": S, "NSM": smalls.shape[1], "woff": woff, "soff": soff}

    x = inputs["L0_attn_a_x"]
    wslices = flat.reshape(8, S // 8)

    in_maps = []
    for c in range(8):
        b = c // 4
        jj = c % 4
        cA, cB = jj, 7 - jj
        xloc = np.concatenate([x[b, cA * CH:(cA + 1) * CH],
                               x[b, cB * CH:(cB + 1) * CH]], axis=0)
        in_maps.append({
            "xin": np.ascontiguousarray(xloc.T),
            "wsl": wslices[c:c + 1],
            "smalls": _core_smalls(smalls, soff, jj),
            "rope": _rope_dup((cA, cB)),
            "dyn": np.array([[jj, 0, 0, 0, 0, 0, 0, 0]], np.int32),
        })

    nc = build_program(meta)
    res = run_bass_kernel_spmd(nc, in_maps, core_ids=list(range(8)))

    out = np.zeros((B, TG, D), np.float32)
    for c in range(8):
        b = c // 4
        jj = c % 4
        cA, cB = jj, 7 - jj
        o = np.asarray(res.results[c]["out"])
        o = o.reshape(DC, 128, T).transpose(2, 0, 1).reshape(T, D)
        out[b, cA * CH:(cA + 1) * CH] = o[0:CH]
        out[b, cB * CH:(cB + 1) * CH] = o[CH:]
    return out


# version 10
# speedup vs baseline: 1.1397x; 1.1397x over previous
import numpy as np
import ml_dtypes
import concourse.bass as bass
import concourse.mybir as mybir
from concourse.bass_utils import run_bass_kernel_spmd

F32 = mybir.dt.float32
BF16 = mybir.dt.bfloat16
I32 = mybir.dt.int32
AF = mybir.ActivationFunctionType
ALU = mybir.AluOpType
AX = mybir.AxisListType.X

B = 2
TG = 8192
T = 2048
CH = 1024
D = 1024
DC = 8
H = 16
DH = 64
DI = 2048
DHI = 16
DS = 16
DTR = 64
NL = 4
E = 8
DF = 4096
EPS = 1e-5
QT = 512
bf = ml_dtypes.bfloat16

ACC_ENG = "gpsimd"
BUILD_UPTO = 4
DETECT_RACES = True



def _weight_layout(inp):
    parts, off, pos = [], {}, 0

    def add(name, arr):
        nonlocal pos
        a = np.ascontiguousarray(arr, dtype=np.float32)
        off[name] = (pos, a.shape)
        parts.append(a.ravel())
        pos += a.size

    def addt(name, arr, mw=128):
        nonlocal pos
        a = np.ascontiguousarray(arr, dtype=np.float32)
        K, M = a.shape
        nm = M // mw
        t = a.reshape(K // 128, 128, nm, mw).transpose(2, 1, 0, 3)
        off[name] = (pos, (K, M, mw, "t"))
        parts.append(np.ascontiguousarray(t).ravel())
        pos += a.size

    addt("W1T", inp["L0_attn_a_W1"].T)
    addt("W2T", inp["L0_attn_a_W2"].T)
    addt("gate", inp["L0_ffn_gate_w"], mw=8)
    for e in range(E):
        addt(f"fc{e}", inp["L0_ffn_fc_w"][e])
    for e in range(E):
        addt(f"pj{e}", inp["L0_ffn_proj_w"][e])
    for l in range(NL):
        addt(f"in{l}", inp["L0_attn_b_in_proj_w"][l].T)
    for l in range(NL):
        addt(f"xp{l}", inp["L0_attn_b_x_proj_w"][l].T, mw=96)
    for l in range(NL):
        add(f"dtp{l}", inp["L0_attn_b_dt_proj_w"][l].T)
    for l in range(NL):
        addt(f"op{l}", inp["L0_attn_b_out_proj_w"][l].T)
    flat = np.concatenate(parts)
    pad = (-flat.size) % 8192
    if pad:
        flat = np.concatenate([flat, np.zeros(pad, np.float32)])
    return flat, off


def _smalls_layout(inp):
    cols, off = [], {}

    def addc(name, arr2d):
        off[name] = (sum(c.shape[1] for c in cols), arr2d.shape[1])
        cols.append(np.ascontiguousarray(arr2d, np.float32))

    for l in range(NL):
        addc(f"nw{l}", inp["L0_attn_b_norm_w"][l].reshape(DC, 128).T)
        addc(f"nb{l}", inp["L0_attn_b_norm_b"][l].reshape(DC, 128).T)
    addc("nfw", inp["L0_attn_b_normf_w"].reshape(DC, 128).T)
    addc("nfb", inp["L0_attn_b_normf_b"].reshape(DC, 128).T)
    for l in range(NL):
        for k in range(4):
            addc(f"cw{l}_{k}", inp["L0_attn_b_conv_w"][l, :, k].reshape(DHI, 128).T)
        addc(f"cb{l}", inp["L0_attn_b_conv_b"][l].reshape(DHI, 128).T)
        addc(f"dtb{l}", inp["L0_attn_b_dt_proj_b"][l].reshape(DHI, 128).T)
        addc(f"dsk{l}", inp["L0_attn_b_D_skip"][l].reshape(DHI, 128).T)
        al = inp["L0_attn_b_A_log"][l].reshape(DHI, 128, DS)
        addc(f"alog{l}", np.transpose(al, (1, 0, 2)).reshape(128, DHI * DS))
    addc("ceps", np.full((128, 1), EPS, np.float32))
    addc("cm05", np.full((128, 1), -0.5, np.float32))
    addc("c0125", np.full((128, 1), 0.125, np.float32))
    addc("cm1", np.full((128, 1), -1.0, np.float32))
    addc("cone", np.full((128, 1), 1.0, np.float32))
    for nm in ("ohA", "ohB", "ohPA", "ohPB"):
        addc(nm, np.zeros((128, 8), np.float32))
    packed = np.concatenate(cols, axis=1)
    return np.ascontiguousarray(packed), off


def _core_smalls(smalls, soff, jj):
    s = smalls.copy()

    def setcol(name, k):
        if 0 <= k < 8:
            s[:, soff[name][0] + k] = 1.0
    setcol("ohA", jj)
    setcol("ohB", 7 - jj)
    setcol("ohPA", jj - 1)
    setcol("ohPB", 6 - jj)
    return s


def _rope_dup(chunks):
    hs = DH
    inv = 1.0 / (10000.0 ** (np.arange(0, hs, 2, dtype=np.float64) / hs))
    pos = np.concatenate([np.arange(c * CH, (c + 1) * CH) for c in chunks])
    ang = pos[:, None].astype(np.float64) * inv[None, :]
    cos = np.concatenate([np.cos(ang), np.cos(ang)], axis=1).T
    sin = np.concatenate([np.sin(ang), np.sin(ang)], axis=1).T
    out = np.zeros((128, 2 * T), np.float32)
    out[0:64, 0:T] = cos
    out[64:128, 0:T] = cos
    out[0:64, T:] = sin
    out[64:128, T:] = sin
    return out


def _tri_masks():
    m = np.zeros((128, 4 * QT), np.float32)
    kk = np.arange(128)[:, None]
    qq = np.arange(QT)[None, :]
    for i in range(4):
        m[:, i * QT:(i + 1) * QT] = (i * 128 + kk <= qq)
    return m.astype(bf)


def _rot_perm():
    p = np.zeros((64, 64), np.float32)
    for jj in range(32):
        p[jj + 32, jj] = -1.0
        p[jj, jj + 32] = 1.0
    return np.vstack([p, p]).astype(bf)


def _sel16():
    s = np.zeros((16, 16 * 128), np.float32)
    for n in range(16):
        s[n, n * 128:(n + 1) * 128] = 1.0
    return s.astype(bf)



class Prog:
    ENGS = ("sync", "vector", "scalar", "tensor", "gpsimd")

    def __init__(self):
        self.steps = []
        self.done_at = []
        self.counts = {e: [0, 0] for e in self.ENGS}
        self.phase = "init"
        self.phases = []

    def step(self, eng, deps=None, n_inst=1, n_dma=0):
        sid = len(self.steps)
        if deps is None:
            deps = [sid - 1] if sid > 0 else []
        deps = [d for d in deps if d is not None and d >= 0]

        def reg(fn):
            self.steps.append((eng, fn, list(deps), n_inst, n_dma))
            self.phases.append(self.phase)
            c = self.counts[eng]
            c[0] += n_inst
            c[1] += n_dma
            da = {}
            if n_inst:
                da[eng + "_i"] = c[0]
            if n_dma:
                da[eng + "_d"] = 16 * c[1]
            if not da:
                da[eng + "_i"] = c[0]
            self.done_at.append(da)
            return len(self.steps) - 1
        return reg

    def last(self):
        return len(self.steps) - 1

    def emit(self, block, sems, nc=None):
        per_eng = {e: [] for e in self.ENGS}
        for sid, st in enumerate(self.steps):
            per_eng[st[0]].append((sid,) + st[1:])
        from contextlib import nullcontext

        def make(eng):
            items = per_eng[eng]

            def body(handle):
                for sid, fn, deps, n_inst, n_dma in items:
                    need = {}
                    for d in deps:
                        for sk, dv in self.done_at[d].items():
                            need[sk] = max(need.get(sk, 0), dv)
                    for sk in sorted(need):
                        handle.wait_ge(sems[sk], need[sk])
                    with (nc.named_scope(self.phases[sid]) if nc is not None
                          else nullcontext()):
                        marks = fn(handle)
                    if marks is None:
                        marks = []
                    elif isinstance(marks, tuple) and len(marks) == 2 and marks[0] == "dma":
                        marks = [marks]
                    elif not isinstance(marks, (list, tuple)):
                        marks = [marks]
                    gi = gd = 0
                    for m in marks:
                        if isinstance(m, tuple) and m[0] == "dma":
                            m[1].then_inc(sems[eng + "_d"], 16)
                            gd += 1
                        else:
                            m.then_inc(sems[eng + "_i"], 1)
                            gi += 1
                    assert gi == n_inst and gd == n_dma, \
                        f"step {sid} {eng}: declared ({n_inst},{n_dma}) got ({gi},{gd})"
            return body

        for eng in self.ENGS:
            getattr(block, eng)(make(eng))



def build_program(meta):
    nc = bass.Bass("TRN2", target_bir_lowering=False, debug=False,
                   detect_race_conditions=DETECT_RACES)
    S = meta["S"]; S8 = S // 8
    NSM = meta["NSM"]
    woff = meta["woff"]; soff = meta["soff"]

    PC = DH * T
    KVN = H * 3 * PC
    AOB = 4 * DH * T
    AON = 4 * AOB

    xin = nc.dram_tensor("xin", [D, T], F32, kind="ExternalInput").ap()
    wsl = nc.dram_tensor("wsl", [1, S8], F32, kind="ExternalInput").ap()
    smalls_in = nc.dram_tensor("smalls", [128, NSM], F32, kind="ExternalInput").ap()
    rope_in = nc.dram_tensor("rope", [128, 2 * T], F32, kind="ExternalInput").ap()
    dyn_in = nc.dram_tensor("dyn", [1, 8], I32, kind="ExternalInput").ap()
    out_ext = nc.dram_tensor("out", [DC * 128, T], F32, kind="ExternalOutput").ap()

    wbf_sl = nc.dram_tensor("wbf_sl", [1, S8], BF16)
    wbf = nc.dram_tensor("wbf", [8, S8], BF16)
    kv_loc = nc.dram_tensor("kv_loc", [1, KVN], BF16)
    kv_all_t = nc.dram_tensor("kv_all", [4, KVN], BF16)
    ao_loc = nc.dram_tensor("ao_loc", [1, AON], BF16)
    ao_all_t = nc.dram_tensor("ao_all", [4, AON], BF16)
    kvstage = nc.dram_tensor("kvstage", [4, 12 * PC], BF16)
    w8_dram = nc.dram_tensor("w8_dram", [8, T], BF16)
    aostage = nc.dram_tensor("aostage", [4, 4 * PC], BF16)
    park_r = nc.dram_tensor("park_r", [128, DC * T], F32)
    park_zs = nc.dram_tensor("park_zs", [128, DHI * T], BF16)
    aga_loc = nc.dram_tensor("aga_loc", [1, 2 * 48 * 128], F32)
    aga_all = [nc.dram_tensor(f"aga_all{l}", [4, 2 * 48 * 128], F32)
               for l in range(NL)]
    agb_loc = nc.dram_tensor("agb_loc", [1, 2 * 2 * 256 * 128], F32)
    agb_all = [nc.dram_tensor(f"agb_all{l}", [4, 2 * 2 * 256 * 128], F32)
               for l in range(NL)]
    bc_dram = nc.dram_tensor("bc_dram", [16, 4 * 1024], BF16)
    park_dt = nc.dram_tensor("park_dt", [128, DHI * T], BF16)

    wflat = wbf.ap().rearrange("r s -> (r s)")

    def wv(name):
        pos, shape = woff[name]
        K, M = shape
        return wflat[pos:pos + K * M].rearrange("(k m) -> k m", m=M)

    def wvt(name, mb):
        pos, (K, M, mw, _) = woff[name]
        blk = K * mw
        return wflat[pos + mb * blk: pos + (mb + 1) * blk].rearrange(
            "(p x) -> p x", x=(K // 128) * mw)

    def wvt2(name, mb2):
        pos, (K, M, mw, _) = woff[name]
        blk = K * mw
        return wflat[pos + 2 * mb2 * blk: pos + (2 * mb2 + 2) * blk].rearrange(
            "(b p x) -> p b x", p=128, x=(K // 128) * mw)

    CEPS = soff["ceps"][0]
    CM05 = soff["cm05"][0]
    C0125 = soff["c0125"][0]
    CM1 = soff["cm1"][0]

    ident = nc.inline_tensor(np.eye(128, dtype=bf), name="ident128").ap()
    ident32 = nc.inline_tensor(np.eye(128, dtype=np.float32), name="ident128f").ap()
    rotp = nc.inline_tensor(_rot_perm(), name="rotperm").ap()
    trimask = nc.inline_tensor(_tri_masks(), name="trimask").ap()
    sel16c = nc.inline_tensor(_sel16(), name="sel16").ap()
    ones1x128 = nc.inline_tensor(np.ones((1, 128), np.float32), name="o1x128c").ap()
    ones128c = nc.inline_tensor(np.ones((128, 1), np.float32), name="o128cc").ap()

    from contextlib import ExitStack
    stack = ExitStack()
    sbuf = lambda n, s, d=F32: stack.enter_context(nc.sbuf_tensor(n, s, d))
    psum = lambda n, s, d=F32: stack.enter_context(nc.psum_tensor(n, s, d))
    semf = lambda n: stack.enter_context(nc.semaphore(n))

    with stack:
        r_sb = sbuf("r_sb", [128, DC * T])
        xn_sb = sbuf("xn_sb", [128, DC * T], BF16)
        bufA = sbuf("bufA", [128, 8192], BF16)
        bufB = sbuf("bufB", [128, 8192], BF16)
        bufC = sbuf("bufC", [128, 8192], BF16)
        wstr = sbuf("wstr", [128, 2048], BF16)
        abuf = sbuf("abuf", [128, 1024])
        scrA = sbuf("scrA", [128, 2048])
        scrB = sbuf("scrB", [128, 512])
        scrD = sbuf("scrD", [128, 2048], BF16)
        scrE = sbuf("scrE", [128, 2048], BF16)
        scrF = sbuf("scrF", [128, 2048], BF16)
        cst = sbuf("cst", [128, 2048], BF16)
        smalls = sbuf("smalls_sb", [128, NSM])
        rows_a = sbuf("rows_a", [1, 512])
        rows_b = sbuf("rows_b", [1, 512])
        rows_c = sbuf("rows_c", [1, 512])
        id_sb = sbuf("id_sb", [128, 128], BF16)
        id32_sb = sbuf("id32_sb", [128, 128])
        gsb = sbuf("gsb", [128, 64], BF16)
        rot_sb = sbuf("rot_sb", [128, 64], BF16)
        o1x = sbuf("o1x", [1, 128])
        o128 = sbuf("o128", [128, 1])
        mxc = sbuf("mxc", [128, 8])
        mxv = sbuf("mxv", [1, 16])
        dyn_sb = sbuf("dyn_sb", [1, 8], I32)
        carry = sbuf("carry_sb", [128, 1296])
        state = sbuf("state_sb", [128, 2 * 256])
        hin = state
        scrG = sbuf("scrG", [128, 1024], BF16)
        ones_bf = sbuf("ones_bf", [128, 512], BF16)
        negA = scrB[:, 256:512]
        cumST = carry[:, 1264:1296]
        kfull = bufA[0:64, :]
        qfull = bufC[0:64, :]
        vstb = bufB[:, 0:64 * 65]
        ropeC = bufC
        tri_sb = cst

        ps_big = psum("ps_big", [128, 2048])
        ps_o = psum("ps_o", [128, 512])
        ps_p = psum("ps_p", [128, 512])
        ps_q = psum("ps_q", [128, 512])
        ps_t = psum("ps_t", [128, 1024], BF16)

        sems = {}
        for e in Prog.ENGS:
            sems[e + "_i"] = semf("sem_" + e + "_i")
            sems[e + "_d"] = semf("sem_" + e + "_d")
        P = Prog()
        regs = {}
        P.phase = "load"

        @P.step("sync", deps=[], n_inst=0, n_dma=9)
        def LOADS(sync):
            return [
                ("dma", sync.dma_start(out=smalls[:], in_=smalls_in[:])),
                ("dma", sync.dma_start(out=id_sb[:], in_=ident[:])),
                ("dma", sync.dma_start(out=id32_sb[:], in_=ident32[:])),
                ("dma", sync.dma_start(out=rot_sb[:], in_=rotp[:])),
                ("dma", sync.dma_start(out=cst[:], in_=trimask[:])),
                                ("dma", sync.dma_start(out=o1x[:], in_=ones1x128[:])),
                ("dma", sync.dma_start(out=o128[:], in_=ones128c[:])),
                ("dma", sync.dma_start(out=dyn_sb[:], in_=dyn_in[:])),
                ("dma", sync.dma_start(
                    out=r_sb[:].rearrange("p (dc t) -> p dc t", t=T),
                    in_=xin.rearrange("(dc p) t -> p dc t", p=128))),
            ]
        LOADS_ID = P.last()

        prev_rl = LOADS_ID
        for rr in range(4):
            @P.step("sync", deps=[prev_rl], n_inst=0, n_dma=1)
            def ROPELD(sync, rr=rr):
                return ("dma", sync.dma_start(
                    out=abuf[:, 0:1024], in_=rope_in[:, rr * 1024:(rr + 1) * 1024]))
            rl = P.last()

            @P.step("vector", deps=[rl])
            def ROPECAST(vector, rr=rr):
                return vector.tensor_copy(out=bufC[:, rr * 1024:(rr + 1) * 1024],
                                          in_=abuf[:, 0:1024])
            prev_rl = P.last()
        ROPE_ID = P.last()

        P.phase = "castag"
        CPC = S8 // 128
        ncast = (CPC + 1023) // 1024
        wsl2 = wsl.rearrange("a (p c) -> a p c", p=128)
        wbf2 = wbf_sl.ap().rearrange("a (p c) -> a p c", p=128)
        prev_store = ROPE_ID
        for i in range(ncast):
            c0 = i * 1024
            c1 = min(CPC, c0 + 1024)

            @P.step("sync", deps=[prev_store], n_inst=0, n_dma=1)
            def CLD(sync, c0=c0, c1=c1):
                return ("dma", sync.dma_start(out=abuf[:, 0:c1 - c0],
                                              in_=wsl2[0, :, c0:c1]))
            ld = P.last()

            @P.step("vector", deps=[ld, prev_store])
            def CCAST(vector, c0=c0, c1=c1):
                return vector.tensor_copy(out=wstr[:, 0:c1 - c0],
                                          in_=abuf[:, 0:c1 - c0])
            cast_id = P.last()

            @P.step("scalar", deps=[cast_id], n_inst=0, n_dma=1)
            def CST(scalar, c0=c0, c1=c1):
                return ("dma", scalar.dma_start(out=wbf2[0, :, c0:c1],
                                                in_=wstr[:, 0:c1 - c0]))
            prev_store = P.last()

        @P.step("gpsimd", deps=[prev_store])
        def AGW(gpsimd):
            return gpsimd.collective_compute(
                "AllGather", ALU.bypass,
                replica_groups=[[0, 1, 2, 3, 4, 5, 6, 7]],
                ins=[wbf_sl.ap().opt()], outs=[wbf.ap().opt()])
        AGW_ID = P.last()

        def ln_stream(src, first_deps, apply_cb):
            out_ids = []
            prev = None
            for tt in range(4):
                t0 = tt * 512
                mmlast = None
                for rnd in range(2):
                    deps = list(first_deps) if (tt == 0 and rnd == 0) else None

                    @P.step("scalar", deps=deps, n_inst=4)
                    def SQ(scalar, t0=t0, rnd=rnd):
                        ms = []
                        for k in range(4):
                            dc = rnd * 4 + k
                            ms.append(scalar.activation(
                                out=scrA[:, k * 512:(k + 1) * 512],
                                in_=src[:, dc * T + t0: dc * T + t0 + 512],
                                func=AF.Square))
                        return ms
                    sq = P.last()

                    @P.step("tensor", deps=[sq])
                    def MM(tensor, t0=t0, rnd=rnd):
                        m = None
                        for k in range(4):
                            dc = rnd * 4 + k
                            tensor.matmul(ps_q[0:1, :], o128[:],
                                          src[:, dc * T + t0: dc * T + t0 + 512],
                                          start=(dc == 0), stop=(dc == DC - 1))
                            m = tensor.matmul(ps_q[32:33, :], o128[:],
                                              scrA[:, k * 512:(k + 1) * 512],
                                              start=(dc == 0), stop=(dc == DC - 1))
                        return m
                    mmlast = P.last()

                @P.step("vector", deps=[mmlast])
                def STAT1(vector):
                    vector.tensor_scalar(out=rows_a[:], in0=ps_q[0:1, :],
                                         scalar1=1.0 / D, scalar2=None, op0=ALU.mult)
                    return vector.tensor_scalar(out=rows_b[:], in0=ps_q[32:33, :],
                                                scalar1=1.0 / D, scalar2=None,
                                                op0=ALU.mult)
                v1 = P.last()

                @P.step("scalar", deps=[v1])
                def MUSQ(scalar):
                    return scalar.activation(out=rows_c[:], in_=rows_a[:],
                                             func=AF.Square)
                s1 = P.last()

                @P.step("vector", deps=[s1])
                def VARV(vector):
                    return vector.tensor_tensor(out=rows_b[:], in0=rows_b[:],
                                                in1=rows_c[:], op=ALU.subtract)
                v2 = P.last()

                @P.step("scalar", deps=[v2])
                def RSTD1(scalar):
                    return scalar.activation(out=rows_c[:], in_=rows_b[:],
                                             func=AF.Ln,
                                             bias=smalls[0:1, CEPS:CEPS + 1])
                s2a = P.last()

                @P.step("scalar", deps=[s2a])
                def RSTD2(scalar):
                    return scalar.activation(out=rows_b[:], in_=rows_c[:],
                                             func=AF.Exp,
                                             scale=smalls[0:1, CM05:CM05 + 1])
                s2 = P.last()

                deps_b = [s2] if prev is None else [s2, prev]

                @P.step("tensor", deps=deps_b)
                def BCAST(tensor):
                    tensor.matmul(ps_o[:, :], o1x[:], rows_a[:],
                                  start=True, stop=True)
                    return tensor.matmul(ps_p[:, :], o1x[:], rows_b[:],
                                         start=True, stop=True)
                bc = P.last()
                prev = apply_cb(tt, bc)
                out_ids.append(prev)
            return out_ids

        def ln_apply(src, dst_bf, gamma=None, beta=None):
            def cb(tt, dep):
                t0 = tt * 512
                last = dep
                for grp in range(2):
                    @P.step("vector", deps=[last], n_inst=4)
                    def SUBS(vector, grp=grp):
                        ms = []
                        for k in range(4):
                            dc = grp * 4 + k
                            ms.append(vector.tensor_tensor(
                                out=scrA[:, k * 512:(k + 1) * 512],
                                in0=src[:, dc * T + t0: dc * T + t0 + 512],
                                in1=ps_o[:, :], op=ALU.subtract))
                        return ms
                    subs = P.last()

                    @P.step("vector", deps=[subs], n_inst=4)
                    def MULS(vector, grp=grp):
                        ms = []
                        for k in range(4):
                            dc = grp * 4 + k
                            tgt = (scrA[:, k * 512:(k + 1) * 512] if gamma
                                   else dst_bf[:, dc * T + t0: dc * T + t0 + 512])
                            ms.append(vector.tensor_tensor(
                                out=tgt,
                                in0=scrA[:, k * 512:(k + 1) * 512],
                                in1=ps_p[:, :], op=ALU.mult))
                        return ms
                    muls = P.last()
                    last = muls
                    if gamma is not None:
                        gc = soff[gamma][0]
                        bcc = soff[beta][0]

                        @P.step("scalar", deps=[muls], n_inst=4)
                        def GB(scalar, grp=grp):
                            ms = []
                            for k in range(4):
                                dc = grp * 4 + k
                                ms.append(scalar.activation(
                                    out=dst_bf[:, dc * T + t0: dc * T + t0 + 512],
                                    in_=scrA[:, k * 512:(k + 1) * 512],
                                    func=AF.Identity,
                                    scale=smalls[:, gc + dc:gc + dc + 1],
                                    bias=smalls[:, bcc + dc:bcc + dc + 1]))
                            return ms
                        last = P.last()
                return last
            return cb

        P.phase = "ln1"
        ln1_ids = ln_stream(r_sb, [LOADS_ID], ln_apply(r_sb, xn_sb))
        LN1_DONE = ln1_ids[-1]

        kvl = kv_loc.ap().rearrange("a n -> (a n)")

        P.phase = "qkv"
        prev_rope_consumer = None
        prev_half_consumer = [None, None]
        prev_psbig_consumer = [None, None]
        prev_scrD_consumer = [None, None]
        prev_scrG_consumer = [None, None]
        last_pub = []

        for mc in range(24):
            wdeps = [AGW_ID, LN1_DONE]
            if prev_half_consumer[mc % 2] is not None:
                wdeps.append(prev_half_consumer[mc % 2])

            @P.step("sync", deps=wdeps, n_inst=0, n_dma=1)
            def WLD(sync, mc=mc):
                return ("dma", sync.dma_start(
                    out=wstr[:, (mc % 2) * 1024:(mc % 2) * 1024 + 1024],
                    in_=wvt("W1T", mc)))
            wld_id = P.last()
            mm_ids = []

            for tt in range(4):
                t0 = tt * 512
                pb = (tt % 2) * 1024
                mmdeps = [wld_id]
                if prev_psbig_consumer[tt % 2] is not None:
                    mmdeps.append(prev_psbig_consumer[tt % 2])

                @P.step("tensor", deps=mmdeps)
                def QKVMM(tensor, mc=mc, t0=t0, pb=pb):
                    m = None
                    for kc in range(DC):
                        m = tensor.matmul(
                            ps_big[:, pb:pb + 512],
                            wstr[:, (mc % 2) * 1024 + kc * 128:
                                 (mc % 2) * 1024 + kc * 128 + 128],
                            xn_sb[:, kc * T + t0: kc * T + t0 + 512],
                            start=(kc == 0), stop=(kc == DC - 1))
                    return m
                mmid = P.last()
                mm_ids.append(mmid)

                evdeps = [mmid]
                if prev_scrD_consumer[tt % 2] is not None:
                    evdeps.append(prev_scrD_consumer[tt % 2])

                @P.step("scalar", deps=evdeps)
                def EVQKV(scalar, pb=pb, tt=tt):
                    return scalar.activation(
                        out=scrD[:, (tt % 2) * 512:(tt % 2) * 512 + 512],
                        in_=ps_big[:, pb:pb + 512], func=AF.Copy)
                ev = P.last()
                prev_psbig_consumer[tt % 2] = ev

                if mc < 16:
                    is_q = mc < 8
                    h0 = (mc % 8) * 2
                    rotdeps = [ev]
                    if prev_rope_consumer is not None:
                        rotdeps.append(prev_rope_consumer)

                    @P.step("tensor", deps=rotdeps)
                    def ROTMM(tensor, tt=tt):
                        c0 = (tt % 2) * 512
                        tensor.matmul(ps_q[0:64, :], rot_sb[0:64, :],
                                      scrD[0:64, c0:c0 + 512],
                                      start=True, stop=True)
                        return tensor.matmul(ps_q[64:128, :], rot_sb[64:128, :],
                                             scrD[64:128, c0:c0 + 512],
                                             start=True, stop=True)
                    rot = P.last()

                    rpdeps = [rot, ROPE_ID]
                    if prev_scrG_consumer[tt % 2] is not None:
                        rpdeps.append(prev_scrG_consumer[tt % 2])

                    @P.step("vector", deps=rpdeps, n_inst=2)
                    def ROPEAP(vector, t0=t0, tt=tt):
                        c0 = (tt % 2) * 512
                        m1 = vector.tensor_tensor(
                            out=scrE[:, c0:c0 + 512],
                            in0=scrD[:, c0:c0 + 512],
                            in1=bufC[:, t0:t0 + 512], op=ALU.mult)
                        m2 = vector.tensor_tensor(
                            out=scrF[:, c0:c0 + 512],
                            in0=ps_q[:, :],
                            in1=bufC[:, T + t0:T + t0 + 512], op=ALU.mult)
                        return [m1, m2]
                    rpa = P.last()

                    @P.step("vector", deps=[rpa])
                    def ROPEADD(vector, tt=tt):
                        c0 = (tt % 2) * 512
                        return vector.tensor_tensor(
                            out=scrE[:, c0:c0 + 512],
                            in0=scrE[:, c0:c0 + 512],
                            in1=scrF[:, c0:c0 + 512], op=ALU.add)
                    rp = P.last()
                    prev_rope_consumer = rp
                    prev_scrD_consumer[tt % 2] = rp

                    @P.step("scalar", deps=[rp], n_inst=0, n_dma=2)
                    def QKPUB(scalar, t0=t0, tt=tt, h0=h0, is_q=is_q):
                        c0 = (tt % 2) * 512
                        kind = 1 if is_q else 0
                        ms = []
                        for hh in range(2):
                            base = (h0 + hh) * 3 * PC + kind * PC
                            dst = kvl[base:base + PC]
                            dst = dst.rearrange("(dh t) -> dh t", t=T)[:, t0:t0 + 512]
                            ms.append(("dma", scalar.dma_start(
                                out=dst,
                                in_=scrE[hh * 64:hh * 64 + 64, c0:c0 + 512])))
                        return ms
                    pub = P.last()
                    last_pub.append(pub)
                    prev_scrG_consumer[tt % 2] = pub
                else:
                    vtdeps = [ev]
                    if prev_rope_consumer is not None:
                        vtdeps.append(prev_rope_consumer)

                    @P.step("tensor", deps=vtdeps)
                    def VT(tensor, tt=tt):
                        c0 = (tt % 2) * 512
                        m = None
                        for blk in range(4):
                            m = tensor.transpose(
                                ps_t[:, blk * 128:blk * 128 + 128],
                                scrD[:, c0 + blk * 128:c0 + blk * 128 + 128],
                                id_sb[:])
                        return m
                    vt = P.last()

                    vtedeps = [vt]
                    if prev_scrG_consumer[tt % 2] is not None:
                        vtedeps.append(prev_scrG_consumer[tt % 2])

                    @P.step("vector", deps=vtedeps)
                    def VTEV(vector, tt=tt):
                        c0 = (tt % 2) * 512
                        return vector.tensor_copy(out=scrE[:, c0:c0 + 512],
                                                  in_=ps_t[:, 0:512])
                    vte = P.last()
                    prev_rope_consumer = vte
                    prev_scrD_consumer[tt % 2] = vte

                    @P.step("scalar", deps=[vte], n_inst=0, n_dma=8)
                    def VPUB(scalar, mc=mc, t0=t0, tt=tt):
                        c0 = (tt % 2) * 512
                        h0 = (mc - 16) * 2
                        ms = []
                        for blk in range(4):
                            tg0 = t0 + blk * 128
                            for hh in range(2):
                                base = (h0 + hh) * 3 * PC + 2 * PC
                                vtv = kvl[base:base + PC].rearrange(
                                    "(t dh) -> t dh", dh=DH)
                                ms.append(("dma", scalar.dma_start(
                                    out=vtv[tg0:tg0 + 128, :],
                                    in_=scrE[:, c0 + blk * 128 + hh * 64:
                                             c0 + blk * 128 + hh * 64 + 64])))
                        return ms
                    pub = P.last()
                    last_pub.append(pub)
                    prev_scrG_consumer[tt % 2] = pub
            prev_half_consumer[mc % 2] = mm_ids[-1]

        P.phase = "attn"

        @P.step("gpsimd", deps=[last_pub[-1]])
        def AGKV(gpsimd):
            return gpsimd.collective_compute(
                "AllGather", ALU.bypass,
                replica_groups=[[0, 1, 2, 3], [4, 5, 6, 7]],
                ins=[kv_loc.ap().opt()], outs=[kv_all_t.ap().opt()])
        AGKV_ID = P.last()

        def rank_half_of_chunk(c):
            return (c, 0) if c < 4 else (7 - c, 1)

        ao_l = ao_loc.ap().rearrange("a n -> (a n)")
        aol_v = ao_l

        @P.step("sync", deps=[AGKV_ID], n_inst=0, n_dma=4)
        def HDYN(sync):
            if "g" not in regs:
                regs["g"] = sync.alloc_register("rg")
                regs["o"] = sync.alloc_register("ro")
            rg, ro = regs["g"], regs["o"]
            sync.reg_load(rg, dyn_sb[0:1, 0:1])
            ms = []
            for rk in range(4):
                sync.reg_mul(ro, rg, 12 * PC)
                sync.reg_add(ro, ro, rk * KVN)
                ms.append(("dma", sync.dma_start(
                    out=kvstage.ap()[rk:rk + 1, :],
                    in_=bass.AP(kv_all_t, ro, [[1, 1], [1, 12 * PC]]))))
            return ms
        HDYN_ID = P.last()

        prev_qt_chain = HDYN_ID
        exps_hist = [HDYN_ID, HDYN_ID]
        pvmm_hist = [HDYN_ID, HDYN_ID]
        gp = 0
        for hl in range(4):
            @P.step("sync", deps=[HDYN_ID, prev_qt_chain], n_inst=0, n_dma=24)
            def HLD(sync, hl=hl):
                ms = []
                for rk in range(4):
                    base = hl * 3 * PC
                    kpiece = kvstage.ap()[rk, base:base + PC].rearrange(
                        "(dh t) -> dh t", t=T)
                    qpiece = kvstage.ap()[rk, base + PC:base + 2 * PC].rearrange(
                        "(dh t) -> dh t", t=T)
                    vpiece = kvstage.ap()[rk, base + 2 * PC:base + 3 * PC].rearrange(
                        "(t dh) -> t dh", dh=DH)
                    for half in range(2):
                        c = rk if half == 0 else 7 - rk
                        ms.append(("dma", sync.dma_start(
                            out=kfull[:, c * CH:(c + 1) * CH],
                            in_=kpiece[:, half * CH:(half + 1) * CH])))
                        ms.append(("dma", sync.dma_start(
                            out=qfull[:, c * CH:(c + 1) * CH],
                            in_=qpiece[:, half * CH:(half + 1) * CH])))
                        dst = vstb[:, c * 8 * 65:(c + 1) * 8 * 65].rearrange(
                            "p (b c2) -> p b c2", c2=65)[:, :, 0:64]
                        ms.append(("dma", sync.dma_start(
                            out=dst,
                            in_=vpiece[half * CH:(half + 1) * CH, :].rearrange(
                                "(b p) dh -> p b dh", p=128))))
                return ms
            hld = P.last()

            @P.step("vector", deps=[hld])
            def VONES(vector):
                return vector.memset(
                    vstb[:].rearrange("p (b c2) -> p b c2", c2=65)[:, :, 64:65], 1.0)
            vones = P.last()

            nm_last = []
            prev_nmx = None
            for which, buf in (("q", qfull), ("k", kfull)):
                for span in range(8):
                    s0 = span * 1024

                    @P.step("scalar", deps=[vones] if span == 0 else None)
                    def NSQ(scalar, buf=buf, s0=s0):
                        return scalar.activation(out=abuf[0:64, 0:1024],
                                                 in_=buf[:, s0:s0 + 1024],
                                                 func=AF.Square)
                    nsq = P.last()

                    nmmdeps = [nsq]
                    if prev_nmx is not None:
                        nmmdeps.append(prev_nmx)

                    @P.step("tensor", deps=nmmdeps)
                    def NMM(tensor):
                        m = None
                        for i in range(2):
                            m = tensor.matmul(ps_big[0:1, i * 512:i * 512 + 512],
                                              o128[0:64, :],
                                              abuf[0:64, i * 512:i * 512 + 512],
                                              start=True, stop=True)
                        return m
                    nmm = P.last()

                    @P.step("vector", deps=[nmm])
                    def NMXa(vector):
                        return vector.reduce_max(out=rows_c[0:1, 0:1],
                                                 in_=ps_big[0:1, 0:1024], axis=AX)
                    nmxa = P.last()

                    @P.step("vector", deps=[nmxa])
                    def NMXb(vector, which=which, hl=hl, span=span):
                        col = hl if which == "q" else 8 + hl
                        if span == 0:
                            return vector.tensor_copy(out=mxv[0:1, col:col + 1],
                                                      in_=rows_c[0:1, 0:1])
                        return vector.tensor_tensor(
                            out=mxv[0:1, col:col + 1],
                            in0=mxv[0:1, col:col + 1],
                            in1=rows_c[0:1, 0:1], op=ALU.max)
                    prev_nmx = P.last()
                    nm_last.append(prev_nmx)

            @P.step("vector", deps=[nm_last[-1]])
            def MH(vector, hl=hl):
                return vector.tensor_tensor(out=rows_c[0:1, 0:1],
                                            in0=mxv[0:1, hl:hl + 1],
                                            in1=mxv[0:1, 8 + hl:8 + hl + 1],
                                            op=ALU.mult)
            mh1 = P.last()

            @P.step("scalar", deps=[mh1])
            def MH2(scalar):
                return scalar.activation(out=rows_c[0:1, 2:3],
                                         in_=rows_c[0:1, 0:1], func=AF.Sqrt)
            mh2 = P.last()

            @P.step("vector", deps=[mh2])
            def MH3(vector):
                return vector.tensor_scalar(out=rows_c[0:1, 3:4],
                                            in0=rows_c[0:1, 2:3],
                                            scalar1=-0.125, scalar2=None,
                                            op0=ALU.mult)
            mh3 = P.last()

            @P.step("tensor", deps=[mh3])
            def MHB(tensor):
                return tensor.matmul(ps_q[:, 0:1], o1x[:], rows_c[0:1, 3:4],
                                     start=True, stop=True)
            mhb = P.last()

            @P.step("vector", deps=[mhb])
            def MHC(vector, hl=hl):
                return vector.tensor_copy(out=mxc[:, hl:hl + 1],
                                          in_=ps_q[:, 0:1])
            mhc = P.last()

            prev_oev = mhc
            prev_onrm = mhc
            prev_aow = mhc
            for qt in range(16):
                nkb = 4 * qt + 4
                npr = (nkb + 1) // 2
                qt_pv = []
                for g in range(npr):
                    kbs = [k for k in (2 * g, 2 * g + 1) if k < nkb]
                    half = gp % 2
                    pb = half * 1024

                    @P.step("tensor", deps=[exps_hist[gp], mhc])
                    def STMM(tensor, qt=qt, kbs=kbs, pb=pb):
                        m = None
                        for i, kb in enumerate(kbs):
                            m = tensor.matmul(
                                ps_big[:, pb + i * 512:pb + i * 512 + 512],
                                kfull[:, kb * 128:(kb + 1) * 128],
                                qfull[:, qt * 512:qt * 512 + 512],
                                start=True, stop=True)
                        return m
                    stmm = P.last()

                    @P.step("scalar", deps=[stmm, pvmm_hist[gp]],
                            n_inst=len(kbs))
                    def EXPS(scalar, hl=hl, kbs=kbs, pb=pb, half=half):
                        ms = []
                        for i, kb in enumerate(kbs):
                            ms.append(scalar.activation(
                                out=scrD[:, half * 1024 + i * 512:
                                         half * 1024 + i * 512 + 512],
                                in_=ps_big[:, pb + i * 512:pb + i * 512 + 512],
                                func=AF.Exp, scale=smalls[:, C0125:C0125 + 1],
                                bias=mxc[:, hl:hl + 1]))
                        return ms
                    exps = P.last()
                    exps_hist.append(exps)

                    diag = [kb for kb in kbs if kb >= 4 * qt]
                    dep_mm2 = exps
                    if diag:
                        @P.step("vector", deps=[exps], n_inst=len(diag))
                        def MASK(vector, diag=diag, kbs=kbs, qt=qt, half=half):
                            ms = []
                            for kb in diag:
                                i = kb - kbs[0]
                                mi = kb - 4 * qt
                                ms.append(vector.tensor_tensor(
                                    out=scrD[:, half * 1024 + i * 512:
                                             half * 1024 + i * 512 + 512],
                                    in0=scrD[:, half * 1024 + i * 512:
                                             half * 1024 + i * 512 + 512],
                                    in1=tri_sb[:, mi * 512:(mi + 1) * 512],
                                    op=ALU.mult))
                            return ms
                        dep_mm2 = P.last()

                    pvdeps = [dep_mm2]
                    if g == 0:
                        pvdeps.append(prev_oev)

                    @P.step("tensor", deps=pvdeps)
                    def PVMM(tensor, kbs=kbs, half=half, first=(g == 0),
                             lastg=(g == npr - 1)):
                        m = None
                        for i, kb in enumerate(kbs):
                            m = tensor.matmul(
                                ps_o[0:65, :],
                                vstb[:, kb * 65:(kb + 1) * 65],
                                scrD[:, half * 1024 + i * 512:
                                     half * 1024 + i * 512 + 512],
                                start=(first and i == 0),
                                stop=(lastg and i == len(kbs) - 1))
                        return m
                    pv = P.last()
                    pvmm_hist.append(pv)
                    qt_pv.append(pv)
                    gp += 1

                @P.step("vector", deps=[qt_pv[-1], prev_onrm, prev_aow])
                def ZREC(vector):
                    return vector.reciprocal(out=rows_c[0:1, 0:512],
                                             in_=ps_o[64:65, :])
                zr = P.last()

                @P.step("tensor", deps=[zr])
                def ZB(tensor):
                    return tensor.matmul(ps_p[0:64, :], o1x[0:1, 0:64],
                                         rows_c[0:1, 0:512],
                                         start=True, stop=True)
                zb = P.last()

                @P.step("scalar", deps=[zb])
                def OEV(scalar):
                    return scalar.activation(out=scrE[0:64, 0:512],
                                             in_=ps_o[0:64, :], func=AF.Copy)
                prev_oev = P.last()

                @P.step("vector", deps=[prev_oev])
                def ONRM(vector):
                    return vector.tensor_tensor(out=scrF[0:64, 0:512],
                                                in0=scrE[0:64, 0:512],
                                                in1=ps_p[0:64, :], op=ALU.mult)
                prev_onrm = P.last()

                c = qt // 2
                rk, half2 = rank_half_of_chunk(c)
                lt = half2 * CH + (qt % 2) * 512

                @P.step("scalar", deps=[prev_onrm], n_inst=0, n_dma=1)
                def AOW(scalar, hl=hl, rk=rk, lt=lt):
                    dst = aol_v[rk * AOB + hl * DH * T:
                                rk * AOB + (hl + 1) * DH * T]
                    dst = dst.rearrange("(dh t) -> dh t", t=T)[:, lt:lt + 512]
                    return ("dma", scalar.dma_start(out=dst,
                                                    in_=scrF[0:64, 0:512]))
                prev_aow = P.last()
                prev_qt_chain = prev_aow

        @P.step("gpsimd", deps=[prev_qt_chain])
        def AGAO(gpsimd):
            return gpsimd.collective_compute(
                "AllGather", ALU.bypass,
                replica_groups=[[0, 1, 2, 3], [4, 5, 6, 7]],
                ins=[ao_loc.ap().opt()], outs=[ao_all_t.ap().opt()])
        AGAO_ID = P.last()

        @P.step("sync", deps=[AGAO_ID], n_inst=0, n_dma=4)
        def AODYN(sync):
            rg, ro = regs["g"], regs["o"]
            sync.reg_load(rg, dyn_sb[0:1, 0:1])
            ms = []
            for r in range(4):
                sync.reg_mul(ro, rg, AOB)
                sync.reg_add(ro, ro, r * AON)
                ms.append(("dma", sync.dma_start(
                    out=aostage.ap()[r:r + 1, :],
                    in_=bass.AP(ao_all_t, ro, [[1, 1], [1, AOB]]))))
            return ms
        aodyn = P.last()

        @P.step("sync", deps=[aodyn], n_inst=0, n_dma=16)
        def AORD(sync):
            ms = []
            for h in range(H):
                r, hlh = h // 4, h % 4
                dc, p0 = h // 2, (h % 2) * 64
                ms.append(("dma", sync.dma_start(
                    out=xn_sb[p0:p0 + 64, dc * T:(dc + 1) * T],
                    in_=aostage.ap()[r, hlh * PC:(hlh + 1) * PC]
                        .rearrange("(dh t) -> dh t", t=T))))
            return ms
        AORD_ID = P.last()

        P.phase = "w2"
        prev = AORD_ID
        prev_half = [AORD_ID, AORD_ID]
        prev_pb = [AORD_ID, AORD_ID]
        for dc in range(DC):
            w2deps = [prev_half[dc % 2]] if dc >= 2 else [AGKV_ID]

            @P.step("sync", deps=w2deps, n_inst=0, n_dma=1)
            def W2LD(sync, dc=dc):
                return ("dma", sync.dma_start(
                    out=wstr[:, (dc % 2) * 1024:(dc % 2) * 1024 + 1024],
                    in_=wvt("W2T", dc)))
            wld2 = P.last()
            mmlist = []
            for tt in range(4):
                t0 = tt * 512

                @P.step("tensor", deps=[wld2, prev_pb[tt % 2], AORD_ID])
                def W2MM(tensor, dc=dc, t0=t0, tt=tt):
                    pp = (tt % 2) * 1024
                    m = None
                    for kc in range(DC):
                        m = tensor.matmul(
                            ps_big[:, pp:pp + 512],
                            wstr[:, (dc % 2) * 1024 + kc * 128:
                                 (dc % 2) * 1024 + kc * 128 + 128],
                            xn_sb[:, kc * T + t0: kc * T + t0 + 512],
                            start=(kc == 0), stop=(kc == DC - 1))
                    return m
                w2mm = P.last()

                @P.step("vector", deps=[w2mm])
                def RADD(vector, dc=dc, t0=t0, tt=tt):
                    pp = (tt % 2) * 1024
                    return vector.tensor_tensor(
                        out=r_sb[:, dc * T + t0: dc * T + t0 + 512],
                        in0=r_sb[:, dc * T + t0: dc * T + t0 + 512],
                        in1=ps_big[:, pp:pp + 512], op=ALU.add)
                prev_pb[tt % 2] = P.last()
                mmlist.append(w2mm)
            prev_half[dc % 2] = mmlist[-1]
        SUBL1 = P.last()


        def ffn_block(prev_done, pre_dep=None):
            @P.step("sync", deps=[prev_done, AGW_ID], n_inst=0, n_dma=1)
            def GLD(sync):
                return ("dma", sync.dma_start(out=gsb[:], in_=wvt("gate", 0)))
            prev = P.last()

            for tt in range(4):
                t0 = tt * 512

                @P.step("tensor", deps=[prev])
                def GMM(tensor, t0=t0):
                    m = None
                    for kc in range(DC):
                        m = tensor.matmul(
                            ps_q[0:8, :], gsb[:, kc * 8:(kc + 1) * 8],
                            xn_sb[:, kc * T + t0: kc * T + t0 + 512],
                            start=(kc == 0), stop=(kc == DC - 1))
                    return m
                gmm = P.last()

                @P.step("scalar", deps=[gmm])
                def GEV(scalar):
                    return scalar.activation(out=scrB[0:8, 0:512],
                                             in_=ps_q[0:8, :], func=AF.Copy)
                gev = P.last()

                @P.step("tensor", deps=[gev])
                def GTR(tensor):
                    m = None
                    for blk in range(4):
                        m = tensor.transpose(ps_p[:, blk * 8:(blk + 1) * 8],
                                             scrB[0:8, blk * 128:(blk + 1) * 128],
                                             id32_sb[0:8, 0:8])
                    return m
                gtr = P.last()

                @P.step("vector", deps=[gtr])
                def GT2(vector):
                    return vector.tensor_copy(out=scrA[:, 1024:1056],
                                              in_=ps_p[:, 0:32])
                gt2 = P.last()

                @P.step("vector", deps=[gt2], n_inst=4)
                def TOPA1(vector):
                    ms = []
                    for blk in range(4):
                        g0 = 1024 + blk * 8
                        mx1 = 1056 + blk
                        ms.append(vector.reduce_max(
                            out=scrA[:, mx1:mx1 + 1],
                            in_=scrA[:, g0:g0 + 8], axis=AX))
                    return ms
                topa1 = P.last()

                @P.step("vector", deps=[topa1], n_inst=4)
                def TOPA2(vector):
                    ms = []
                    for blk in range(4):
                        g0 = 1024 + blk * 8
                        mx1 = 1056 + blk
                        d1 = 1064 + blk * 8
                        ms.append(vector.tensor_scalar(
                            out=scrA[:, d1:d1 + 8], in0=scrA[:, g0:g0 + 8],
                            scalar1=scrA[:, mx1:mx1 + 1], scalar2=None,
                            op0=ALU.subtract))
                    return ms
                topa = P.last()

                @P.step("scalar", deps=[topa], n_inst=4)
                def SGN1(scalar):
                    ms = []
                    for blk in range(4):
                        d1 = 1064 + blk * 8
                        s1 = 1096 + blk * 8
                        ms.append(scalar.activation(
                            out=scrA[:, s1:s1 + 8], in_=scrA[:, d1:d1 + 8],
                            func=AF.Sign))
                    return ms
                sgn1 = P.last()

                @P.step("vector", deps=[sgn1], n_inst=4)
                def S4a(vector):
                    ms = []
                    for blk in range(4):
                        s1 = 1096 + blk * 8
                        mk1 = 1128 + blk * 8
                        ms.append(vector.tensor_scalar(
                            out=scrA[:, mk1:mk1 + 8], in0=scrA[:, s1:s1 + 8],
                            scalar1=1.0, scalar2=None, op0=ALU.add))
                    return ms
                s4a = P.last()

                @P.step("vector", deps=[s4a], n_inst=4)
                def S4b(vector):
                    ms = []
                    for blk in range(4):
                        g0 = 1024 + blk * 8
                        mk1 = 1128 + blk * 8
                        g2 = 1160 + blk * 8
                        ms.append(vector.scalar_tensor_tensor(
                            out=scrA[:, g2:g2 + 8], in0=scrA[:, mk1:mk1 + 8],
                            scalar=-1e9, in1=scrA[:, g0:g0 + 8],
                            op0=ALU.mult, op1=ALU.add))
                    return ms
                s4 = P.last()

                @P.step("vector", deps=[s4], n_inst=4)
                def S5(vector):
                    ms = []
                    for blk in range(4):
                        g2 = 1160 + blk * 8
                        mx2 = 1192 + blk
                        ms.append(vector.reduce_max(
                            out=scrA[:, mx2:mx2 + 1],
                            in_=scrA[:, g2:g2 + 8], axis=AX))
                    return ms
                s5 = P.last()

                @P.step("vector", deps=[s5], n_inst=4 * 2)
                def S6(vector):
                    ms = []
                    for blk in range(4):
                        g2 = 1160 + blk * 8
                        mx1 = 1056 + blk
                        mx2 = 1192 + blk
                        d12 = 1196 + blk
                        d2 = 1208 + blk * 8
                        ms.append(vector.tensor_scalar(
                            out=scrA[:, d2:d2 + 8], in0=scrA[:, g2:g2 + 8],
                            scalar1=scrA[:, mx2:mx2 + 1], scalar2=None,
                            op0=ALU.subtract))
                        ms.append(vector.tensor_tensor(
                            out=scrA[:, d12:d12 + 1], in0=scrA[:, mx1:mx1 + 1],
                            in1=scrA[:, mx2:mx2 + 1], op=ALU.subtract))
                    return ms
                topb = P.last()

                @P.step("scalar", deps=[topb], n_inst=8)
                def S7(scalar):
                    ms = []
                    for blk in range(4):
                        d12 = 1196 + blk
                        w1 = 1200 + blk
                        d2 = 1208 + blk * 8
                        s2 = 1240 + blk * 8
                        ms.append(scalar.activation(
                            out=scrA[:, w1:w1 + 1], in_=scrA[:, d12:d12 + 1],
                            func=AF.Sigmoid))
                        ms.append(scalar.activation(
                            out=scrA[:, s2:s2 + 8], in_=scrA[:, d2:d2 + 8],
                            func=AF.Sign))
                    return ms
                sigw = P.last()

                @P.step("vector", deps=[sigw], n_inst=4 * 2)
                def S8(vector):
                    ms = []
                    for blk in range(4):
                        w1 = 1200 + blk
                        w2 = 1204 + blk
                        s2 = 1240 + blk * 8
                        mk2 = 1272 + blk * 8
                        ms.append(vector.tensor_scalar(
                            out=scrA[:, w2:w2 + 1], in0=scrA[:, w1:w1 + 1],
                            scalar1=-1.0, scalar2=1.0, op0=ALU.mult, op1=ALU.add))
                        ms.append(vector.tensor_scalar(
                            out=scrA[:, mk2:mk2 + 8], in0=scrA[:, s2:s2 + 8],
                            scalar1=1.0, scalar2=None, op0=ALU.add))
                    return ms
                s8 = P.last()

                @P.step("vector", deps=[s8], n_inst=4 * 2)
                def S9a(vector):
                    ms = []
                    for blk in range(4):
                        mk1 = 1128 + blk * 8
                        mk2 = 1272 + blk * 8
                        w1 = 1200 + blk
                        w2 = 1204 + blk
                        t2 = 1304 + blk * 8
                        wt = 1336 + blk * 8
                        ms.append(vector.tensor_scalar(
                            out=scrA[:, wt:wt + 8], in0=scrA[:, mk1:mk1 + 8],
                            scalar1=scrA[:, w1:w1 + 1], scalar2=None,
                            op0=ALU.mult))
                        ms.append(vector.tensor_scalar(
                            out=scrA[:, t2:t2 + 8], in0=scrA[:, mk2:mk2 + 8],
                            scalar1=scrA[:, w2:w2 + 1], scalar2=None,
                            op0=ALU.mult))
                    return ms
                s9a = P.last()

                @P.step("vector", deps=[s9a], n_inst=4)
                def S9b(vector):
                    ms = []
                    for blk in range(4):
                        t2 = 1304 + blk * 8
                        wt = 1336 + blk * 8
                        ms.append(vector.tensor_tensor(
                            out=scrA[:, wt:wt + 8], in0=scrA[:, wt:wt + 8],
                            in1=scrA[:, t2:t2 + 8], op=ALU.add))
                    return ms
                wbuild = P.last()

                @P.step("tensor", deps=[wbuild])
                def WTR(tensor):
                    m = None
                    for blk in range(4):
                        wt = 1336 + blk * 8
                        m = tensor.transpose(ps_q[0:8, blk * 128:(blk + 1) * 128]
                                             if False else
                                             ps_p[0:8, blk * 128:(blk + 1) * 128],
                                             scrA[:, wt:wt + 8], id32_sb[:])
                    return m
                wtr = P.last()

                @P.step("scalar", deps=[wtr])
                def W8EV(scalar):
                    return scalar.activation(out=scrD[0:8, 0:512],
                                             in_=ps_p[0:8, 0:512], func=AF.Copy)
                w8ev = P.last()

                @P.step("scalar", deps=[w8ev], n_inst=0, n_dma=1)
                def W8ST(scalar, t0=t0):
                    return ("dma", scalar.dma_start(
                        out=w8_dram.ap()[:, t0:t0 + 512], in_=scrD[0:8, 0:512]))
                prev = P.last()

            w8_done = prev
            ring_cons = []
            gel_hist = []
            prev_last_pmm = w8_done
            prev_cmbs = [w8_done, w8_done]
            last = w8_done
            rglob = 0
            for tc in range(4):
                t0 = tc * 512

                @P.step("sync", deps=[last, w8_done], n_inst=0, n_dma=8)
                def WBLD(sync, t0=t0):
                    ms = []
                    for e in range(E):
                        slot = scrD if e < 4 else scrE
                        ms.append(("dma", sync.dma_start(
                            out=slot[:, (e % 4) * 512:(e % 4) * 512 + 512],
                            in_=bass.AP(w8_dram, e * T + t0, [[0, 128], [1, 512]]))))
                    return ms
                wbld = P.last()

                for e in range(E):
                    gels_this = []
                    for fcp in range(16):
                        slot = rglob % 4
                        eng = ("sync", "gpsimd", "sync", "scalar")[rglob % 4]
                        if rglob >= 4:
                            ldeps = [ring_cons[rglob - 4]]
                        else:
                            ldeps = [wbld] if pre_dep is None else [pre_dep]

                        @P.step(eng, deps=ldeps, n_inst=0, n_dma=1)
                        def FLD(eh, e=e, fcp=fcp, slot=slot):
                            return ("dma", eh.dma_start(
                                out=bufC[:, slot * 2048:(slot + 1) * 2048]
                                    .rearrange("p (b x) -> p b x", b=2),
                                in_=wvt2(f"fc{e}", fcp)))
                        ld = P.last()

                        mdeps = [ld, wbld]
                        if len(gel_hist) >= 1:
                            mdeps.append(gel_hist[-1])

                        @P.step("tensor", deps=mdeps)
                        def FMM(tensor, t0=t0, fcp=fcp, slot=slot):
                            m = None
                            for i in range(2):
                                for kc in range(DC):
                                    m = tensor.matmul(
                                        ps_big[:, (fcp % 2) * 1024 + i * 512:
                                               (fcp % 2) * 1024 + i * 512 + 512],
                                        bufC[:, slot * 2048 + i * 1024 + kc * 128:
                                             slot * 2048 + i * 1024 + kc * 128 + 128],
                                        xn_sb[:, kc * T + t0: kc * T + t0 + 512],
                                        start=(kc == 0), stop=(kc == DC - 1))
                            return m
                        fmm = P.last()
                        ring_cons.append(fmm)
                        rglob += 1

                        @P.step("scalar", deps=[fmm, prev_last_pmm])
                        def GEL(scalar, fcp=fcp):
                            hb = bufA if fcp < 8 else bufB
                            return scalar.activation(
                                out=hb[:, (fcp % 8) * 1024:(fcp % 8) * 1024 + 1024],
                                in_=ps_big[:, (fcp % 2) * 1024:
                                           (fcp % 2) * 1024 + 1024],
                                func=AF.Gelu)
                        gel_hist.append(P.last())
                        gels_this.append(gel_hist[-1])

                    for dc in range(DC):
                        pp = ps_o if dc % 2 == 0 else ps_p
                        pmm = None
                        for php in range(2):
                            slot = rglob % 4
                            eng = ("sync", "gpsimd", "sync", "scalar")[rglob % 4]
                            ldeps = [ring_cons[rglob - 4]]

                            @P.step(eng, deps=ldeps, n_inst=0, n_dma=1)
                            def PLD(eh, e=e, dc=dc, php=php, slot=slot):
                                return ("dma", eh.dma_start(
                                    out=bufC[:, slot * 2048:(slot + 1) * 2048],
                                    in_=wvt(f"pj{e}", dc)
                                        [:, php * 2048:(php + 1) * 2048]))
                            ld = P.last()

                            mdeps = [ld, gels_this[php * 8 + 7]]
                            if php == 0:
                                mdeps.append(prev_cmbs[dc % 2])

                            @P.step("tensor", deps=mdeps)
                            def PMM(tensor, dc=dc, php=php, slot=slot, pp=pp):
                                m = None
                                for b2 in range(16):
                                    fc = php * 16 + b2
                                    hb = bufA if fc < 16 else bufB
                                    m = tensor.matmul(
                                        pp[:, :],
                                        bufC[:, slot * 2048 + b2 * 128:
                                             slot * 2048 + b2 * 128 + 128],
                                        hb[:, (fc % 16) * 512:(fc % 16) * 512 + 512],
                                        start=(php == 0 and b2 == 0),
                                        stop=(php == 1 and b2 == 15))
                                return m
                            pmm = P.last()
                            ring_cons.append(pmm)
                            rglob += 1

                        @P.step("vector", deps=[pmm])
                        def CMBa(vector, e=e, pp=pp):
                            slot = scrD if e < 4 else scrE
                            return vector.tensor_tensor(
                                out=scrF[:, 0:512], in0=pp[:, :],
                                in1=slot[:, (e % 4) * 512:(e % 4) * 512 + 512],
                                op=ALU.mult)
                        cmba = P.last()

                        @P.step("vector", deps=[cmba])
                        def CMBb(vector, dc=dc, t0=t0):
                            return vector.tensor_tensor(
                                out=r_sb[:, dc * T + t0: dc * T + t0 + 512],
                                in0=r_sb[:, dc * T + t0: dc * T + t0 + 512],
                                in1=scrF[:, 0:512], op=ALU.add)
                        prev_cmbs[dc % 2] = P.last()
                        last = prev_cmbs[dc % 2]
                    prev_last_pmm = ring_cons[-1]
            return last

        if BUILD_UPTO >= 2:
            P.phase = "ffn1"
            ln2_ids = ln_stream(r_sb, [SUBL1], ln_apply(r_sb, xn_sb))
            SUBL2 = ffn_block(ln2_ids[-1], pre_dep=AGAO_ID)
        else:
            SUBL2 = SUBL1

        def sel_view(n):
            return cst[0:16, n * 128:(n + 1) * 128]

        def mamba_block(prev_done):
            OHA = soff["ohA"][0]
            OHB = soff["ohB"][0]
            OHPA = soff["ohPA"][0]
            OHPB = soff["ohPB"][0]
            CONE = soff["cone"][0]

            P.phase = "mm_prep"

            @P.step("sync", deps=[prev_done], n_inst=0, n_dma=2)
            def PARKR(sync):
                return [("dma", sync.dma_start(out=park_r.ap()[:, :], in_=r_sb[:])),
                        ("dma", sync.dma_start(out=cst[0:16, :], in_=sel16c[:]))]
            parkr = P.last()

            @P.step("vector", deps=[parkr])
            def ONESB(vector):
                return vector.memset(ones_bf[:], 1.0)
            onesb = P.last()

            ln3 = ln_stream(r_sb, [onesb], ln_apply(r_sb, r_sb))
            prev = ln3[-1]

            dtv_l = {l: wv(f"dtp{l}") for l in range(NL)}

            for l in range(NL):
                alc = soff[f"alog{l}"][0]
                dtbc = soff[f"dtb{l}"][0]
                dskc = soff[f"dsk{l}"][0]
                cbc = soff[f"cb{l}"][0]

                P.phase = "mm_ln"

                @P.step("scalar", deps=[prev])
                def NEGA1(scalar, alc=alc):
                    return scalar.activation(out=negA[:, 0:256],
                                             in_=smalls[:, alc:alc + 256],
                                             func=AF.Exp)
                nega1 = P.last()

                @P.step("vector", deps=[nega1])
                def NEGA2(vector):
                    return vector.tensor_scalar(out=negA[:, 0:256],
                                                in0=negA[:, 0:256],
                                                scalar1=-1.0, scalar2=None,
                                                op0=ALU.mult)
                nega = P.last()

                lnl = ln_stream(r_sb, [nega],
                                ln_apply(r_sb, xn_sb, gamma=f"nw{l}", beta=f"nb{l}"))
                LNL = lnl[-1]

                prev_q = LNL

                for q in range(4):
                    t0q = q * 512
                    ci = q // 2

                    P.phase = "mm_inproj"
                    imms = []
                    ievs = []
                    for oc in range(32):
                        eng = ("sync", "gpsimd", "sync")[oc % 3]
                        wdeps = [imms[oc - 2]] if oc >= 2 else [prev_q]

                        @P.step(eng, deps=wdeps, n_inst=0, n_dma=1)
                        def IWLD(eh, l=l, oc=oc):
                            return ("dma", eh.dma_start(
                                out=wstr[:, (oc % 2) * 1024:(oc % 2) * 1024 + 1024],
                                in_=wvt(f"in{l}", oc)))
                        iwld = P.last()

                        mdeps = [iwld] + ([ievs[oc - 2]] if oc >= 2 else [prev_q])

                        @P.step("tensor", deps=mdeps)
                        def IMM(tensor, oc=oc, t0q=t0q, q=q):
                            m = None
                            for kc in range(DC):
                                m = tensor.matmul(
                                    ps_big[:, (oc % 2) * 512:(oc % 2) * 512 + 512],
                                    wstr[:, (oc % 2) * 1024 + kc * 128:
                                         (oc % 2) * 1024 + kc * 128 + 128],
                                    xn_sb[:, kc * T + t0q: kc * T + t0q + 512],
                                    start=(kc == 0), stop=(kc == DC - 1))
                            if q == 0 and oc < 16:
                                for ci2, toff in enumerate((CH - 3, T - 3)):
                                    for kc in range(DC):
                                        m = tensor.matmul(
                                            ps_q[:, (oc % 2) * 8 + ci2 * 4:
                                                 (oc % 2) * 8 + ci2 * 4 + 3],
                                            wstr[:, (oc % 2) * 1024 + kc * 128:
                                                 (oc % 2) * 1024 + kc * 128 + 128],
                                            xn_sb[:, kc * T + toff:
                                                  kc * T + toff + 3],
                                            start=(kc == 0), stop=(kc == DC - 1))
                            return m
                        imm = P.last()
                        imms.append(imm)

                        nin = 3 if (q == 0 and oc < 16) else 1

                        @P.step("scalar", deps=[imm], n_inst=nin)
                        def IEV(scalar, oc=oc, q=q):
                            ms = []
                            if oc < 16:
                                ms.append(scalar.activation(
                                    out=bufA[:, oc * 512:(oc + 1) * 512],
                                    in_=ps_big[:, (oc % 2) * 512:
                                               (oc % 2) * 512 + 512],
                                    func=AF.Copy))
                                if q == 0:
                                    for ci2 in range(2):
                                        ms.append(scalar.activation(
                                            out=carry[:, 1024 + ci2 * 48 + oc * 3:
                                                      1024 + ci2 * 48 + oc * 3 + 3],
                                            in_=ps_q[:, (oc % 2) * 8 + ci2 * 4:
                                                     (oc % 2) * 8 + ci2 * 4 + 3],
                                            func=AF.Copy))
                            else:
                                ms.append(scalar.activation(
                                    out=bufC[:, (oc - 16) * 512:(oc - 15) * 512],
                                    in_=ps_big[:, (oc % 2) * 512:
                                               (oc % 2) * 512 + 512],
                                    func=AF.Silu))
                            return ms
                        ievs.append(P.last())
                    INPJ = ievs[-1]

                    hexid = None
                    if q == 0:
                        @P.step("sync", deps=[ievs[15]], n_inst=0, n_dma=1)
                        def AGAW(sync):
                            return ("dma", sync.dma_start(
                                out=aga_loc.ap().rearrange("a (p c) -> (a p) c",
                                                           c=96),
                                in_=carry[:, 1024:1120]))
                        agaw = P.last()

                        @P.step("gpsimd", deps=[agaw])
                        def AGA(gpsimd, l=l):
                            return gpsimd.collective_compute(
                                "AllGather", ALU.bypass,
                                replica_groups=[[0, 1, 2, 3], [4, 5, 6, 7]],
                                ins=[aga_loc.ap().opt()],
                                outs=[aga_all[l].ap().opt()])
                        aga = P.last()

                        @P.step("sync", deps=[aga], n_inst=0, n_dma=8)
                        def HALD(sync, l=l):
                            ms = []
                            for g in range(8):
                                rk = g if g < 4 else 7 - g
                                sl = 0 if g < 4 else 1
                                src2 = aga_all[l].ap()[rk:rk + 1, :].rearrange(
                                    "r (p c) -> (r p) c", c=96)[:,
                                                                sl * 48:sl * 48 + 48]
                                ms.append(("dma", sync.dma_start(
                                    out=scrA[:, g * 48:(g + 1) * 48], in_=src2)))
                            return ms
                        hald = P.last()

                        @P.step("vector", deps=[hald])
                        def HZERO(vector):
                            return vector.memset(carry[:, 1168:1264], 0.0)
                        hz = P.last()

                        @P.step("vector", deps=[hz], n_inst=32)
                        def HEX(vector):
                            ms = []
                            for g in range(8):
                                ms.append(vector.tensor_scalar(
                                    out=scrA[:, 384:432],
                                    in0=scrA[:, g * 48:(g + 1) * 48],
                                    scalar1=smalls[:, OHPA + g:OHPA + g + 1],
                                    scalar2=None, op0=ALU.mult))
                                ms.append(vector.tensor_scalar(
                                    out=scrA[:, 432:480],
                                    in0=scrA[:, g * 48:(g + 1) * 48],
                                    scalar1=smalls[:, OHPB + g:OHPB + g + 1],
                                    scalar2=None, op0=ALU.mult))
                                ms.append(vector.tensor_tensor(
                                    out=carry[:, 1168:1216],
                                    in0=carry[:, 1168:1216],
                                    in1=scrA[:, 384:432], op=ALU.add))
                                ms.append(vector.tensor_tensor(
                                    out=carry[:, 1216:1264],
                                    in0=carry[:, 1216:1264],
                                    in1=scrA[:, 432:480], op=ALU.add))
                            return ms
                        hexid = P.last()

                    @P.step("sync", deps=[INPJ], n_inst=0, n_dma=1)
                    def ZPARK(scalar, q=q):
                        return ("dma", scalar.dma_start(
                            out=park_zs.ap()[:, q * 8192:(q + 1) * 8192],
                            in_=bufC[:]))
                    zpark_id = P.last()

                    P.phase = "mm_conv"
                    csils = [None, None]
                    for dhi in range(DHI):
                        cdeps = [csils[dhi], ievs[dhi]]
                        if dhi == 0 and hexid is not None:
                            cdeps.append(hexid)

                        @P.step("vector", deps=cdeps, n_inst=9)
                        def CONVV(vector, l=l, dhi=dhi, q=q, ci=ci):
                            if q % 2 == 0:
                                lead = carry[:, 1168 + ci * 48 + dhi * 3:
                                             1168 + ci * 48 + dhi * 3 + 3]
                            else:
                                lead = carry[:, 1120 + dhi * 3:1120 + dhi * 3 + 3]
                            ms = [vector.tensor_copy(out=scrG[:, 0:3], in_=lead),
                                  vector.tensor_copy(
                                      out=scrG[:, 3:515],
                                      in_=bufA[:, dhi * 512:(dhi + 1) * 512])]
                            for k in range(4):
                                wc = soff[f"cw{l}_{k}"][0]
                                ms.append(vector.tensor_scalar(
                                    out=scrD[:, k * 512:(k + 1) * 512],
                                    in0=scrG[:, k:k + 512],
                                    scalar1=smalls[:, wc + dhi:wc + dhi + 1],
                                    scalar2=None, op0=ALU.mult))
                            ms.append(vector.tensor_tensor(
                                out=scrE[:, 0:512], in0=scrD[:, 0:512],
                                in1=scrD[:, 512:1024], op=ALU.add))
                            ms.append(vector.tensor_tensor(
                                out=scrE[:, 512:1024], in0=scrD[:, 1024:1536],
                                in1=scrD[:, 1536:2048], op=ALU.add))
                            ms.append(vector.tensor_tensor(
                                out=scrE[:, 1024 + (dhi % 2) * 512:
                                         1024 + (dhi % 2) * 512 + 512],
                                in0=scrE[:, 0:512],
                                in1=scrE[:, 512:1024], op=ALU.add))
                            return ms
                        convv = P.last()

                        @P.step("scalar", deps=[convv])
                        def CSIL(scalar, dhi=dhi, cbc=cbc):
                            return scalar.activation(
                                out=bufB[:, dhi * 512:(dhi + 1) * 512],
                                in_=scrE[:, 1024 + (dhi % 2) * 512:
                                         1024 + (dhi % 2) * 512 + 512],
                                func=AF.Silu,
                                bias=smalls[:, cbc + dhi:cbc + dhi + 1])
                        csils.append(P.last())
                        prev_c = csils[-1]
                    CONVD = prev_c

                    if q % 2 == 0:
                        @P.step("vector", deps=[CONVD])
                        def QTL(vector):
                            return vector.tensor_copy(
                                out=carry[:, 1120:1168].rearrange(
                                    "p (dhi t) -> p dhi t", t=3),
                                in_=bufA[:].rearrange("p (dhi t) -> p dhi t",
                                                      t=512)[:, :, 509:512])
                        CONVD = P.last()

                    P.phase = "mm_xdt"

                    @P.step("sync", deps=[CONVD, zpark_id], n_inst=0, n_dma=1)
                    def XWLD(sync, l=l):
                        return ("dma", sync.dma_start(
                            out=wstr[:, 0:1536], in_=wvt(f"xp{l}", 0)))
                    xwld = P.last()

                    @P.step("tensor", deps=[xwld], n_inst=3)
                    def XMM(tensor):
                        outs = []
                        for (o0, o1, pp) in ((0, 64, ps_o[0:64, :]),
                                             (64, 80, ps_p[0:16, :]),
                                             (80, 96, ps_q[0:16, :])):
                            m = None
                            for fc in range(16):
                                m = tensor.matmul(
                                    pp, wstr[:, fc * 96 + o0:fc * 96 + o1],
                                    bufB[:, fc * 512:(fc + 1) * 512],
                                    start=(fc == 0), stop=(fc == 15))
                            outs.append(m)
                        return outs
                    xmm = P.last()

                    @P.step("scalar", deps=[xmm], n_inst=3)
                    def XEV(scalar):
                        return [scalar.activation(out=scrF[0:64, 0:512],
                                                  in_=ps_o[0:64, :], func=AF.Copy),
                                scalar.activation(out=scrG[0:16, 0:512],
                                                  in_=ps_p[0:16, :], func=AF.Copy),
                                scalar.activation(out=scrG[0:16, 512:1024],
                                                  in_=ps_q[0:16, :], func=AF.Copy)]
                    xev = P.last()

                    @P.step("sync", deps=[xev], n_inst=0, n_dma=2)
                    def BCST(scalar, q=q):
                        return [("dma", scalar.dma_start(
                            out=bc_dram.ap()[:, q * 1024:q * 1024 + 512],
                            in_=scrG[0:16, 0:512])),
                            ("dma", scalar.dma_start(
                                out=bc_dram.ap()[:, q * 1024 + 512:q * 1024 + 1024],
                                in_=scrG[0:16, 512:1024]))]
                    bcst = P.last()

                    @P.step("sync", deps=[bcst], n_inst=0, n_dma=1)
                    def DWLD(sync, l=l):
                        return ("dma", sync.dma_start(
                            out=wstr[0:64, 0:2048], in_=dtv_l[l][:, :]))
                    prev_dt = P.last()
                    dvrels = []

                    for g4 in range(4):
                        @P.step("tensor", deps=[prev_dt])
                        def DMM(tensor, g4=g4):
                            m = None
                            for k in range(4):
                                dhi = g4 * 4 + k
                                m = tensor.matmul(
                                    ps_big[:, k * 512:(k + 1) * 512],
                                    wstr[0:64, dhi * 128:(dhi + 1) * 128],
                                    scrF[0:64, 0:512], start=True, stop=True)
                            return m
                        dmm = P.last()

                        @P.step("scalar", deps=[dmm], n_inst=4)
                        def DABS(scalar, g4=g4, dtbc=dtbc):
                            ms = []
                            for k in range(4):
                                dhi = g4 * 4 + k
                                ms.append(scalar.activation(
                                    out=scrA[:, k * 512:(k + 1) * 512],
                                    in_=ps_big[:, k * 512:(k + 1) * 512],
                                    func=AF.Abs,
                                    bias=smalls[:, dtbc + dhi:dtbc + dhi + 1]))
                            return ms
                        dabs = P.last()

                        @P.step("scalar", deps=[dabs], n_inst=4)
                        def DEXP(scalar):
                            return [scalar.activation(
                                out=scrA[:, k * 512:(k + 1) * 512],
                                in_=scrA[:, k * 512:(k + 1) * 512],
                                func=AF.Exp, scale=smalls[:, CM1:CM1 + 1])
                                for k in range(4)]
                        dexp = P.last()

                        @P.step("scalar", deps=[dexp], n_inst=4)
                        def DLN(scalar):
                            return [scalar.activation(
                                out=scrD[:, k * 512:(k + 1) * 512],
                                in_=scrA[:, k * 512:(k + 1) * 512],
                                func=AF.Ln, bias=smalls[:, CONE:CONE + 1])
                                for k in range(4)]
                        dln = P.last()

                        @P.step("vector", deps=[dln], n_inst=8)
                        def DVREL(vector, g4=g4, dtbc=dtbc):
                            ms = []
                            for k in range(4):
                                dhi = g4 * 4 + k
                                ms.append(vector.tensor_scalar(
                                    out=bufA[:, dhi * 512:(dhi + 1) * 512],
                                    in0=ps_big[:, k * 512:(k + 1) * 512],
                                    scalar1=smalls[:, dtbc + dhi:dtbc + dhi + 1],
                                    scalar2=0.0, op0=ALU.add, op1=ALU.max))
                            for k in range(4):
                                dhi = g4 * 4 + k
                                ms.append(vector.tensor_tensor(
                                    out=bufA[:, dhi * 512:(dhi + 1) * 512],
                                    in0=bufA[:, dhi * 512:(dhi + 1) * 512],
                                    in1=scrD[:, k * 512:(k + 1) * 512],
                                    op=ALU.add))
                            return ms
                        prev_dt = P.last()
                        dvrels.append(prev_dt)

                        @P.step("sync", deps=[prev_dt], n_inst=0, n_dma=1)
                        def DTPK(sync, q=q, g4=g4):
                            return ("dma", sync.dma_start(
                                out=park_dt.ap()[:, q * 8192 + g4 * 2048:
                                                 q * 8192 + (g4 + 1) * 2048],
                                in_=bufA[:, g4 * 2048:(g4 + 1) * 2048]))

                    prev_scan = dvrels[-1]

                    P.phase = "mm_scan"
                    for dhi in range(DHI):
                        sdeps = [prev_scan]
                        if dhi % 4 == 0:
                            sdeps.append(dvrels[dhi // 4])

                        @P.step("vector", deps=sdeps, n_inst=3)
                        def DTXC(vector, dhi=dhi, ci=ci, q=q):
                            init = (0.0 if q % 2 == 0 else
                                    cumST[:, ci * 16 + dhi:ci * 16 + dhi + 1])
                            return [
                                vector.tensor_tensor(
                                    out=scrF[:, 0:512],
                                    in0=bufA[:, dhi * 512:(dhi + 1) * 512],
                                    in1=bufB[:, dhi * 512:(dhi + 1) * 512],
                                    op=ALU.mult),
                                vector.tensor_tensor_scan(
                                    out=scrF[:, 512:1024], data0=ones_bf[:],
                                    data1=bufA[:, dhi * 512:(dhi + 1) * 512],
                                    initial=init, op0=ALU.mult, op1=ALU.add),
                                vector.tensor_copy(
                                    out=cumST[:, ci * 16 + dhi:ci * 16 + dhi + 1],
                                    in_=scrF[:, 1023:1024])]
                        dtxc = P.last()

                        vsteps = [dtxc, dtxc]
                        accs = [dtxc, dtxc]
                        for n in range(16):
                            half = n % 2
                            pb = half * 1024
                            fs = 1024 + half * 512

                            @P.step("tensor", deps=[vsteps[n], accs[n]])
                            def BCB(tensor, n=n, pb=pb):
                                tensor.matmul(ps_big[:, pb:pb + 512],
                                              sel_view(n), scrG[0:16, 0:512],
                                              start=True, stop=True)
                                return tensor.matmul(
                                    ps_big[:, pb + 512:pb + 1024],
                                    sel_view(n), scrG[0:16, 512:1024],
                                    start=True, stop=True)
                            bcb = P.last()

                            @P.step("scalar", deps=[vsteps[n]])
                            def AEXP(scalar, dhi=dhi, n=n, half=half):
                                nn = dhi * 16 + n
                                return scalar.activation(
                                    out=abuf[:, half * 512:half * 512 + 512],
                                    in_=bufA[:, dhi * 512:(dhi + 1) * 512],
                                    func=AF.Exp, scale=negA[:, nn:nn + 1])
                            aexp = P.last()

                            es = (n % 4) * 512
                            vdeps = [bcb, aexp]
                            if n >= 4:
                                vdeps.append(accs[n - 2])

                            @P.step("vector", deps=vdeps, n_inst=4)
                            def VSTEP(vector, dhi=dhi, ci=ci, n=n, q=q,
                                      half=half, pb=pb, fs=fs, es=es):
                                sc = ci * 256 + dhi * 16 + n
                                init = (0.0 if q % 2 == 0 else
                                        state[:, sc:sc + 1])
                                return [
                                    vector.tensor_tensor(
                                        out=scrF[:, fs:fs + 512],
                                        in0=scrF[:, 0:512],
                                        in1=ps_big[:, pb:pb + 512],
                                        op=ALU.mult),
                                    vector.tensor_tensor_scan(
                                        out=scrE[:, es:es + 512],
                                        data0=abuf[:, half * 512:
                                                   half * 512 + 512],
                                        data1=scrF[:, fs:fs + 512],
                                        initial=init, op0=ALU.mult,
                                        op1=ALU.add),
                                    vector.tensor_copy(
                                        out=state[:, sc:sc + 1],
                                        in_=scrE[:, es + 511:es + 512]),
                                    vector.tensor_tensor(
                                        out=scrE[:, es:es + 512],
                                        in0=scrE[:, es:es + 512],
                                        in1=ps_big[:, pb + 512:pb + 1024],
                                        op=ALU.mult)]
                            vstep = P.last()
                            vsteps.append(vstep)

                            @P.step("gpsimd", deps=[vstep])
                            def PACC(gpsimd, n=n, es=es):
                                slot = n % 4
                                dst = scrD[:, slot * 512:(slot + 1) * 512]
                                if n < 4:
                                    return gpsimd.tensor_copy(
                                        out=dst, in_=scrE[:, es:es + 512])
                                return gpsimd.tensor_tensor(
                                    out=dst, in0=dst, in1=scrE[:, es:es + 512],
                                    op=ALU.add)
                            accs.append(P.last())

                        @P.step("vector", deps=[accs[-1], accs[-2]], n_inst=6)
                        def YSG(vector, dhi=dhi, dskc=dskc):
                            return [
                                vector.tensor_tensor(
                                    out=scrF[:, 0:512], in0=scrD[:, 0:512],
                                    in1=scrD[:, 512:1024], op=ALU.add),
                                vector.tensor_tensor(
                                    out=scrF[:, 512:1024], in0=scrD[:, 1024:1536],
                                    in1=scrD[:, 1536:2048], op=ALU.add),
                                vector.tensor_tensor(
                                    out=scrF[:, 0:512], in0=scrF[:, 0:512],
                                    in1=scrF[:, 512:1024], op=ALU.add),
                                vector.tensor_scalar(
                                    out=scrF[:, 1536:2048],
                                    in0=bufB[:, dhi * 512:(dhi + 1) * 512],
                                    scalar1=smalls[:, dskc + dhi:dskc + dhi + 1],
                                    scalar2=None, op0=ALU.mult),
                                vector.tensor_tensor(
                                    out=scrF[:, 0:512], in0=scrF[:, 0:512],
                                    in1=scrF[:, 1536:2048], op=ALU.add),
                                vector.tensor_tensor(
                                    out=bufC[:, dhi * 512:(dhi + 1) * 512],
                                    in0=scrF[:, 0:512],
                                    in1=bufC[:, dhi * 512:(dhi + 1) * 512],
                                    op=ALU.mult)]
                        prev_scan = P.last()
                    SCAND = prev_scan

                    P.phase = "mm_oproj"
                    omms = []
                    oadds = []
                    for dc in range(DC):
                        wb = wstr[:, 0:2048] if dc % 2 == 0 else bufA[:, 0:2048]
                        pp = ps_o if dc % 2 == 0 else ps_p
                        eng = ("sync", "gpsimd")[dc % 2]
                        wdeps = [omms[dc - 2]] if dc >= 2 else [SCAND]

                        @P.step(eng, deps=wdeps, n_inst=0, n_dma=1)
                        def OWLD(eh, l=l, dc=dc, wb=wb):
                            return ("dma", eh.dma_start(out=wb,
                                                        in_=wvt(f"op{l}", dc)))
                        owld = P.last()

                        mdeps = [owld] + ([oadds[dc - 2]] if dc >= 2 else [SCAND])

                        @P.step("tensor", deps=mdeps)
                        def OMM(tensor, wb=wb, pp=pp):
                            m = None
                            for dhi in range(DHI):
                                m = tensor.matmul(
                                    pp[:, :], wb[:, dhi * 128:(dhi + 1) * 128],
                                    bufC[:, dhi * 512:(dhi + 1) * 512],
                                    start=(dhi == 0), stop=(dhi == DHI - 1))
                            return m
                        omms.append(P.last())

                        @P.step("vector", deps=[omms[-1]])
                        def OADD(vector, dc=dc, t0q=t0q, pp=pp):
                            return vector.tensor_tensor(
                                out=r_sb[:, dc * T + t0q: dc * T + t0q + 512],
                                in0=r_sb[:, dc * T + t0q: dc * T + t0q + 512],
                                in1=pp[:, :], op=ALU.add)
                        oadds.append(P.last())
                    prev_q = oadds[-1]

                    if q % 2 == 1:
                        @P.step("vector", deps=[prev_q], n_inst=17)
                        def SCPG(vector, ci=ci):
                            ms = [vector.tensor_copy(
                                out=carry[:, 512 + ci * 256:512 + (ci + 1) * 256],
                                in_=state[:, ci * 256:(ci + 1) * 256])]
                            for dhi in range(DHI):
                                ms.append(vector.tensor_scalar(
                                    out=scrA[:, dhi * 16:(dhi + 1) * 16],
                                    in0=negA[:, dhi * 16:(dhi + 1) * 16],
                                    scalar1=cumST[:, ci * 16 + dhi:
                                                  ci * 16 + dhi + 1],
                                    scalar2=None, op0=ALU.mult))
                            return ms
                        scpg = P.last()

                        @P.step("scalar", deps=[scpg], n_inst=16)
                        def PEXP(scalar, ci=ci):
                            return [scalar.activation(
                                out=carry[:, ci * 256 + dhi * 16:
                                          ci * 256 + dhi * 16 + 16],
                                in_=scrA[:, dhi * 16:(dhi + 1) * 16],
                                func=AF.Exp) for dhi in range(DHI)]
                        prev_q = P.last()

                P.phase = "mm_comb"

                @P.step("sync", deps=[prev_q], n_inst=0, n_dma=1)
                def AGBW(sync):
                    return ("dma", sync.dma_start(
                        out=agb_loc.ap().rearrange("a (p c) -> (a p) c", c=1024),
                        in_=carry[:, 0:1024]))
                agbw = P.last()

                @P.step("gpsimd", deps=[agbw])
                def AGB(gpsimd, l=l):
                    return gpsimd.collective_compute(
                        "AllGather", ALU.bypass,
                        replica_groups=[[0, 1, 2, 3], [4, 5, 6, 7]],
                        ins=[agb_loc.ap().opt()], outs=[agb_all[l].ap().opt()])
                agb = P.last()

                @P.step("vector", deps=[agb], n_inst=2)
                def CMB0(vector):
                    return [vector.memset(scrB[:, 0:256], 0.0),
                            vector.memset(hin[:, 0:512], 0.0)]
                cprev = P.last()

                for k in range(1, 8):
                    g = k - 1
                    rk = g if g < 4 else 7 - g
                    sl = 0 if g < 4 else 1

                    @P.step("sync", deps=[cprev], n_inst=0, n_dma=2)
                    def CLDPS(sync, l=l, rk=rk, sl=sl):
                        v = agb_all[l].ap()[rk:rk + 1, :].rearrange(
                            "r (p c) -> (r p) c", c=1024)
                        return [("dma", sync.dma_start(
                            out=scrA[:, 0:256],
                            in_=v[:, sl * 256:(sl + 1) * 256])),
                            ("dma", sync.dma_start(
                                out=scrA[:, 256:512],
                                in_=v[:, 512 + sl * 256:512 + (sl + 1) * 256]))]
                    cld = P.last()

                    @P.step("vector", deps=[cld], n_inst=6)
                    def CMB(vector, k=k):
                        return [
                            vector.tensor_tensor(out=scrB[:, 0:256],
                                                 in0=scrB[:, 0:256],
                                                 in1=scrA[:, 0:256], op=ALU.mult),
                            vector.tensor_tensor(out=scrB[:, 0:256],
                                                 in0=scrB[:, 0:256],
                                                 in1=scrA[:, 256:512], op=ALU.add),
                            vector.tensor_scalar(
                                out=scrA[:, 512:768], in0=scrB[:, 0:256],
                                scalar1=smalls[:, OHA + k:OHA + k + 1],
                                scalar2=None, op0=ALU.mult),
                            vector.tensor_scalar(
                                out=scrA[:, 768:1024], in0=scrB[:, 0:256],
                                scalar1=smalls[:, OHB + k:OHB + k + 1],
                                scalar2=None, op0=ALU.mult),
                            vector.tensor_tensor(out=hin[:, 0:256],
                                                 in0=hin[:, 0:256],
                                                 in1=scrA[:, 512:768], op=ALU.add),
                            vector.tensor_tensor(out=hin[:, 256:512],
                                                 in0=hin[:, 256:512],
                                                 in1=scrA[:, 768:1024],
                                                 op=ALU.add)]
                    cprev = P.last()

                cqprev = cprev

                P.phase = "mm_corr"
                for q in range(4):
                    t0q = q * 512
                    ci = q // 2

                    @P.step("sync", deps=[cqprev], n_inst=0, n_dma=4)
                    def CRLD(sync, q=q):
                        return [
                            ("dma", sync.dma_start(
                                out=bufA[:],
                                in_=park_dt.ap()[:, q * 8192:(q + 1) * 8192])),
                            ("dma", sync.dma_start(
                                out=bufC[:],
                                in_=park_zs.ap()[:, q * 8192:(q + 1) * 8192])),
                            ("dma", sync.dma_start(
                                out=scrG[0:16, 0:512],
                                in_=bc_dram.ap()[:, q * 1024:q * 1024 + 512])),
                            ("dma", sync.dma_start(
                                out=scrG[0:16, 512:1024],
                                in_=bc_dram.ap()[:, q * 1024 + 512:
                                                 q * 1024 + 1024]))]
                    prev_cd = P.last()

                    for dhi in range(DHI):
                        @P.step("vector", deps=[prev_cd], n_inst=2)
                        def CCUM(vector, dhi=dhi, ci=ci, q=q):
                            init = (0.0 if q % 2 == 0 else
                                    cumST[:, ci * 16 + dhi:ci * 16 + dhi + 1])
                            return [
                                vector.tensor_tensor_scan(
                                    out=scrF[:, 512:1024], data0=ones_bf[:],
                                    data1=bufA[:, dhi * 512:(dhi + 1) * 512],
                                    initial=init, op0=ALU.mult, op1=ALU.add),
                                vector.tensor_copy(
                                    out=cumST[:, ci * 16 + dhi:ci * 16 + dhi + 1],
                                    in_=scrF[:, 1023:1024])]
                        ccum = P.last()

                        cvs = [ccum, ccum]
                        cbhs = [ccum, ccum]
                        for n in range(16):
                            half = n % 2
                            pb = half * 1024
                            fs = 1024 + half * 512

                            @P.step("tensor", deps=[cvs[n], cbhs[n]])
                            def CCB(tensor, n=n, pb=pb):
                                return tensor.matmul(
                                    ps_big[:, pb:pb + 512], sel_view(n),
                                    scrG[0:16, 512:1024], start=True, stop=True)
                            ccb = P.last()

                            @P.step("scalar", deps=[cvs[n]])
                            def CEX(scalar, dhi=dhi, n=n, half=half):
                                nn = dhi * 16 + n
                                return scalar.activation(
                                    out=scrE[:, 1024 + half * 512:
                                             1024 + half * 512 + 512],
                                    in_=scrF[:, 512:1024], func=AF.Exp,
                                    scale=negA[:, nn:nn + 1])
                            cex = P.last()

                            @P.step("scalar", deps=[ccb, cvs[n]])
                            def CBH(scalar, n=n, ci=ci, dhi=dhi, half=half,
                                    pb=pb):
                                hc = ci * 256 + dhi * 16 + n
                                return scalar.activation(
                                    out=scrE[:, half * 512:half * 512 + 512],
                                    in_=ps_big[:, pb:pb + 512],
                                    func=AF.Identity,
                                    scale=hin[:, hc:hc + 1])
                            cbhs.append(P.last())

                            @P.step("vector", deps=[cex, cbhs[-1]], n_inst=2)
                            def CV(vector, n=n, half=half, fs=fs):
                                m1 = vector.tensor_tensor(
                                    out=scrF[:, fs:fs + 512],
                                    in0=scrE[:, fs:fs + 512],
                                    in1=scrE[:, half * 512:half * 512 + 512],
                                    op=ALU.mult)
                                if n == 0:
                                    m2 = vector.tensor_copy(
                                        out=scrD[:, 0:512],
                                        in_=scrF[:, fs:fs + 512])
                                else:
                                    m2 = vector.tensor_tensor(
                                        out=scrD[:, 0:512], in0=scrD[:, 0:512],
                                        in1=scrF[:, fs:fs + 512], op=ALU.add)
                                return [m1, m2]
                            cvs.append(P.last())
                        cps = [cvs[-1]]

                        @P.step("vector", deps=[cps[-1]])
                        def CGATE(vector, dhi=dhi):
                            return vector.tensor_tensor(
                                out=bufC[:, dhi * 512:(dhi + 1) * 512],
                                in0=scrD[:, 0:512],
                                in1=bufC[:, dhi * 512:(dhi + 1) * 512],
                                op=ALU.mult)
                        prev_cd = P.last()

                    P.phase = "mm_coproj"
                    omms2 = []
                    oadds2 = []
                    for dc in range(DC):
                        wb = wstr[:, 0:2048] if dc % 2 == 0 else bufB[:, 0:2048]
                        pp = ps_o if dc % 2 == 0 else ps_p
                        eng = ("sync", "gpsimd")[dc % 2]
                        wdeps = [omms2[dc - 2]] if dc >= 2 else [prev_cd]

                        @P.step(eng, deps=wdeps, n_inst=0, n_dma=1)
                        def OWLD2(eh, l=l, dc=dc, wb=wb):
                            return ("dma", eh.dma_start(out=wb,
                                                        in_=wvt(f"op{l}", dc)))
                        owld2 = P.last()

                        mdeps = [owld2] + ([oadds2[dc - 2]] if dc >= 2
                                           else [prev_cd])

                        @P.step("tensor", deps=mdeps)
                        def OMM2(tensor, wb=wb, pp=pp):
                            m = None
                            for dhi in range(DHI):
                                m = tensor.matmul(
                                    pp[:, :], wb[:, dhi * 128:(dhi + 1) * 128],
                                    bufC[:, dhi * 512:(dhi + 1) * 512],
                                    start=(dhi == 0), stop=(dhi == DHI - 1))
                            return m
                        omms2.append(P.last())

                        @P.step("vector", deps=[omms2[-1]])
                        def OADD2(vector, dc=dc, t0q=t0q, pp=pp):
                            return vector.tensor_tensor(
                                out=r_sb[:, dc * T + t0q: dc * T + t0q + 512],
                                in0=r_sb[:, dc * T + t0q: dc * T + t0q + 512],
                                in1=pp[:, :], op=ALU.add)
                        oadds2.append(P.last())
                    prev_omm2 = oadds2[-1]
                    cqprev = prev_omm2
                prev = cqprev

            P.phase = "mm_fin"
            lnf = ln_stream(r_sb, [prev],
                            ln_apply(r_sb, r_sb, gamma="nfw", beta="nfb"))
            radd = lnf[-1]
            for tt in range(4):
                for grp in range(2):
                    @P.step("sync", deps=[radd], n_inst=0, n_dma=4)
                    def RLD(sync, tt=tt, grp=grp):
                        ms = []
                        for k in range(4):
                            dc = grp * 4 + k
                            ms.append(("dma", sync.dma_start(
                                out=scrA[:, k * 512:(k + 1) * 512],
                                in_=park_r.ap()[:, dc * T + tt * 512:
                                                dc * T + tt * 512 + 512])))
                        return ms
                    rld = P.last()

                    @P.step("vector", deps=[rld], n_inst=4)
                    def RADDF(vector, tt=tt, grp=grp):
                        ms = []
                        for k in range(4):
                            dc = grp * 4 + k
                            ms.append(vector.tensor_tensor(
                                out=r_sb[:, dc * T + tt * 512:
                                         dc * T + tt * 512 + 512],
                                in0=r_sb[:, dc * T + tt * 512:
                                         dc * T + tt * 512 + 512],
                                in1=scrA[:, k * 512:(k + 1) * 512], op=ALU.add))
                        return ms
                    radd = P.last()
            return radd

        if BUILD_UPTO >= 3:
            SUBL3 = mamba_block(SUBL2)
        else:
            SUBL3 = SUBL2

        if BUILD_UPTO >= 4:
            P.phase = "ffn2"
            ln4_ids = ln_stream(r_sb, [SUBL3], ln_apply(r_sb, xn_sb))
            SUBL4 = ffn_block(ln4_ids[-1], pre_dep=SUBL3)
        else:
            SUBL4 = SUBL3

        FINAL = SUBL4

        P.phase = "out"

        @P.step("sync", deps=[FINAL], n_inst=0, n_dma=1)
        def OUT(sync):
            return ("dma", sync.dma_start(
                out=out_ext[:].rearrange("(dc p) t -> p dc t", p=128),
                in_=r_sb[:].rearrange("p (dc t) -> p dc t", t=T)))
        outid = P.last()

        @P.step("sync", deps=[outid])
        def FIN(sync):
            return sync.nop()

        with nc.Block() as block:
            P.emit(block, sems, nc)

    return nc



def kernel(**inputs):
    flat, woff = _weight_layout(inputs)
    smalls, soff = _smalls_layout(inputs)
    S = flat.size
    meta = {"S": S, "NSM": smalls.shape[1], "woff": woff, "soff": soff}

    x = inputs["L0_attn_a_x"]
    wslices = flat.reshape(8, S // 8)

    in_maps = []
    for c in range(8):
        b = c // 4
        jj = c % 4
        cA, cB = jj, 7 - jj
        xloc = np.concatenate([x[b, cA * CH:(cA + 1) * CH],
                               x[b, cB * CH:(cB + 1) * CH]], axis=0)
        in_maps.append({
            "xin": np.ascontiguousarray(xloc.T),
            "wsl": wslices[c:c + 1],
            "smalls": _core_smalls(smalls, soff, jj),
            "rope": _rope_dup((cA, cB)),
            "dyn": np.array([[jj, 0, 0, 0, 0, 0, 0, 0]], np.int32),
        })

    nc = build_program(meta)
    res = run_bass_kernel_spmd(nc, in_maps, core_ids=list(range(8)))

    out = np.zeros((B, TG, D), np.float32)
    for c in range(8):
        b = c // 4
        jj = c % 4
        cA, cB = jj, 7 - jj
        o = np.asarray(res.results[c]["out"])
        o = o.reshape(DC, 128, T).transpose(2, 0, 1).reshape(T, D)
        out[b, cA * CH:(cA + 1) * CH] = o[0:CH]
        out[b, cB * CH:(cB + 1) * CH] = o[CH:]
    return out


# version 11 (final)
# speedup vs baseline: 1.1434x; 1.0033x over previous
import numpy as np
import ml_dtypes
import concourse.bass as bass
import concourse.mybir as mybir
from concourse.bass_utils import run_bass_kernel_spmd

F32 = mybir.dt.float32
BF16 = mybir.dt.bfloat16
I32 = mybir.dt.int32
AF = mybir.ActivationFunctionType
ALU = mybir.AluOpType
AX = mybir.AxisListType.X

B = 2
TG = 8192
T = 2048
CH = 1024
D = 1024
DC = 8
H = 16
DH = 64
DI = 2048
DHI = 16
DS = 16
DTR = 64
NL = 4
E = 8
DF = 4096
EPS = 1e-5
QT = 512
bf = ml_dtypes.bfloat16

ACC_ENG = "gpsimd"
BUILD_UPTO = 4
DETECT_RACES = True



def _weight_layout(inp):
    parts, off, pos = [], {}, 0

    def add(name, arr):
        nonlocal pos
        a = np.ascontiguousarray(arr, dtype=np.float32)
        off[name] = (pos, a.shape)
        parts.append(a.ravel())
        pos += a.size

    def addt(name, arr, mw=128):
        nonlocal pos
        a = np.ascontiguousarray(arr, dtype=np.float32)
        K, M = a.shape
        nm = M // mw
        t = a.reshape(K // 128, 128, nm, mw).transpose(2, 1, 0, 3)
        off[name] = (pos, (K, M, mw, "t"))
        parts.append(np.ascontiguousarray(t).ravel())
        pos += a.size

    addt("W1T", inp["L0_attn_a_W1"].T)
    addt("W2T", inp["L0_attn_a_W2"].T)
    addt("gate", inp["L0_ffn_gate_w"], mw=8)
    for e in range(E):
        addt(f"fc{e}", inp["L0_ffn_fc_w"][e])
    for e in range(E):
        addt(f"pj{e}", inp["L0_ffn_proj_w"][e])
    for l in range(NL):
        addt(f"in{l}", inp["L0_attn_b_in_proj_w"][l].T)
    for l in range(NL):
        addt(f"xp{l}", inp["L0_attn_b_x_proj_w"][l].T, mw=96)
    for l in range(NL):
        add(f"dtp{l}", inp["L0_attn_b_dt_proj_w"][l].T)
    for l in range(NL):
        addt(f"op{l}", inp["L0_attn_b_out_proj_w"][l].T)
    flat = np.concatenate(parts)
    pad = (-flat.size) % 8192
    if pad:
        flat = np.concatenate([flat, np.zeros(pad, np.float32)])
    return flat, off


def _smalls_layout(inp):
    cols, off = [], {}

    def addc(name, arr2d):
        off[name] = (sum(c.shape[1] for c in cols), arr2d.shape[1])
        cols.append(np.ascontiguousarray(arr2d, np.float32))

    for l in range(NL):
        addc(f"nw{l}", inp["L0_attn_b_norm_w"][l].reshape(DC, 128).T)
        addc(f"nb{l}", inp["L0_attn_b_norm_b"][l].reshape(DC, 128).T)
    addc("nfw", inp["L0_attn_b_normf_w"].reshape(DC, 128).T)
    addc("nfb", inp["L0_attn_b_normf_b"].reshape(DC, 128).T)
    for l in range(NL):
        for k in range(4):
            addc(f"cw{l}_{k}", inp["L0_attn_b_conv_w"][l, :, k].reshape(DHI, 128).T)
        addc(f"cb{l}", inp["L0_attn_b_conv_b"][l].reshape(DHI, 128).T)
        addc(f"dtb{l}", inp["L0_attn_b_dt_proj_b"][l].reshape(DHI, 128).T)
        addc(f"dsk{l}", inp["L0_attn_b_D_skip"][l].reshape(DHI, 128).T)
        al = inp["L0_attn_b_A_log"][l].reshape(DHI, 128, DS)
        addc(f"alog{l}", np.transpose(al, (1, 0, 2)).reshape(128, DHI * DS))
    addc("ceps", np.full((128, 1), EPS, np.float32))
    addc("cm05", np.full((128, 1), -0.5, np.float32))
    addc("c0125", np.full((128, 1), 0.125, np.float32))
    addc("cm1", np.full((128, 1), -1.0, np.float32))
    addc("cone", np.full((128, 1), 1.0, np.float32))
    for nm in ("ohA", "ohB", "ohPA", "ohPB"):
        addc(nm, np.zeros((128, 8), np.float32))
    packed = np.concatenate(cols, axis=1)
    return np.ascontiguousarray(packed), off


def _core_smalls(smalls, soff, jj):
    s = smalls.copy()

    def setcol(name, k):
        if 0 <= k < 8:
            s[:, soff[name][0] + k] = 1.0
    setcol("ohA", jj)
    setcol("ohB", 7 - jj)
    setcol("ohPA", jj - 1)
    setcol("ohPB", 6 - jj)
    return s


def _rope_dup(chunks):
    hs = DH
    inv = 1.0 / (10000.0 ** (np.arange(0, hs, 2, dtype=np.float64) / hs))
    pos = np.concatenate([np.arange(c * CH, (c + 1) * CH) for c in chunks])
    ang = pos[:, None].astype(np.float64) * inv[None, :]
    cos = np.concatenate([np.cos(ang), np.cos(ang)], axis=1).T
    sin = np.concatenate([np.sin(ang), np.sin(ang)], axis=1).T
    out = np.zeros((128, 2 * T), np.float32)
    out[0:64, 0:T] = cos
    out[64:128, 0:T] = cos
    out[0:64, T:] = sin
    out[64:128, T:] = sin
    return out


def _tri_masks():
    m = np.zeros((128, 4 * QT), np.float32)
    kk = np.arange(128)[:, None]
    qq = np.arange(QT)[None, :]
    for i in range(4):
        m[:, i * QT:(i + 1) * QT] = (i * 128 + kk <= qq)
    return m.astype(bf)


def _rot_perm():
    p = np.zeros((64, 64), np.float32)
    for jj in range(32):
        p[jj + 32, jj] = -1.0
        p[jj, jj + 32] = 1.0
    return np.vstack([p, p]).astype(bf)


def _sel16():
    s = np.zeros((16, 16 * 128), np.float32)
    for n in range(16):
        s[n, n * 128:(n + 1) * 128] = 1.0
    return s.astype(bf)



class Prog:
    ENGS = ("sync", "vector", "scalar", "tensor", "gpsimd")

    def __init__(self):
        self.steps = []
        self.done_at = []
        self.counts = {e: [0, 0] for e in self.ENGS}
        self.phase = "init"
        self.phases = []

    def step(self, eng, deps=None, n_inst=1, n_dma=0):
        sid = len(self.steps)
        if deps is None:
            deps = [sid - 1] if sid > 0 else []
        deps = [d for d in deps if d is not None and d >= 0]

        def reg(fn):
            self.steps.append((eng, fn, list(deps), n_inst, n_dma))
            self.phases.append(self.phase)
            c = self.counts[eng]
            c[0] += n_inst
            c[1] += n_dma
            da = {}
            if n_inst:
                da[eng + "_i"] = c[0]
            if n_dma:
                da[eng + "_d"] = 16 * c[1]
            if not da:
                da[eng + "_i"] = c[0]
            self.done_at.append(da)
            return len(self.steps) - 1
        return reg

    def last(self):
        return len(self.steps) - 1

    def emit(self, block, sems, nc=None):
        per_eng = {e: [] for e in self.ENGS}
        for sid, st in enumerate(self.steps):
            per_eng[st[0]].append((sid,) + st[1:])
        from contextlib import nullcontext

        def make(eng):
            items = per_eng[eng]

            def body(handle):
                for sid, fn, deps, n_inst, n_dma in items:
                    need = {}
                    for d in deps:
                        for sk, dv in self.done_at[d].items():
                            need[sk] = max(need.get(sk, 0), dv)
                    for sk in sorted(need):
                        handle.wait_ge(sems[sk], need[sk])
                    with (nc.named_scope(self.phases[sid]) if nc is not None
                          else nullcontext()):
                        marks = fn(handle)
                    if marks is None:
                        marks = []
                    elif isinstance(marks, tuple) and len(marks) == 2 and marks[0] == "dma":
                        marks = [marks]
                    elif not isinstance(marks, (list, tuple)):
                        marks = [marks]
                    gi = gd = 0
                    for m in marks:
                        if isinstance(m, tuple) and m[0] == "dma":
                            m[1].then_inc(sems[eng + "_d"], 16)
                            gd += 1
                        else:
                            m.then_inc(sems[eng + "_i"], 1)
                            gi += 1
                    assert gi == n_inst and gd == n_dma, \
                        f"step {sid} {eng}: declared ({n_inst},{n_dma}) got ({gi},{gd})"
            return body

        for eng in self.ENGS:
            getattr(block, eng)(make(eng))



def build_program(meta):
    nc = bass.Bass("TRN2", target_bir_lowering=False, debug=False,
                   detect_race_conditions=DETECT_RACES)
    S = meta["S"]; S8 = S // 8
    NSM = meta["NSM"]
    woff = meta["woff"]; soff = meta["soff"]

    PC = DH * T
    KVN = H * 3 * PC
    AOB = 4 * DH * T
    AON = 4 * AOB

    xin = nc.dram_tensor("xin", [D, T], F32, kind="ExternalInput").ap()
    wsl = nc.dram_tensor("wsl", [1, S8], F32, kind="ExternalInput").ap()
    smalls_in = nc.dram_tensor("smalls", [128, NSM], F32, kind="ExternalInput").ap()
    rope_in = nc.dram_tensor("rope", [128, 2 * T], F32, kind="ExternalInput").ap()
    dyn_in = nc.dram_tensor("dyn", [1, 8], I32, kind="ExternalInput").ap()
    out_ext = nc.dram_tensor("out", [DC * 128, T], F32, kind="ExternalOutput").ap()

    wbf_sl = nc.dram_tensor("wbf_sl", [1, S8], BF16)
    wbf = nc.dram_tensor("wbf", [8, S8], BF16)
    kv_loc = nc.dram_tensor("kv_loc", [1, KVN], BF16)
    kv_all_t = nc.dram_tensor("kv_all", [4, KVN], BF16)
    ao_loc = nc.dram_tensor("ao_loc", [1, AON], BF16)
    ao_all_t = nc.dram_tensor("ao_all", [4, AON], BF16)
    kvstage = nc.dram_tensor("kvstage", [4, 12 * PC], BF16)
    w8_dram = nc.dram_tensor("w8_dram", [8, T], BF16)
    aostage = nc.dram_tensor("aostage", [4, 4 * PC], BF16)
    park_r = nc.dram_tensor("park_r", [128, DC * T], F32)
    park_zs = nc.dram_tensor("park_zs", [128, DHI * T], BF16)
    aga_loc = nc.dram_tensor("aga_loc", [1, 2 * 48 * 128], F32)
    aga_all = [nc.dram_tensor(f"aga_all{l}", [4, 2 * 48 * 128], F32)
               for l in range(NL)]
    agb_loc = nc.dram_tensor("agb_loc", [1, 2 * 2 * 256 * 128], F32)
    agb_all = [nc.dram_tensor(f"agb_all{l}", [4, 2 * 2 * 256 * 128], F32)
               for l in range(NL)]
    bc_dram = nc.dram_tensor("bc_dram", [16, 4 * 1024], BF16)
    park_dt = nc.dram_tensor("park_dt", [128, DHI * T], BF16)

    wflat = wbf.ap().rearrange("r s -> (r s)")

    def wv(name):
        pos, shape = woff[name]
        K, M = shape
        return wflat[pos:pos + K * M].rearrange("(k m) -> k m", m=M)

    def wvt(name, mb):
        pos, (K, M, mw, _) = woff[name]
        blk = K * mw
        return wflat[pos + mb * blk: pos + (mb + 1) * blk].rearrange(
            "(p x) -> p x", x=(K // 128) * mw)

    def wvt2(name, mb2):
        pos, (K, M, mw, _) = woff[name]
        blk = K * mw
        return wflat[pos + 2 * mb2 * blk: pos + (2 * mb2 + 2) * blk].rearrange(
            "(b p x) -> p b x", p=128, x=(K // 128) * mw)

    CEPS = soff["ceps"][0]
    CM05 = soff["cm05"][0]
    C0125 = soff["c0125"][0]
    CM1 = soff["cm1"][0]

    ident = nc.inline_tensor(np.eye(128, dtype=bf), name="ident128").ap()
    ident32 = nc.inline_tensor(np.eye(128, dtype=np.float32), name="ident128f").ap()
    rotp = nc.inline_tensor(_rot_perm(), name="rotperm").ap()
    trimask = nc.inline_tensor(_tri_masks(), name="trimask").ap()
    sel16c = nc.inline_tensor(_sel16(), name="sel16").ap()
    ones1x128 = nc.inline_tensor(np.ones((1, 128), np.float32), name="o1x128c").ap()
    ones128c = nc.inline_tensor(np.ones((128, 1), np.float32), name="o128cc").ap()

    from contextlib import ExitStack
    stack = ExitStack()
    sbuf = lambda n, s, d=F32: stack.enter_context(nc.sbuf_tensor(n, s, d))
    psum = lambda n, s, d=F32: stack.enter_context(nc.psum_tensor(n, s, d))
    semf = lambda n: stack.enter_context(nc.semaphore(n))

    with stack:
        r_sb = sbuf("r_sb", [128, DC * T])
        xn_sb = sbuf("xn_sb", [128, DC * T], BF16)
        bufA = sbuf("bufA", [128, 8192], BF16)
        bufB = sbuf("bufB", [128, 8192], BF16)
        bufC = sbuf("bufC", [128, 8192], BF16)
        wstr = sbuf("wstr", [128, 2048], BF16)
        abuf = sbuf("abuf", [128, 1024])
        scrA = sbuf("scrA", [128, 2048])
        scrB = sbuf("scrB", [128, 512])
        scrD = sbuf("scrD", [128, 2048], BF16)
        scrE = sbuf("scrE", [128, 2048], BF16)
        scrF = sbuf("scrF", [128, 2048], BF16)
        cst = sbuf("cst", [128, 2048], BF16)
        smalls = sbuf("smalls_sb", [128, NSM])
        rows_a = sbuf("rows_a", [1, 512])
        rows_b = sbuf("rows_b", [1, 512])
        rows_c = sbuf("rows_c", [1, 512])
        id_sb = sbuf("id_sb", [128, 128], BF16)
        id32_sb = sbuf("id32_sb", [128, 128])
        gsb = sbuf("gsb", [128, 64], BF16)
        rot_sb = sbuf("rot_sb", [128, 64], BF16)
        o1x = sbuf("o1x", [1, 128])
        o128 = sbuf("o128", [128, 1])
        mxc = sbuf("mxc", [128, 8])
        mxv = sbuf("mxv", [1, 16])
        dyn_sb = sbuf("dyn_sb", [1, 8], I32)
        carry = sbuf("carry_sb", [128, 1296])
        state = sbuf("state_sb", [128, 2 * 256])
        hin = state
        scrG = sbuf("scrG", [128, 1024], BF16)
        ones_bf = sbuf("ones_bf", [128, 512], BF16)
        negA = scrB[:, 256:512]
        cumST = carry[:, 1264:1296]
        kfull = bufA[0:64, :]
        qfull = bufC[0:64, :]
        vstb = bufB[:, 0:64 * 65]
        ropeC = bufC
        tri_sb = cst

        ps_big = psum("ps_big", [128, 2048])
        ps_o = psum("ps_o", [128, 512])
        ps_p = psum("ps_p", [128, 512])
        ps_q = psum("ps_q", [128, 512])
        ps_t = psum("ps_t", [128, 1024], BF16)

        sems = {}
        for e in Prog.ENGS:
            sems[e + "_i"] = semf("sem_" + e + "_i")
            sems[e + "_d"] = semf("sem_" + e + "_d")
        P = Prog()
        regs = {}
        P.phase = "load"

        @P.step("sync", deps=[], n_inst=0, n_dma=9)
        def LOADS(sync):
            return [
                ("dma", sync.dma_start(out=smalls[:], in_=smalls_in[:])),
                ("dma", sync.dma_start(out=id_sb[:], in_=ident[:])),
                ("dma", sync.dma_start(out=id32_sb[:], in_=ident32[:])),
                ("dma", sync.dma_start(out=rot_sb[:], in_=rotp[:])),
                ("dma", sync.dma_start(out=cst[:], in_=trimask[:])),
                                ("dma", sync.dma_start(out=o1x[:], in_=ones1x128[:])),
                ("dma", sync.dma_start(out=o128[:], in_=ones128c[:])),
                ("dma", sync.dma_start(out=dyn_sb[:], in_=dyn_in[:])),
                ("dma", sync.dma_start(
                    out=r_sb[:].rearrange("p (dc t) -> p dc t", t=T),
                    in_=xin.rearrange("(dc p) t -> p dc t", p=128))),
            ]
        LOADS_ID = P.last()

        prev_rl = LOADS_ID
        for rr in range(4):
            @P.step("sync", deps=[prev_rl], n_inst=0, n_dma=1)
            def ROPELD(sync, rr=rr):
                return ("dma", sync.dma_start(
                    out=abuf[:, 0:1024], in_=rope_in[:, rr * 1024:(rr + 1) * 1024]))
            rl = P.last()

            @P.step("vector", deps=[rl])
            def ROPECAST(vector, rr=rr):
                return vector.tensor_copy(out=bufC[:, rr * 1024:(rr + 1) * 1024],
                                          in_=abuf[:, 0:1024])
            prev_rl = P.last()
        ROPE_ID = P.last()

        P.phase = "castag"
        CPC = S8 // 128
        ncast = (CPC + 1023) // 1024
        wsl2 = wsl.rearrange("a (p c) -> a p c", p=128)
        wbf2 = wbf_sl.ap().rearrange("a (p c) -> a p c", p=128)
        prev_store = ROPE_ID
        for i in range(ncast):
            c0 = i * 1024
            c1 = min(CPC, c0 + 1024)

            @P.step("sync", deps=[prev_store], n_inst=0, n_dma=1)
            def CLD(sync, c0=c0, c1=c1):
                return ("dma", sync.dma_start(out=abuf[:, 0:c1 - c0],
                                              in_=wsl2[0, :, c0:c1]))
            ld = P.last()

            @P.step("vector", deps=[ld, prev_store])
            def CCAST(vector, c0=c0, c1=c1):
                return vector.tensor_copy(out=wstr[:, 0:c1 - c0],
                                          in_=abuf[:, 0:c1 - c0])
            cast_id = P.last()

            @P.step("scalar", deps=[cast_id], n_inst=0, n_dma=1)
            def CST(scalar, c0=c0, c1=c1):
                return ("dma", scalar.dma_start(out=wbf2[0, :, c0:c1],
                                                in_=wstr[:, 0:c1 - c0]))
            prev_store = P.last()

        @P.step("gpsimd", deps=[prev_store])
        def AGW(gpsimd):
            return gpsimd.collective_compute(
                "AllGather", ALU.bypass,
                replica_groups=[[0, 1, 2, 3, 4, 5, 6, 7]],
                ins=[wbf_sl.ap().opt()], outs=[wbf.ap().opt()])
        AGW_ID = P.last()

        def ln_stream(src, first_deps, apply_cb):
            out_ids = []
            prev = None
            for tt in range(4):
                t0 = tt * 512
                mmlast = None
                for rnd in range(2):
                    deps = list(first_deps) if (tt == 0 and rnd == 0) else None

                    @P.step("scalar", deps=deps, n_inst=4)
                    def SQ(scalar, t0=t0, rnd=rnd):
                        ms = []
                        for k in range(4):
                            dc = rnd * 4 + k
                            ms.append(scalar.activation(
                                out=scrA[:, k * 512:(k + 1) * 512],
                                in_=src[:, dc * T + t0: dc * T + t0 + 512],
                                func=AF.Square))
                        return ms
                    sq = P.last()

                    @P.step("tensor", deps=[sq])
                    def MM(tensor, t0=t0, rnd=rnd):
                        m = None
                        for k in range(4):
                            dc = rnd * 4 + k
                            tensor.matmul(ps_q[0:1, :], o128[:],
                                          src[:, dc * T + t0: dc * T + t0 + 512],
                                          start=(dc == 0), stop=(dc == DC - 1))
                            m = tensor.matmul(ps_q[32:33, :], o128[:],
                                              scrA[:, k * 512:(k + 1) * 512],
                                              start=(dc == 0), stop=(dc == DC - 1))
                        return m
                    mmlast = P.last()

                @P.step("vector", deps=[mmlast])
                def STAT1(vector):
                    vector.tensor_scalar(out=rows_a[:], in0=ps_q[0:1, :],
                                         scalar1=1.0 / D, scalar2=None, op0=ALU.mult)
                    return vector.tensor_scalar(out=rows_b[:], in0=ps_q[32:33, :],
                                                scalar1=1.0 / D, scalar2=None,
                                                op0=ALU.mult)
                v1 = P.last()

                @P.step("scalar", deps=[v1])
                def MUSQ(scalar):
                    return scalar.activation(out=rows_c[:], in_=rows_a[:],
                                             func=AF.Square)
                s1 = P.last()

                @P.step("vector", deps=[s1])
                def VARV(vector):
                    return vector.tensor_tensor(out=rows_b[:], in0=rows_b[:],
                                                in1=rows_c[:], op=ALU.subtract)
                v2 = P.last()

                @P.step("scalar", deps=[v2])
                def RSTD1(scalar):
                    return scalar.activation(out=rows_c[:], in_=rows_b[:],
                                             func=AF.Ln,
                                             bias=smalls[0:1, CEPS:CEPS + 1])
                s2a = P.last()

                @P.step("scalar", deps=[s2a])
                def RSTD2(scalar):
                    return scalar.activation(out=rows_b[:], in_=rows_c[:],
                                             func=AF.Exp,
                                             scale=smalls[0:1, CM05:CM05 + 1])
                s2 = P.last()

                deps_b = [s2] if prev is None else [s2, prev]

                @P.step("tensor", deps=deps_b)
                def BCAST(tensor):
                    tensor.matmul(ps_o[:, :], o1x[:], rows_a[:],
                                  start=True, stop=True)
                    return tensor.matmul(ps_p[:, :], o1x[:], rows_b[:],
                                         start=True, stop=True)
                bc = P.last()
                prev = apply_cb(tt, bc)
                out_ids.append(prev)
            return out_ids

        def ln_apply(src, dst_bf, gamma=None, beta=None):
            def cb(tt, dep):
                t0 = tt * 512
                last = dep
                for grp in range(2):
                    @P.step("vector", deps=[last], n_inst=4)
                    def SUBS(vector, grp=grp):
                        ms = []
                        for k in range(4):
                            dc = grp * 4 + k
                            ms.append(vector.tensor_tensor(
                                out=scrA[:, k * 512:(k + 1) * 512],
                                in0=src[:, dc * T + t0: dc * T + t0 + 512],
                                in1=ps_o[:, :], op=ALU.subtract))
                        return ms
                    subs = P.last()

                    @P.step("vector", deps=[subs], n_inst=4)
                    def MULS(vector, grp=grp):
                        ms = []
                        for k in range(4):
                            dc = grp * 4 + k
                            tgt = (scrA[:, k * 512:(k + 1) * 512] if gamma
                                   else dst_bf[:, dc * T + t0: dc * T + t0 + 512])
                            ms.append(vector.tensor_tensor(
                                out=tgt,
                                in0=scrA[:, k * 512:(k + 1) * 512],
                                in1=ps_p[:, :], op=ALU.mult))
                        return ms
                    muls = P.last()
                    last = muls
                    if gamma is not None:
                        gc = soff[gamma][0]
                        bcc = soff[beta][0]

                        @P.step("scalar", deps=[muls], n_inst=4)
                        def GB(scalar, grp=grp):
                            ms = []
                            for k in range(4):
                                dc = grp * 4 + k
                                ms.append(scalar.activation(
                                    out=dst_bf[:, dc * T + t0: dc * T + t0 + 512],
                                    in_=scrA[:, k * 512:(k + 1) * 512],
                                    func=AF.Identity,
                                    scale=smalls[:, gc + dc:gc + dc + 1],
                                    bias=smalls[:, bcc + dc:bcc + dc + 1]))
                            return ms
                        last = P.last()
                return last
            return cb

        P.phase = "ln1"
        ln1_ids = ln_stream(r_sb, [LOADS_ID], ln_apply(r_sb, xn_sb))
        LN1_DONE = ln1_ids[-1]

        kvl = kv_loc.ap().rearrange("a n -> (a n)")

        P.phase = "qkv"
        prev_rope_consumer = None
        prev_half_consumer = [None, None]
        prev_psbig_consumer = [None, None]
        prev_scrD_consumer = [None, None]
        prev_scrG_consumer = [None, None]
        last_pub = []

        for mc in range(24):
            wdeps = [AGW_ID, LN1_DONE]
            if prev_half_consumer[mc % 2] is not None:
                wdeps.append(prev_half_consumer[mc % 2])

            @P.step("sync", deps=wdeps, n_inst=0, n_dma=1)
            def WLD(sync, mc=mc):
                return ("dma", sync.dma_start(
                    out=wstr[:, (mc % 2) * 1024:(mc % 2) * 1024 + 1024],
                    in_=wvt("W1T", mc)))
            wld_id = P.last()
            mm_ids = []

            for tt in range(4):
                t0 = tt * 512
                pb = (tt % 2) * 1024
                mmdeps = [wld_id]
                if prev_psbig_consumer[tt % 2] is not None:
                    mmdeps.append(prev_psbig_consumer[tt % 2])

                @P.step("tensor", deps=mmdeps)
                def QKVMM(tensor, mc=mc, t0=t0, pb=pb):
                    m = None
                    for kc in range(DC):
                        m = tensor.matmul(
                            ps_big[:, pb:pb + 512],
                            wstr[:, (mc % 2) * 1024 + kc * 128:
                                 (mc % 2) * 1024 + kc * 128 + 128],
                            xn_sb[:, kc * T + t0: kc * T + t0 + 512],
                            start=(kc == 0), stop=(kc == DC - 1))
                    return m
                mmid = P.last()
                mm_ids.append(mmid)

                evdeps = [mmid]
                if prev_scrD_consumer[tt % 2] is not None:
                    evdeps.append(prev_scrD_consumer[tt % 2])

                @P.step("scalar", deps=evdeps)
                def EVQKV(scalar, pb=pb, tt=tt):
                    return scalar.activation(
                        out=scrD[:, (tt % 2) * 512:(tt % 2) * 512 + 512],
                        in_=ps_big[:, pb:pb + 512], func=AF.Copy)
                ev = P.last()
                prev_psbig_consumer[tt % 2] = ev

                if mc < 16:
                    is_q = mc < 8
                    h0 = (mc % 8) * 2
                    rotdeps = [ev]
                    if prev_rope_consumer is not None:
                        rotdeps.append(prev_rope_consumer)

                    @P.step("tensor", deps=rotdeps)
                    def ROTMM(tensor, tt=tt):
                        c0 = (tt % 2) * 512
                        tensor.matmul(ps_q[0:64, :], rot_sb[0:64, :],
                                      scrD[0:64, c0:c0 + 512],
                                      start=True, stop=True)
                        return tensor.matmul(ps_q[64:128, :], rot_sb[64:128, :],
                                             scrD[64:128, c0:c0 + 512],
                                             start=True, stop=True)
                    rot = P.last()

                    rpdeps = [rot, ROPE_ID]
                    if prev_scrG_consumer[tt % 2] is not None:
                        rpdeps.append(prev_scrG_consumer[tt % 2])

                    @P.step("vector", deps=rpdeps, n_inst=2)
                    def ROPEAP(vector, t0=t0, tt=tt):
                        c0 = (tt % 2) * 512
                        m1 = vector.tensor_tensor(
                            out=scrE[:, c0:c0 + 512],
                            in0=scrD[:, c0:c0 + 512],
                            in1=bufC[:, t0:t0 + 512], op=ALU.mult)
                        m2 = vector.tensor_tensor(
                            out=scrF[:, c0:c0 + 512],
                            in0=ps_q[:, :],
                            in1=bufC[:, T + t0:T + t0 + 512], op=ALU.mult)
                        return [m1, m2]
                    rpa = P.last()

                    @P.step("vector", deps=[rpa])
                    def ROPEADD(vector, tt=tt):
                        c0 = (tt % 2) * 512
                        return vector.tensor_tensor(
                            out=scrE[:, c0:c0 + 512],
                            in0=scrE[:, c0:c0 + 512],
                            in1=scrF[:, c0:c0 + 512], op=ALU.add)
                    rp = P.last()
                    prev_rope_consumer = rp
                    prev_scrD_consumer[tt % 2] = rp

                    @P.step("scalar", deps=[rp], n_inst=0, n_dma=2)
                    def QKPUB(scalar, t0=t0, tt=tt, h0=h0, is_q=is_q):
                        c0 = (tt % 2) * 512
                        kind = 1 if is_q else 0
                        ms = []
                        for hh in range(2):
                            base = (h0 + hh) * 3 * PC + kind * PC
                            dst = kvl[base:base + PC]
                            dst = dst.rearrange("(dh t) -> dh t", t=T)[:, t0:t0 + 512]
                            ms.append(("dma", scalar.dma_start(
                                out=dst,
                                in_=scrE[hh * 64:hh * 64 + 64, c0:c0 + 512])))
                        return ms
                    pub = P.last()
                    last_pub.append(pub)
                    prev_scrG_consumer[tt % 2] = pub
                else:
                    vtdeps = [ev]
                    if prev_rope_consumer is not None:
                        vtdeps.append(prev_rope_consumer)

                    @P.step("tensor", deps=vtdeps)
                    def VT(tensor, tt=tt):
                        c0 = (tt % 2) * 512
                        m = None
                        for blk in range(4):
                            m = tensor.transpose(
                                ps_t[:, blk * 128:blk * 128 + 128],
                                scrD[:, c0 + blk * 128:c0 + blk * 128 + 128],
                                id_sb[:])
                        return m
                    vt = P.last()

                    vtedeps = [vt]
                    if prev_scrG_consumer[tt % 2] is not None:
                        vtedeps.append(prev_scrG_consumer[tt % 2])

                    @P.step("vector", deps=vtedeps)
                    def VTEV(vector, tt=tt):
                        c0 = (tt % 2) * 512
                        return vector.tensor_copy(out=scrE[:, c0:c0 + 512],
                                                  in_=ps_t[:, 0:512])
                    vte = P.last()
                    prev_rope_consumer = vte
                    prev_scrD_consumer[tt % 2] = vte

                    @P.step("scalar", deps=[vte], n_inst=0, n_dma=8)
                    def VPUB(scalar, mc=mc, t0=t0, tt=tt):
                        c0 = (tt % 2) * 512
                        h0 = (mc - 16) * 2
                        ms = []
                        for blk in range(4):
                            tg0 = t0 + blk * 128
                            for hh in range(2):
                                base = (h0 + hh) * 3 * PC + 2 * PC
                                vtv = kvl[base:base + PC].rearrange(
                                    "(t dh) -> t dh", dh=DH)
                                ms.append(("dma", scalar.dma_start(
                                    out=vtv[tg0:tg0 + 128, :],
                                    in_=scrE[:, c0 + blk * 128 + hh * 64:
                                             c0 + blk * 128 + hh * 64 + 64])))
                        return ms
                    pub = P.last()
                    last_pub.append(pub)
                    prev_scrG_consumer[tt % 2] = pub
            prev_half_consumer[mc % 2] = mm_ids[-1]

        P.phase = "attn"

        @P.step("gpsimd", deps=[last_pub[-1]])
        def AGKV(gpsimd):
            return gpsimd.collective_compute(
                "AllGather", ALU.bypass,
                replica_groups=[[0, 1, 2, 3], [4, 5, 6, 7]],
                ins=[kv_loc.ap().opt()], outs=[kv_all_t.ap().opt()])
        AGKV_ID = P.last()

        def rank_half_of_chunk(c):
            return (c, 0) if c < 4 else (7 - c, 1)

        ao_l = ao_loc.ap().rearrange("a n -> (a n)")
        aol_v = ao_l

        @P.step("sync", deps=[AGKV_ID], n_inst=0, n_dma=4)
        def HDYN(sync):
            if "g" not in regs:
                regs["g"] = sync.alloc_register("rg")
                regs["o"] = sync.alloc_register("ro")
            rg, ro = regs["g"], regs["o"]
            sync.reg_load(rg, dyn_sb[0:1, 0:1])
            ms = []
            for rk in range(4):
                sync.reg_mul(ro, rg, 12 * PC)
                sync.reg_add(ro, ro, rk * KVN)
                ms.append(("dma", sync.dma_start(
                    out=kvstage.ap()[rk:rk + 1, :],
                    in_=bass.AP(kv_all_t, ro, [[1, 1], [1, 12 * PC]]))))
            return ms
        HDYN_ID = P.last()

        prev_qt_chain = HDYN_ID
        exps_hist = [HDYN_ID, HDYN_ID]
        pvmm_hist = [HDYN_ID, HDYN_ID]
        gp = 0
        for hl in range(4):
            @P.step("sync", deps=[HDYN_ID, prev_qt_chain], n_inst=0, n_dma=24)
            def HLD(sync, hl=hl):
                ms = []
                for rk in range(4):
                    base = hl * 3 * PC
                    kpiece = kvstage.ap()[rk, base:base + PC].rearrange(
                        "(dh t) -> dh t", t=T)
                    qpiece = kvstage.ap()[rk, base + PC:base + 2 * PC].rearrange(
                        "(dh t) -> dh t", t=T)
                    vpiece = kvstage.ap()[rk, base + 2 * PC:base + 3 * PC].rearrange(
                        "(t dh) -> t dh", dh=DH)
                    for half in range(2):
                        c = rk if half == 0 else 7 - rk
                        ms.append(("dma", sync.dma_start(
                            out=kfull[:, c * CH:(c + 1) * CH],
                            in_=kpiece[:, half * CH:(half + 1) * CH])))
                        ms.append(("dma", sync.dma_start(
                            out=qfull[:, c * CH:(c + 1) * CH],
                            in_=qpiece[:, half * CH:(half + 1) * CH])))
                        dst = vstb[:, c * 8 * 65:(c + 1) * 8 * 65].rearrange(
                            "p (b c2) -> p b c2", c2=65)[:, :, 0:64]
                        ms.append(("dma", sync.dma_start(
                            out=dst,
                            in_=vpiece[half * CH:(half + 1) * CH, :].rearrange(
                                "(b p) dh -> p b dh", p=128))))
                return ms
            hld = P.last()

            @P.step("vector", deps=[hld])
            def VONES(vector):
                return vector.memset(
                    vstb[:].rearrange("p (b c2) -> p b c2", c2=65)[:, :, 64:65], 1.0)
            vones = P.last()

            nm_last = []
            prev_nmx = None
            for which, buf in (("q", qfull), ("k", kfull)):
                for span in range(8):
                    s0 = span * 1024

                    @P.step("scalar", deps=[vones] if span == 0 else None)
                    def NSQ(scalar, buf=buf, s0=s0):
                        return scalar.activation(out=abuf[0:64, 0:1024],
                                                 in_=buf[:, s0:s0 + 1024],
                                                 func=AF.Square)
                    nsq = P.last()

                    nmmdeps = [nsq]
                    if prev_nmx is not None:
                        nmmdeps.append(prev_nmx)

                    @P.step("tensor", deps=nmmdeps)
                    def NMM(tensor):
                        m = None
                        for i in range(2):
                            m = tensor.matmul(ps_big[0:1, i * 512:i * 512 + 512],
                                              o128[0:64, :],
                                              abuf[0:64, i * 512:i * 512 + 512],
                                              start=True, stop=True)
                        return m
                    nmm = P.last()

                    @P.step("vector", deps=[nmm])
                    def NMXa(vector):
                        return vector.reduce_max(out=rows_c[0:1, 0:1],
                                                 in_=ps_big[0:1, 0:1024], axis=AX)
                    nmxa = P.last()

                    @P.step("vector", deps=[nmxa])
                    def NMXb(vector, which=which, hl=hl, span=span):
                        col = hl if which == "q" else 8 + hl
                        if span == 0:
                            return vector.tensor_copy(out=mxv[0:1, col:col + 1],
                                                      in_=rows_c[0:1, 0:1])
                        return vector.tensor_tensor(
                            out=mxv[0:1, col:col + 1],
                            in0=mxv[0:1, col:col + 1],
                            in1=rows_c[0:1, 0:1], op=ALU.max)
                    prev_nmx = P.last()
                    nm_last.append(prev_nmx)

            @P.step("vector", deps=[nm_last[-1]])
            def MH(vector, hl=hl):
                return vector.tensor_tensor(out=rows_c[0:1, 0:1],
                                            in0=mxv[0:1, hl:hl + 1],
                                            in1=mxv[0:1, 8 + hl:8 + hl + 1],
                                            op=ALU.mult)
            mh1 = P.last()

            @P.step("scalar", deps=[mh1])
            def MH2(scalar):
                return scalar.activation(out=rows_c[0:1, 2:3],
                                         in_=rows_c[0:1, 0:1], func=AF.Sqrt)
            mh2 = P.last()

            @P.step("vector", deps=[mh2])
            def MH3(vector):
                return vector.tensor_scalar(out=rows_c[0:1, 3:4],
                                            in0=rows_c[0:1, 2:3],
                                            scalar1=-0.125, scalar2=None,
                                            op0=ALU.mult)
            mh3 = P.last()

            @P.step("tensor", deps=[mh3])
            def MHB(tensor):
                return tensor.matmul(ps_q[:, 0:1], o1x[:], rows_c[0:1, 3:4],
                                     start=True, stop=True)
            mhb = P.last()

            @P.step("vector", deps=[mhb])
            def MHC(vector, hl=hl):
                return vector.tensor_copy(out=mxc[:, hl:hl + 1],
                                          in_=ps_q[:, 0:1])
            mhc = P.last()

            prev_oev = mhc
            prev_onrm = mhc
            prev_aow = mhc
            for qt in range(16):
                nkb = 4 * qt + 4
                npr = (nkb + 1) // 2
                qt_pv = []
                for g in range(npr):
                    kbs = [k for k in (2 * g, 2 * g + 1) if k < nkb]
                    half = gp % 2
                    pb = half * 1024

                    @P.step("tensor", deps=[exps_hist[gp], mhc])
                    def STMM(tensor, qt=qt, kbs=kbs, pb=pb):
                        m = None
                        for i, kb in enumerate(kbs):
                            m = tensor.matmul(
                                ps_big[:, pb + i * 512:pb + i * 512 + 512],
                                kfull[:, kb * 128:(kb + 1) * 128],
                                qfull[:, qt * 512:qt * 512 + 512],
                                start=True, stop=True)
                        return m
                    stmm = P.last()

                    @P.step("scalar", deps=[stmm, pvmm_hist[gp]],
                            n_inst=len(kbs))
                    def EXPS(scalar, hl=hl, kbs=kbs, pb=pb, half=half):
                        ms = []
                        for i, kb in enumerate(kbs):
                            ms.append(scalar.activation(
                                out=scrD[:, half * 1024 + i * 512:
                                         half * 1024 + i * 512 + 512],
                                in_=ps_big[:, pb + i * 512:pb + i * 512 + 512],
                                func=AF.Exp, scale=smalls[:, C0125:C0125 + 1],
                                bias=mxc[:, hl:hl + 1]))
                        return ms
                    exps = P.last()
                    exps_hist.append(exps)

                    diag = [kb for kb in kbs if kb >= 4 * qt]
                    dep_mm2 = exps
                    if diag:
                        @P.step("vector", deps=[exps], n_inst=len(diag))
                        def MASK(vector, diag=diag, kbs=kbs, qt=qt, half=half):
                            ms = []
                            for kb in diag:
                                i = kb - kbs[0]
                                mi = kb - 4 * qt
                                ms.append(vector.tensor_tensor(
                                    out=scrD[:, half * 1024 + i * 512:
                                             half * 1024 + i * 512 + 512],
                                    in0=scrD[:, half * 1024 + i * 512:
                                             half * 1024 + i * 512 + 512],
                                    in1=tri_sb[:, mi * 512:(mi + 1) * 512],
                                    op=ALU.mult))
                            return ms
                        dep_mm2 = P.last()

                    pvdeps = [dep_mm2]
                    if g == 0:
                        pvdeps.append(prev_oev)

                    @P.step("tensor", deps=pvdeps)
                    def PVMM(tensor, kbs=kbs, half=half, first=(g == 0),
                             lastg=(g == npr - 1)):
                        m = None
                        for i, kb in enumerate(kbs):
                            m = tensor.matmul(
                                ps_o[0:65, :],
                                vstb[:, kb * 65:(kb + 1) * 65],
                                scrD[:, half * 1024 + i * 512:
                                     half * 1024 + i * 512 + 512],
                                start=(first and i == 0),
                                stop=(lastg and i == len(kbs) - 1))
                        return m
                    pv = P.last()
                    pvmm_hist.append(pv)
                    qt_pv.append(pv)
                    gp += 1

                @P.step("vector", deps=[qt_pv[-1], prev_onrm, prev_aow])
                def ZREC(vector):
                    return vector.reciprocal(out=rows_c[0:1, 0:512],
                                             in_=ps_o[64:65, :])
                zr = P.last()

                @P.step("tensor", deps=[zr])
                def ZB(tensor):
                    return tensor.matmul(ps_p[0:64, :], o1x[0:1, 0:64],
                                         rows_c[0:1, 0:512],
                                         start=True, stop=True)
                zb = P.last()

                @P.step("scalar", deps=[qt_pv[-1], prev_onrm])
                def OEV(scalar):
                    return scalar.activation(out=scrE[0:64, 0:512],
                                             in_=ps_o[0:64, :], func=AF.Copy)
                prev_oev = P.last()

                @P.step("vector", deps=[prev_oev, zb])
                def ONRM(vector):
                    return vector.tensor_tensor(out=scrF[0:64, 0:512],
                                                in0=scrE[0:64, 0:512],
                                                in1=ps_p[0:64, :], op=ALU.mult)
                prev_onrm = P.last()

                c = qt // 2
                rk, half2 = rank_half_of_chunk(c)
                lt = half2 * CH + (qt % 2) * 512

                @P.step("scalar", deps=[prev_onrm], n_inst=0, n_dma=1)
                def AOW(scalar, hl=hl, rk=rk, lt=lt):
                    dst = aol_v[rk * AOB + hl * DH * T:
                                rk * AOB + (hl + 1) * DH * T]
                    dst = dst.rearrange("(dh t) -> dh t", t=T)[:, lt:lt + 512]
                    return ("dma", scalar.dma_start(out=dst,
                                                    in_=scrF[0:64, 0:512]))
                prev_aow = P.last()
                prev_qt_chain = prev_aow

        @P.step("gpsimd", deps=[prev_qt_chain])
        def AGAO(gpsimd):
            return gpsimd.collective_compute(
                "AllGather", ALU.bypass,
                replica_groups=[[0, 1, 2, 3], [4, 5, 6, 7]],
                ins=[ao_loc.ap().opt()], outs=[ao_all_t.ap().opt()])
        AGAO_ID = P.last()

        @P.step("sync", deps=[AGAO_ID], n_inst=0, n_dma=4)
        def AODYN(sync):
            rg, ro = regs["g"], regs["o"]
            sync.reg_load(rg, dyn_sb[0:1, 0:1])
            ms = []
            for r in range(4):
                sync.reg_mul(ro, rg, AOB)
                sync.reg_add(ro, ro, r * AON)
                ms.append(("dma", sync.dma_start(
                    out=aostage.ap()[r:r + 1, :],
                    in_=bass.AP(ao_all_t, ro, [[1, 1], [1, AOB]]))))
            return ms
        aodyn = P.last()

        @P.step("sync", deps=[aodyn], n_inst=0, n_dma=16)
        def AORD(sync):
            ms = []
            for h in range(H):
                r, hlh = h // 4, h % 4
                dc, p0 = h // 2, (h % 2) * 64
                ms.append(("dma", sync.dma_start(
                    out=xn_sb[p0:p0 + 64, dc * T:(dc + 1) * T],
                    in_=aostage.ap()[r, hlh * PC:(hlh + 1) * PC]
                        .rearrange("(dh t) -> dh t", t=T))))
            return ms
        AORD_ID = P.last()

        P.phase = "w2"
        prev = AORD_ID
        prev_half = [AORD_ID, AORD_ID]
        prev_pb = [AORD_ID, AORD_ID]
        for dc in range(DC):
            w2deps = [prev_half[dc % 2]] if dc >= 2 else [AGKV_ID]

            @P.step("sync", deps=w2deps, n_inst=0, n_dma=1)
            def W2LD(sync, dc=dc):
                return ("dma", sync.dma_start(
                    out=wstr[:, (dc % 2) * 1024:(dc % 2) * 1024 + 1024],
                    in_=wvt("W2T", dc)))
            wld2 = P.last()
            mmlist = []
            for tt in range(4):
                t0 = tt * 512

                @P.step("tensor", deps=[wld2, prev_pb[tt % 2], AORD_ID])
                def W2MM(tensor, dc=dc, t0=t0, tt=tt):
                    pp = (tt % 2) * 1024
                    m = None
                    for kc in range(DC):
                        m = tensor.matmul(
                            ps_big[:, pp:pp + 512],
                            wstr[:, (dc % 2) * 1024 + kc * 128:
                                 (dc % 2) * 1024 + kc * 128 + 128],
                            xn_sb[:, kc * T + t0: kc * T + t0 + 512],
                            start=(kc == 0), stop=(kc == DC - 1))
                    return m
                w2mm = P.last()

                @P.step("vector", deps=[w2mm])
                def RADD(vector, dc=dc, t0=t0, tt=tt):
                    pp = (tt % 2) * 1024
                    return vector.tensor_tensor(
                        out=r_sb[:, dc * T + t0: dc * T + t0 + 512],
                        in0=r_sb[:, dc * T + t0: dc * T + t0 + 512],
                        in1=ps_big[:, pp:pp + 512], op=ALU.add)
                prev_pb[tt % 2] = P.last()
                mmlist.append(w2mm)
            prev_half[dc % 2] = mmlist[-1]
        SUBL1 = P.last()


        def ffn_block(prev_done, pre_dep=None):
            @P.step("sync", deps=[prev_done, AGW_ID], n_inst=0, n_dma=1)
            def GLD(sync):
                return ("dma", sync.dma_start(out=gsb[:], in_=wvt("gate", 0)))
            prev = P.last()

            for tt in range(4):
                t0 = tt * 512

                @P.step("tensor", deps=[prev])
                def GMM(tensor, t0=t0):
                    m = None
                    for kc in range(DC):
                        m = tensor.matmul(
                            ps_q[0:8, :], gsb[:, kc * 8:(kc + 1) * 8],
                            xn_sb[:, kc * T + t0: kc * T + t0 + 512],
                            start=(kc == 0), stop=(kc == DC - 1))
                    return m
                gmm = P.last()

                @P.step("scalar", deps=[gmm])
                def GEV(scalar):
                    return scalar.activation(out=scrB[0:8, 0:512],
                                             in_=ps_q[0:8, :], func=AF.Copy)
                gev = P.last()

                @P.step("tensor", deps=[gev])
                def GTR(tensor):
                    m = None
                    for blk in range(4):
                        m = tensor.transpose(ps_p[:, blk * 8:(blk + 1) * 8],
                                             scrB[0:8, blk * 128:(blk + 1) * 128],
                                             id32_sb[0:8, 0:8])
                    return m
                gtr = P.last()

                @P.step("vector", deps=[gtr])
                def GT2(vector):
                    return vector.tensor_copy(out=scrA[:, 1024:1056],
                                              in_=ps_p[:, 0:32])
                gt2 = P.last()

                @P.step("vector", deps=[gt2], n_inst=4)
                def TOPA1(vector):
                    ms = []
                    for blk in range(4):
                        g0 = 1024 + blk * 8
                        mx1 = 1056 + blk
                        ms.append(vector.reduce_max(
                            out=scrA[:, mx1:mx1 + 1],
                            in_=scrA[:, g0:g0 + 8], axis=AX))
                    return ms
                topa1 = P.last()

                @P.step("vector", deps=[topa1], n_inst=4)
                def TOPA2(vector):
                    ms = []
                    for blk in range(4):
                        g0 = 1024 + blk * 8
                        mx1 = 1056 + blk
                        d1 = 1064 + blk * 8
                        ms.append(vector.tensor_scalar(
                            out=scrA[:, d1:d1 + 8], in0=scrA[:, g0:g0 + 8],
                            scalar1=scrA[:, mx1:mx1 + 1], scalar2=None,
                            op0=ALU.subtract))
                    return ms
                topa = P.last()

                @P.step("scalar", deps=[topa], n_inst=4)
                def SGN1(scalar):
                    ms = []
                    for blk in range(4):
                        d1 = 1064 + blk * 8
                        s1 = 1096 + blk * 8
                        ms.append(scalar.activation(
                            out=scrA[:, s1:s1 + 8], in_=scrA[:, d1:d1 + 8],
                            func=AF.Sign))
                    return ms
                sgn1 = P.last()

                @P.step("vector", deps=[sgn1], n_inst=4)
                def S4a(vector):
                    ms = []
                    for blk in range(4):
                        s1 = 1096 + blk * 8
                        mk1 = 1128 + blk * 8
                        ms.append(vector.tensor_scalar(
                            out=scrA[:, mk1:mk1 + 8], in0=scrA[:, s1:s1 + 8],
                            scalar1=1.0, scalar2=None, op0=ALU.add))
                    return ms
                s4a = P.last()

                @P.step("vector", deps=[s4a], n_inst=4)
                def S4b(vector):
                    ms = []
                    for blk in range(4):
                        g0 = 1024 + blk * 8
                        mk1 = 1128 + blk * 8
                        g2 = 1160 + blk * 8
                        ms.append(vector.scalar_tensor_tensor(
                            out=scrA[:, g2:g2 + 8], in0=scrA[:, mk1:mk1 + 8],
                            scalar=-1e9, in1=scrA[:, g0:g0 + 8],
                            op0=ALU.mult, op1=ALU.add))
                    return ms
                s4 = P.last()

                @P.step("vector", deps=[s4], n_inst=4)
                def S5(vector):
                    ms = []
                    for blk in range(4):
                        g2 = 1160 + blk * 8
                        mx2 = 1192 + blk
                        ms.append(vector.reduce_max(
                            out=scrA[:, mx2:mx2 + 1],
                            in_=scrA[:, g2:g2 + 8], axis=AX))
                    return ms
                s5 = P.last()

                @P.step("vector", deps=[s5], n_inst=4 * 2)
                def S6(vector):
                    ms = []
                    for blk in range(4):
                        g2 = 1160 + blk * 8
                        mx1 = 1056 + blk
                        mx2 = 1192 + blk
                        d12 = 1196 + blk
                        d2 = 1208 + blk * 8
                        ms.append(vector.tensor_scalar(
                            out=scrA[:, d2:d2 + 8], in0=scrA[:, g2:g2 + 8],
                            scalar1=scrA[:, mx2:mx2 + 1], scalar2=None,
                            op0=ALU.subtract))
                        ms.append(vector.tensor_tensor(
                            out=scrA[:, d12:d12 + 1], in0=scrA[:, mx1:mx1 + 1],
                            in1=scrA[:, mx2:mx2 + 1], op=ALU.subtract))
                    return ms
                topb = P.last()

                @P.step("scalar", deps=[topb], n_inst=8)
                def S7(scalar):
                    ms = []
                    for blk in range(4):
                        d12 = 1196 + blk
                        w1 = 1200 + blk
                        d2 = 1208 + blk * 8
                        s2 = 1240 + blk * 8
                        ms.append(scalar.activation(
                            out=scrA[:, w1:w1 + 1], in_=scrA[:, d12:d12 + 1],
                            func=AF.Sigmoid))
                        ms.append(scalar.activation(
                            out=scrA[:, s2:s2 + 8], in_=scrA[:, d2:d2 + 8],
                            func=AF.Sign))
                    return ms
                sigw = P.last()

                @P.step("vector", deps=[sigw], n_inst=4 * 2)
                def S8(vector):
                    ms = []
                    for blk in range(4):
                        w1 = 1200 + blk
                        w2 = 1204 + blk
                        s2 = 1240 + blk * 8
                        mk2 = 1272 + blk * 8
                        ms.append(vector.tensor_scalar(
                            out=scrA[:, w2:w2 + 1], in0=scrA[:, w1:w1 + 1],
                            scalar1=-1.0, scalar2=1.0, op0=ALU.mult, op1=ALU.add))
                        ms.append(vector.tensor_scalar(
                            out=scrA[:, mk2:mk2 + 8], in0=scrA[:, s2:s2 + 8],
                            scalar1=1.0, scalar2=None, op0=ALU.add))
                    return ms
                s8 = P.last()

                @P.step("vector", deps=[s8], n_inst=4 * 2)
                def S9a(vector):
                    ms = []
                    for blk in range(4):
                        mk1 = 1128 + blk * 8
                        mk2 = 1272 + blk * 8
                        w1 = 1200 + blk
                        w2 = 1204 + blk
                        t2 = 1304 + blk * 8
                        wt = 1336 + blk * 8
                        ms.append(vector.tensor_scalar(
                            out=scrA[:, wt:wt + 8], in0=scrA[:, mk1:mk1 + 8],
                            scalar1=scrA[:, w1:w1 + 1], scalar2=None,
                            op0=ALU.mult))
                        ms.append(vector.tensor_scalar(
                            out=scrA[:, t2:t2 + 8], in0=scrA[:, mk2:mk2 + 8],
                            scalar1=scrA[:, w2:w2 + 1], scalar2=None,
                            op0=ALU.mult))
                    return ms
                s9a = P.last()

                @P.step("vector", deps=[s9a], n_inst=4)
                def S9b(vector):
                    ms = []
                    for blk in range(4):
                        t2 = 1304 + blk * 8
                        wt = 1336 + blk * 8
                        ms.append(vector.tensor_tensor(
                            out=scrA[:, wt:wt + 8], in0=scrA[:, wt:wt + 8],
                            in1=scrA[:, t2:t2 + 8], op=ALU.add))
                    return ms
                wbuild = P.last()

                @P.step("tensor", deps=[wbuild])
                def WTR(tensor):
                    m = None
                    for blk in range(4):
                        wt = 1336 + blk * 8
                        m = tensor.transpose(ps_q[0:8, blk * 128:(blk + 1) * 128]
                                             if False else
                                             ps_p[0:8, blk * 128:(blk + 1) * 128],
                                             scrA[:, wt:wt + 8], id32_sb[:])
                    return m
                wtr = P.last()

                @P.step("scalar", deps=[wtr])
                def W8EV(scalar):
                    return scalar.activation(out=scrD[0:8, 0:512],
                                             in_=ps_p[0:8, 0:512], func=AF.Copy)
                w8ev = P.last()

                @P.step("scalar", deps=[w8ev], n_inst=0, n_dma=1)
                def W8ST(scalar, t0=t0):
                    return ("dma", scalar.dma_start(
                        out=w8_dram.ap()[:, t0:t0 + 512], in_=scrD[0:8, 0:512]))
                prev = P.last()

            w8_done = prev
            ring_cons = []
            gel_hist = []
            prev_last_pmm = w8_done
            prev_cmbs = [w8_done, w8_done]
            last = w8_done
            rglob = 0
            for tc in range(4):
                t0 = tc * 512

                @P.step("sync", deps=[last, w8_done], n_inst=0, n_dma=8)
                def WBLD(sync, t0=t0):
                    ms = []
                    for e in range(E):
                        slot = scrD if e < 4 else scrE
                        ms.append(("dma", sync.dma_start(
                            out=slot[:, (e % 4) * 512:(e % 4) * 512 + 512],
                            in_=bass.AP(w8_dram, e * T + t0, [[0, 128], [1, 512]]))))
                    return ms
                wbld = P.last()

                for e in range(E):
                    gels_this = []
                    for fcp in range(16):
                        slot = rglob % 4
                        eng = ("sync", "gpsimd", "sync", "scalar")[rglob % 4]
                        if rglob >= 4:
                            ldeps = [ring_cons[rglob - 4]]
                        else:
                            ldeps = [wbld] if pre_dep is None else [pre_dep]

                        @P.step(eng, deps=ldeps, n_inst=0, n_dma=1)
                        def FLD(eh, e=e, fcp=fcp, slot=slot):
                            return ("dma", eh.dma_start(
                                out=bufC[:, slot * 2048:(slot + 1) * 2048]
                                    .rearrange("p (b x) -> p b x", b=2),
                                in_=wvt2(f"fc{e}", fcp)))
                        ld = P.last()

                        mdeps = [ld, wbld]
                        if len(gel_hist) >= 1:
                            mdeps.append(gel_hist[-1])

                        @P.step("tensor", deps=mdeps)
                        def FMM(tensor, t0=t0, fcp=fcp, slot=slot):
                            m = None
                            for i in range(2):
                                for kc in range(DC):
                                    m = tensor.matmul(
                                        ps_big[:, (fcp % 2) * 1024 + i * 512:
                                               (fcp % 2) * 1024 + i * 512 + 512],
                                        bufC[:, slot * 2048 + i * 1024 + kc * 128:
                                             slot * 2048 + i * 1024 + kc * 128 + 128],
                                        xn_sb[:, kc * T + t0: kc * T + t0 + 512],
                                        start=(kc == 0), stop=(kc == DC - 1))
                            return m
                        fmm = P.last()
                        ring_cons.append(fmm)
                        rglob += 1

                        @P.step("scalar", deps=[fmm, prev_last_pmm])
                        def GEL(scalar, fcp=fcp):
                            hb = bufA if fcp < 8 else bufB
                            return scalar.activation(
                                out=hb[:, (fcp % 8) * 1024:(fcp % 8) * 1024 + 1024],
                                in_=ps_big[:, (fcp % 2) * 1024:
                                           (fcp % 2) * 1024 + 1024],
                                func=AF.Gelu)
                        gel_hist.append(P.last())
                        gels_this.append(gel_hist[-1])

                    for dc in range(DC):
                        pp = ps_o if dc % 2 == 0 else ps_p
                        pmm = None
                        for php in range(2):
                            slot = rglob % 4
                            eng = ("sync", "gpsimd", "sync", "scalar")[rglob % 4]
                            ldeps = [ring_cons[rglob - 4]]

                            @P.step(eng, deps=ldeps, n_inst=0, n_dma=1)
                            def PLD(eh, e=e, dc=dc, php=php, slot=slot):
                                return ("dma", eh.dma_start(
                                    out=bufC[:, slot * 2048:(slot + 1) * 2048],
                                    in_=wvt(f"pj{e}", dc)
                                        [:, php * 2048:(php + 1) * 2048]))
                            ld = P.last()

                            mdeps = [ld, gels_this[php * 8 + 7]]
                            if php == 0:
                                mdeps.append(prev_cmbs[dc % 2])

                            @P.step("tensor", deps=mdeps)
                            def PMM(tensor, dc=dc, php=php, slot=slot, pp=pp):
                                m = None
                                for b2 in range(16):
                                    fc = php * 16 + b2
                                    hb = bufA if fc < 16 else bufB
                                    m = tensor.matmul(
                                        pp[:, :],
                                        bufC[:, slot * 2048 + b2 * 128:
                                             slot * 2048 + b2 * 128 + 128],
                                        hb[:, (fc % 16) * 512:(fc % 16) * 512 + 512],
                                        start=(php == 0 and b2 == 0),
                                        stop=(php == 1 and b2 == 15))
                                return m
                            pmm = P.last()
                            ring_cons.append(pmm)
                            rglob += 1

                        @P.step("vector", deps=[pmm])
                        def CMBa(vector, e=e, pp=pp):
                            slot = scrD if e < 4 else scrE
                            return vector.tensor_tensor(
                                out=scrF[:, 0:512], in0=pp[:, :],
                                in1=slot[:, (e % 4) * 512:(e % 4) * 512 + 512],
                                op=ALU.mult)
                        cmba = P.last()

                        @P.step("vector", deps=[cmba])
                        def CMBb(vector, dc=dc, t0=t0):
                            return vector.tensor_tensor(
                                out=r_sb[:, dc * T + t0: dc * T + t0 + 512],
                                in0=r_sb[:, dc * T + t0: dc * T + t0 + 512],
                                in1=scrF[:, 0:512], op=ALU.add)
                        prev_cmbs[dc % 2] = P.last()
                        last = prev_cmbs[dc % 2]
                    prev_last_pmm = ring_cons[-1]
            return last

        if BUILD_UPTO >= 2:
            P.phase = "ffn1"
            ln2_ids = ln_stream(r_sb, [SUBL1], ln_apply(r_sb, xn_sb))
            SUBL2 = ffn_block(ln2_ids[-1], pre_dep=AGAO_ID)
        else:
            SUBL2 = SUBL1

        def sel_view(n):
            return cst[0:16, n * 128:(n + 1) * 128]

        def mamba_block(prev_done):
            OHA = soff["ohA"][0]
            OHB = soff["ohB"][0]
            OHPA = soff["ohPA"][0]
            OHPB = soff["ohPB"][0]
            CONE = soff["cone"][0]

            P.phase = "mm_prep"

            @P.step("sync", deps=[prev_done], n_inst=0, n_dma=2)
            def PARKR(sync):
                return [("dma", sync.dma_start(out=park_r.ap()[:, :], in_=r_sb[:])),
                        ("dma", sync.dma_start(out=cst[0:16, :], in_=sel16c[:]))]
            parkr = P.last()

            @P.step("vector", deps=[parkr])
            def ONESB(vector):
                return vector.memset(ones_bf[:], 1.0)
            onesb = P.last()

            ln3 = ln_stream(r_sb, [onesb], ln_apply(r_sb, r_sb))
            prev = ln3[-1]

            dtv_l = {l: wv(f"dtp{l}") for l in range(NL)}

            for l in range(NL):
                alc = soff[f"alog{l}"][0]
                dtbc = soff[f"dtb{l}"][0]
                dskc = soff[f"dsk{l}"][0]
                cbc = soff[f"cb{l}"][0]

                P.phase = "mm_ln"

                @P.step("scalar", deps=[prev])
                def NEGA1(scalar, alc=alc):
                    return scalar.activation(out=negA[:, 0:256],
                                             in_=smalls[:, alc:alc + 256],
                                             func=AF.Exp)
                nega1 = P.last()

                @P.step("vector", deps=[nega1])
                def NEGA2(vector):
                    return vector.tensor_scalar(out=negA[:, 0:256],
                                                in0=negA[:, 0:256],
                                                scalar1=-1.0, scalar2=None,
                                                op0=ALU.mult)
                nega = P.last()

                lnl = ln_stream(r_sb, [nega],
                                ln_apply(r_sb, xn_sb, gamma=f"nw{l}", beta=f"nb{l}"))
                LNL = lnl[-1]

                prev_q = LNL

                for q in range(4):
                    t0q = q * 512
                    ci = q // 2

                    P.phase = "mm_inproj"
                    imms = []
                    ievs = []
                    for oc in range(32):
                        eng = ("sync", "gpsimd", "sync")[oc % 3]
                        wdeps = [imms[oc - 2]] if oc >= 2 else [prev_q]

                        @P.step(eng, deps=wdeps, n_inst=0, n_dma=1)
                        def IWLD(eh, l=l, oc=oc):
                            return ("dma", eh.dma_start(
                                out=wstr[:, (oc % 2) * 1024:(oc % 2) * 1024 + 1024],
                                in_=wvt(f"in{l}", oc)))
                        iwld = P.last()

                        mdeps = [iwld] + ([ievs[oc - 2]] if oc >= 2 else [prev_q])

                        @P.step("tensor", deps=mdeps)
                        def IMM(tensor, oc=oc, t0q=t0q, q=q):
                            m = None
                            for kc in range(DC):
                                m = tensor.matmul(
                                    ps_big[:, (oc % 2) * 512:(oc % 2) * 512 + 512],
                                    wstr[:, (oc % 2) * 1024 + kc * 128:
                                         (oc % 2) * 1024 + kc * 128 + 128],
                                    xn_sb[:, kc * T + t0q: kc * T + t0q + 512],
                                    start=(kc == 0), stop=(kc == DC - 1))
                            if q == 0 and oc < 16:
                                for ci2, toff in enumerate((CH - 3, T - 3)):
                                    for kc in range(DC):
                                        m = tensor.matmul(
                                            ps_q[:, (oc % 2) * 8 + ci2 * 4:
                                                 (oc % 2) * 8 + ci2 * 4 + 3],
                                            wstr[:, (oc % 2) * 1024 + kc * 128:
                                                 (oc % 2) * 1024 + kc * 128 + 128],
                                            xn_sb[:, kc * T + toff:
                                                  kc * T + toff + 3],
                                            start=(kc == 0), stop=(kc == DC - 1))
                            return m
                        imm = P.last()
                        imms.append(imm)

                        nin = 3 if (q == 0 and oc < 16) else 1

                        @P.step("scalar", deps=[imm], n_inst=nin)
                        def IEV(scalar, oc=oc, q=q):
                            ms = []
                            if oc < 16:
                                ms.append(scalar.activation(
                                    out=bufA[:, oc * 512:(oc + 1) * 512],
                                    in_=ps_big[:, (oc % 2) * 512:
                                               (oc % 2) * 512 + 512],
                                    func=AF.Copy))
                                if q == 0:
                                    for ci2 in range(2):
                                        ms.append(scalar.activation(
                                            out=carry[:, 1024 + ci2 * 48 + oc * 3:
                                                      1024 + ci2 * 48 + oc * 3 + 3],
                                            in_=ps_q[:, (oc % 2) * 8 + ci2 * 4:
                                                     (oc % 2) * 8 + ci2 * 4 + 3],
                                            func=AF.Copy))
                            else:
                                ms.append(scalar.activation(
                                    out=bufC[:, (oc - 16) * 512:(oc - 15) * 512],
                                    in_=ps_big[:, (oc % 2) * 512:
                                               (oc % 2) * 512 + 512],
                                    func=AF.Silu))
                            return ms
                        ievs.append(P.last())
                    INPJ = ievs[-1]

                    hexid = None
                    if q == 0:
                        @P.step("sync", deps=[ievs[15]], n_inst=0, n_dma=1)
                        def AGAW(sync):
                            return ("dma", sync.dma_start(
                                out=aga_loc.ap().rearrange("a (p c) -> (a p) c",
                                                           c=96),
                                in_=carry[:, 1024:1120]))
                        agaw = P.last()

                        @P.step("gpsimd", deps=[agaw])
                        def AGA(gpsimd, l=l):
                            return gpsimd.collective_compute(
                                "AllGather", ALU.bypass,
                                replica_groups=[[0, 1, 2, 3], [4, 5, 6, 7]],
                                ins=[aga_loc.ap().opt()],
                                outs=[aga_all[l].ap().opt()])
                        aga = P.last()

                        @P.step("sync", deps=[aga], n_inst=0, n_dma=8)
                        def HALD(sync, l=l):
                            ms = []
                            for g in range(8):
                                rk = g if g < 4 else 7 - g
                                sl = 0 if g < 4 else 1
                                src2 = aga_all[l].ap()[rk:rk + 1, :].rearrange(
                                    "r (p c) -> (r p) c", c=96)[:,
                                                                sl * 48:sl * 48 + 48]
                                ms.append(("dma", sync.dma_start(
                                    out=scrA[:, g * 48:(g + 1) * 48], in_=src2)))
                            return ms
                        hald = P.last()

                        @P.step("vector", deps=[hald])
                        def HZERO(vector):
                            return vector.memset(carry[:, 1168:1264], 0.0)
                        hz = P.last()

                        @P.step("vector", deps=[hz], n_inst=32)
                        def HEX(vector):
                            ms = []
                            for g in range(8):
                                ms.append(vector.tensor_scalar(
                                    out=scrA[:, 384:432],
                                    in0=scrA[:, g * 48:(g + 1) * 48],
                                    scalar1=smalls[:, OHPA + g:OHPA + g + 1],
                                    scalar2=None, op0=ALU.mult))
                                ms.append(vector.tensor_scalar(
                                    out=scrA[:, 432:480],
                                    in0=scrA[:, g * 48:(g + 1) * 48],
                                    scalar1=smalls[:, OHPB + g:OHPB + g + 1],
                                    scalar2=None, op0=ALU.mult))
                                ms.append(vector.tensor_tensor(
                                    out=carry[:, 1168:1216],
                                    in0=carry[:, 1168:1216],
                                    in1=scrA[:, 384:432], op=ALU.add))
                                ms.append(vector.tensor_tensor(
                                    out=carry[:, 1216:1264],
                                    in0=carry[:, 1216:1264],
                                    in1=scrA[:, 432:480], op=ALU.add))
                            return ms
                        hexid = P.last()

                    @P.step("sync", deps=[INPJ], n_inst=0, n_dma=1)
                    def ZPARK(scalar, q=q):
                        return ("dma", scalar.dma_start(
                            out=park_zs.ap()[:, q * 8192:(q + 1) * 8192],
                            in_=bufC[:]))
                    zpark_id = P.last()

                    P.phase = "mm_conv"
                    csils = [None, None]
                    for dhi in range(DHI):
                        cdeps = [csils[dhi], ievs[dhi]]
                        if dhi == 0 and hexid is not None:
                            cdeps.append(hexid)

                        @P.step("vector", deps=cdeps, n_inst=9)
                        def CONVV(vector, l=l, dhi=dhi, q=q, ci=ci):
                            if q % 2 == 0:
                                lead = carry[:, 1168 + ci * 48 + dhi * 3:
                                             1168 + ci * 48 + dhi * 3 + 3]
                            else:
                                lead = carry[:, 1120 + dhi * 3:1120 + dhi * 3 + 3]
                            ms = [vector.tensor_copy(out=scrG[:, 0:3], in_=lead),
                                  vector.tensor_copy(
                                      out=scrG[:, 3:515],
                                      in_=bufA[:, dhi * 512:(dhi + 1) * 512])]
                            for k in range(4):
                                wc = soff[f"cw{l}_{k}"][0]
                                ms.append(vector.tensor_scalar(
                                    out=scrD[:, k * 512:(k + 1) * 512],
                                    in0=scrG[:, k:k + 512],
                                    scalar1=smalls[:, wc + dhi:wc + dhi + 1],
                                    scalar2=None, op0=ALU.mult))
                            ms.append(vector.tensor_tensor(
                                out=scrE[:, 0:512], in0=scrD[:, 0:512],
                                in1=scrD[:, 512:1024], op=ALU.add))
                            ms.append(vector.tensor_tensor(
                                out=scrE[:, 512:1024], in0=scrD[:, 1024:1536],
                                in1=scrD[:, 1536:2048], op=ALU.add))
                            ms.append(vector.tensor_tensor(
                                out=scrE[:, 1024 + (dhi % 2) * 512:
                                         1024 + (dhi % 2) * 512 + 512],
                                in0=scrE[:, 0:512],
                                in1=scrE[:, 512:1024], op=ALU.add))
                            return ms
                        convv = P.last()

                        @P.step("scalar", deps=[convv])
                        def CSIL(scalar, dhi=dhi, cbc=cbc):
                            return scalar.activation(
                                out=bufB[:, dhi * 512:(dhi + 1) * 512],
                                in_=scrE[:, 1024 + (dhi % 2) * 512:
                                         1024 + (dhi % 2) * 512 + 512],
                                func=AF.Silu,
                                bias=smalls[:, cbc + dhi:cbc + dhi + 1])
                        csils.append(P.last())
                        prev_c = csils[-1]
                    CONVD = prev_c

                    if q % 2 == 0:
                        @P.step("vector", deps=[CONVD])
                        def QTL(vector):
                            return vector.tensor_copy(
                                out=carry[:, 1120:1168].rearrange(
                                    "p (dhi t) -> p dhi t", t=3),
                                in_=bufA[:].rearrange("p (dhi t) -> p dhi t",
                                                      t=512)[:, :, 509:512])
                        CONVD = P.last()

                    P.phase = "mm_xdt"

                    @P.step("sync", deps=[CONVD, zpark_id], n_inst=0, n_dma=1)
                    def XWLD(sync, l=l):
                        return ("dma", sync.dma_start(
                            out=wstr[:, 0:1536], in_=wvt(f"xp{l}", 0)))
                    xwld = P.last()

                    @P.step("tensor", deps=[xwld], n_inst=3)
                    def XMM(tensor):
                        outs = []
                        for (o0, o1, pp) in ((0, 64, ps_o[0:64, :]),
                                             (64, 80, ps_p[0:16, :]),
                                             (80, 96, ps_q[0:16, :])):
                            m = None
                            for fc in range(16):
                                m = tensor.matmul(
                                    pp, wstr[:, fc * 96 + o0:fc * 96 + o1],
                                    bufB[:, fc * 512:(fc + 1) * 512],
                                    start=(fc == 0), stop=(fc == 15))
                            outs.append(m)
                        return outs
                    xmm = P.last()

                    @P.step("scalar", deps=[xmm], n_inst=3)
                    def XEV(scalar):
                        return [scalar.activation(out=scrF[0:64, 0:512],
                                                  in_=ps_o[0:64, :], func=AF.Copy),
                                scalar.activation(out=scrG[0:16, 0:512],
                                                  in_=ps_p[0:16, :], func=AF.Copy),
                                scalar.activation(out=scrG[0:16, 512:1024],
                                                  in_=ps_q[0:16, :], func=AF.Copy)]
                    xev = P.last()

                    @P.step("sync", deps=[xev], n_inst=0, n_dma=2)
                    def BCST(scalar, q=q):
                        return [("dma", scalar.dma_start(
                            out=bc_dram.ap()[:, q * 1024:q * 1024 + 512],
                            in_=scrG[0:16, 0:512])),
                            ("dma", scalar.dma_start(
                                out=bc_dram.ap()[:, q * 1024 + 512:q * 1024 + 1024],
                                in_=scrG[0:16, 512:1024]))]
                    bcst = P.last()

                    @P.step("sync", deps=[bcst], n_inst=0, n_dma=1)
                    def DWLD(sync, l=l):
                        return ("dma", sync.dma_start(
                            out=wstr[0:64, 0:2048], in_=dtv_l[l][:, :]))
                    prev_dt = P.last()
                    dvrels = []

                    for g4 in range(4):
                        @P.step("tensor", deps=[prev_dt])
                        def DMM(tensor, g4=g4):
                            m = None
                            for k in range(4):
                                dhi = g4 * 4 + k
                                m = tensor.matmul(
                                    ps_big[:, k * 512:(k + 1) * 512],
                                    wstr[0:64, dhi * 128:(dhi + 1) * 128],
                                    scrF[0:64, 0:512], start=True, stop=True)
                            return m
                        dmm = P.last()

                        @P.step("scalar", deps=[dmm], n_inst=4)
                        def DABS(scalar, g4=g4, dtbc=dtbc):
                            ms = []
                            for k in range(4):
                                dhi = g4 * 4 + k
                                ms.append(scalar.activation(
                                    out=scrA[:, k * 512:(k + 1) * 512],
                                    in_=ps_big[:, k * 512:(k + 1) * 512],
                                    func=AF.Abs,
                                    bias=smalls[:, dtbc + dhi:dtbc + dhi + 1]))
                            return ms
                        dabs = P.last()

                        @P.step("scalar", deps=[dabs], n_inst=4)
                        def DEXP(scalar):
                            return [scalar.activation(
                                out=scrA[:, k * 512:(k + 1) * 512],
                                in_=scrA[:, k * 512:(k + 1) * 512],
                                func=AF.Exp, scale=smalls[:, CM1:CM1 + 1])
                                for k in range(4)]
                        dexp = P.last()

                        @P.step("scalar", deps=[dexp], n_inst=4)
                        def DLN(scalar):
                            return [scalar.activation(
                                out=scrD[:, k * 512:(k + 1) * 512],
                                in_=scrA[:, k * 512:(k + 1) * 512],
                                func=AF.Ln, bias=smalls[:, CONE:CONE + 1])
                                for k in range(4)]
                        dln = P.last()

                        @P.step("vector", deps=[dln], n_inst=8)
                        def DVREL(vector, g4=g4, dtbc=dtbc):
                            ms = []
                            for k in range(4):
                                dhi = g4 * 4 + k
                                ms.append(vector.tensor_scalar(
                                    out=bufA[:, dhi * 512:(dhi + 1) * 512],
                                    in0=ps_big[:, k * 512:(k + 1) * 512],
                                    scalar1=smalls[:, dtbc + dhi:dtbc + dhi + 1],
                                    scalar2=0.0, op0=ALU.add, op1=ALU.max))
                            for k in range(4):
                                dhi = g4 * 4 + k
                                ms.append(vector.tensor_tensor(
                                    out=bufA[:, dhi * 512:(dhi + 1) * 512],
                                    in0=bufA[:, dhi * 512:(dhi + 1) * 512],
                                    in1=scrD[:, k * 512:(k + 1) * 512],
                                    op=ALU.add))
                            return ms
                        prev_dt = P.last()
                        dvrels.append(prev_dt)

                        @P.step("sync", deps=[prev_dt], n_inst=0, n_dma=1)
                        def DTPK(sync, q=q, g4=g4):
                            return ("dma", sync.dma_start(
                                out=park_dt.ap()[:, q * 8192 + g4 * 2048:
                                                 q * 8192 + (g4 + 1) * 2048],
                                in_=bufA[:, g4 * 2048:(g4 + 1) * 2048]))

                    prev_scan = dvrels[-1]

                    P.phase = "mm_scan"
                    for dhi in range(DHI):
                        sdeps = [prev_scan]
                        if dhi % 4 == 0:
                            sdeps.append(dvrels[dhi // 4])

                        @P.step("vector", deps=sdeps, n_inst=3)
                        def DTXC(vector, dhi=dhi, ci=ci, q=q):
                            init = (0.0 if q % 2 == 0 else
                                    cumST[:, ci * 16 + dhi:ci * 16 + dhi + 1])
                            return [
                                vector.tensor_tensor(
                                    out=scrF[:, 0:512],
                                    in0=bufA[:, dhi * 512:(dhi + 1) * 512],
                                    in1=bufB[:, dhi * 512:(dhi + 1) * 512],
                                    op=ALU.mult),
                                vector.tensor_tensor_scan(
                                    out=scrF[:, 512:1024], data0=ones_bf[:],
                                    data1=bufA[:, dhi * 512:(dhi + 1) * 512],
                                    initial=init, op0=ALU.mult, op1=ALU.add),
                                vector.tensor_copy(
                                    out=cumST[:, ci * 16 + dhi:ci * 16 + dhi + 1],
                                    in_=scrF[:, 1023:1024])]
                        dtxc = P.last()

                        vsteps = [dtxc, dtxc]
                        accs = [dtxc, dtxc]
                        for n in range(16):
                            half = n % 2
                            pb = half * 1024
                            fs = 1024 + half * 512

                            @P.step("tensor", deps=[vsteps[n], accs[n]])
                            def BCB(tensor, n=n, pb=pb):
                                tensor.matmul(ps_big[:, pb:pb + 512],
                                              sel_view(n), scrG[0:16, 0:512],
                                              start=True, stop=True)
                                return tensor.matmul(
                                    ps_big[:, pb + 512:pb + 1024],
                                    sel_view(n), scrG[0:16, 512:1024],
                                    start=True, stop=True)
                            bcb = P.last()

                            @P.step("scalar", deps=[vsteps[n]])
                            def AEXP(scalar, dhi=dhi, n=n, half=half):
                                nn = dhi * 16 + n
                                return scalar.activation(
                                    out=abuf[:, half * 512:half * 512 + 512],
                                    in_=bufA[:, dhi * 512:(dhi + 1) * 512],
                                    func=AF.Exp, scale=negA[:, nn:nn + 1])
                            aexp = P.last()

                            es = (n % 4) * 512
                            vdeps = [bcb, aexp]
                            if n >= 4:
                                vdeps.append(accs[n - 2])

                            @P.step("vector", deps=vdeps, n_inst=4)
                            def VSTEP(vector, dhi=dhi, ci=ci, n=n, q=q,
                                      half=half, pb=pb, fs=fs, es=es):
                                sc = ci * 256 + dhi * 16 + n
                                init = (0.0 if q % 2 == 0 else
                                        state[:, sc:sc + 1])
                                return [
                                    vector.tensor_tensor(
                                        out=scrF[:, fs:fs + 512],
                                        in0=scrF[:, 0:512],
                                        in1=ps_big[:, pb:pb + 512],
                                        op=ALU.mult),
                                    vector.tensor_tensor_scan(
                                        out=scrE[:, es:es + 512],
                                        data0=abuf[:, half * 512:
                                                   half * 512 + 512],
                                        data1=scrF[:, fs:fs + 512],
                                        initial=init, op0=ALU.mult,
                                        op1=ALU.add),
                                    vector.tensor_copy(
                                        out=state[:, sc:sc + 1],
                                        in_=scrE[:, es + 511:es + 512]),
                                    vector.tensor_tensor(
                                        out=scrE[:, es:es + 512],
                                        in0=scrE[:, es:es + 512],
                                        in1=ps_big[:, pb + 512:pb + 1024],
                                        op=ALU.mult)]
                            vstep = P.last()
                            vsteps.append(vstep)

                            @P.step("gpsimd", deps=[vstep])
                            def PACC(gpsimd, n=n, es=es):
                                slot = n % 4
                                dst = scrD[:, slot * 512:(slot + 1) * 512]
                                if n < 4:
                                    return gpsimd.tensor_copy(
                                        out=dst, in_=scrE[:, es:es + 512])
                                return gpsimd.tensor_tensor(
                                    out=dst, in0=dst, in1=scrE[:, es:es + 512],
                                    op=ALU.add)
                            accs.append(P.last())

                        @P.step("vector", deps=[accs[-1], accs[-2]], n_inst=6)
                        def YSG(vector, dhi=dhi, dskc=dskc):
                            return [
                                vector.tensor_tensor(
                                    out=scrF[:, 0:512], in0=scrD[:, 0:512],
                                    in1=scrD[:, 512:1024], op=ALU.add),
                                vector.tensor_tensor(
                                    out=scrF[:, 512:1024], in0=scrD[:, 1024:1536],
                                    in1=scrD[:, 1536:2048], op=ALU.add),
                                vector.tensor_tensor(
                                    out=scrF[:, 0:512], in0=scrF[:, 0:512],
                                    in1=scrF[:, 512:1024], op=ALU.add),
                                vector.tensor_scalar(
                                    out=scrF[:, 1536:2048],
                                    in0=bufB[:, dhi * 512:(dhi + 1) * 512],
                                    scalar1=smalls[:, dskc + dhi:dskc + dhi + 1],
                                    scalar2=None, op0=ALU.mult),
                                vector.tensor_tensor(
                                    out=scrF[:, 0:512], in0=scrF[:, 0:512],
                                    in1=scrF[:, 1536:2048], op=ALU.add),
                                vector.tensor_tensor(
                                    out=bufC[:, dhi * 512:(dhi + 1) * 512],
                                    in0=scrF[:, 0:512],
                                    in1=bufC[:, dhi * 512:(dhi + 1) * 512],
                                    op=ALU.mult)]
                        prev_scan = P.last()
                    SCAND = prev_scan

                    P.phase = "mm_oproj"
                    omms = []
                    oadds = []
                    for dc in range(DC):
                        wb = wstr[:, 0:2048] if dc % 2 == 0 else bufA[:, 0:2048]
                        pp = ps_o if dc % 2 == 0 else ps_p
                        eng = ("sync", "gpsimd")[dc % 2]
                        wdeps = [omms[dc - 2]] if dc >= 2 else [SCAND]

                        @P.step(eng, deps=wdeps, n_inst=0, n_dma=1)
                        def OWLD(eh, l=l, dc=dc, wb=wb):
                            return ("dma", eh.dma_start(out=wb,
                                                        in_=wvt(f"op{l}", dc)))
                        owld = P.last()

                        mdeps = [owld] + ([oadds[dc - 2]] if dc >= 2 else [SCAND])

                        @P.step("tensor", deps=mdeps)
                        def OMM(tensor, wb=wb, pp=pp):
                            m = None
                            for dhi in range(DHI):
                                m = tensor.matmul(
                                    pp[:, :], wb[:, dhi * 128:(dhi + 1) * 128],
                                    bufC[:, dhi * 512:(dhi + 1) * 512],
                                    start=(dhi == 0), stop=(dhi == DHI - 1))
                            return m
                        omms.append(P.last())

                        @P.step("vector", deps=[omms[-1]])
                        def OADD(vector, dc=dc, t0q=t0q, pp=pp):
                            return vector.tensor_tensor(
                                out=r_sb[:, dc * T + t0q: dc * T + t0q + 512],
                                in0=r_sb[:, dc * T + t0q: dc * T + t0q + 512],
                                in1=pp[:, :], op=ALU.add)
                        oadds.append(P.last())
                    prev_q = oadds[-1]

                    if q % 2 == 1:
                        @P.step("vector", deps=[prev_q], n_inst=17)
                        def SCPG(vector, ci=ci):
                            ms = [vector.tensor_copy(
                                out=carry[:, 512 + ci * 256:512 + (ci + 1) * 256],
                                in_=state[:, ci * 256:(ci + 1) * 256])]
                            for dhi in range(DHI):
                                ms.append(vector.tensor_scalar(
                                    out=scrA[:, dhi * 16:(dhi + 1) * 16],
                                    in0=negA[:, dhi * 16:(dhi + 1) * 16],
                                    scalar1=cumST[:, ci * 16 + dhi:
                                                  ci * 16 + dhi + 1],
                                    scalar2=None, op0=ALU.mult))
                            return ms
                        scpg = P.last()

                        @P.step("scalar", deps=[scpg], n_inst=16)
                        def PEXP(scalar, ci=ci):
                            return [scalar.activation(
                                out=carry[:, ci * 256 + dhi * 16:
                                          ci * 256 + dhi * 16 + 16],
                                in_=scrA[:, dhi * 16:(dhi + 1) * 16],
                                func=AF.Exp) for dhi in range(DHI)]
                        prev_q = P.last()

                P.phase = "mm_comb"

                @P.step("sync", deps=[prev_q], n_inst=0, n_dma=1)
                def AGBW(sync):
                    return ("dma", sync.dma_start(
                        out=agb_loc.ap().rearrange("a (p c) -> (a p) c", c=1024),
                        in_=carry[:, 0:1024]))
                agbw = P.last()

                @P.step("gpsimd", deps=[agbw])
                def AGB(gpsimd, l=l):
                    return gpsimd.collective_compute(
                        "AllGather", ALU.bypass,
                        replica_groups=[[0, 1, 2, 3], [4, 5, 6, 7]],
                        ins=[agb_loc.ap().opt()], outs=[agb_all[l].ap().opt()])
                agb = P.last()

                @P.step("vector", deps=[agb], n_inst=2)
                def CMB0(vector):
                    return [vector.memset(scrB[:, 0:256], 0.0),
                            vector.memset(hin[:, 0:512], 0.0)]
                cprev = P.last()

                for k in range(1, 8):
                    g = k - 1
                    rk = g if g < 4 else 7 - g
                    sl = 0 if g < 4 else 1

                    @P.step("sync", deps=[cprev], n_inst=0, n_dma=2)
                    def CLDPS(sync, l=l, rk=rk, sl=sl):
                        v = agb_all[l].ap()[rk:rk + 1, :].rearrange(
                            "r (p c) -> (r p) c", c=1024)
                        return [("dma", sync.dma_start(
                            out=scrA[:, 0:256],
                            in_=v[:, sl * 256:(sl + 1) * 256])),
                            ("dma", sync.dma_start(
                                out=scrA[:, 256:512],
                                in_=v[:, 512 + sl * 256:512 + (sl + 1) * 256]))]
                    cld = P.last()

                    @P.step("vector", deps=[cld], n_inst=6)
                    def CMB(vector, k=k):
                        return [
                            vector.tensor_tensor(out=scrB[:, 0:256],
                                                 in0=scrB[:, 0:256],
                                                 in1=scrA[:, 0:256], op=ALU.mult),
                            vector.tensor_tensor(out=scrB[:, 0:256],
                                                 in0=scrB[:, 0:256],
                                                 in1=scrA[:, 256:512], op=ALU.add),
                            vector.tensor_scalar(
                                out=scrA[:, 512:768], in0=scrB[:, 0:256],
                                scalar1=smalls[:, OHA + k:OHA + k + 1],
                                scalar2=None, op0=ALU.mult),
                            vector.tensor_scalar(
                                out=scrA[:, 768:1024], in0=scrB[:, 0:256],
                                scalar1=smalls[:, OHB + k:OHB + k + 1],
                                scalar2=None, op0=ALU.mult),
                            vector.tensor_tensor(out=hin[:, 0:256],
                                                 in0=hin[:, 0:256],
                                                 in1=scrA[:, 512:768], op=ALU.add),
                            vector.tensor_tensor(out=hin[:, 256:512],
                                                 in0=hin[:, 256:512],
                                                 in1=scrA[:, 768:1024],
                                                 op=ALU.add)]
                    cprev = P.last()

                cqprev = cprev

                P.phase = "mm_corr"
                for q in range(4):
                    t0q = q * 512
                    ci = q // 2

                    @P.step("sync", deps=[cqprev], n_inst=0, n_dma=4)
                    def CRLD(sync, q=q):
                        return [
                            ("dma", sync.dma_start(
                                out=bufA[:],
                                in_=park_dt.ap()[:, q * 8192:(q + 1) * 8192])),
                            ("dma", sync.dma_start(
                                out=bufC[:],
                                in_=park_zs.ap()[:, q * 8192:(q + 1) * 8192])),
                            ("dma", sync.dma_start(
                                out=scrG[0:16, 0:512],
                                in_=bc_dram.ap()[:, q * 1024:q * 1024 + 512])),
                            ("dma", sync.dma_start(
                                out=scrG[0:16, 512:1024],
                                in_=bc_dram.ap()[:, q * 1024 + 512:
                                                 q * 1024 + 1024]))]
                    prev_cd = P.last()

                    for dhi in range(DHI):
                        @P.step("vector", deps=[prev_cd], n_inst=2)
                        def CCUM(vector, dhi=dhi, ci=ci, q=q):
                            init = (0.0 if q % 2 == 0 else
                                    cumST[:, ci * 16 + dhi:ci * 16 + dhi + 1])
                            return [
                                vector.tensor_tensor_scan(
                                    out=scrF[:, 512:1024], data0=ones_bf[:],
                                    data1=bufA[:, dhi * 512:(dhi + 1) * 512],
                                    initial=init, op0=ALU.mult, op1=ALU.add),
                                vector.tensor_copy(
                                    out=cumST[:, ci * 16 + dhi:ci * 16 + dhi + 1],
                                    in_=scrF[:, 1023:1024])]
                        ccum = P.last()

                        cvs = [ccum, ccum]
                        cbhs = [ccum, ccum]
                        for n in range(16):
                            half = n % 2
                            pb = half * 1024
                            fs = 1024 + half * 512

                            @P.step("tensor", deps=[cvs[n], cbhs[n]])
                            def CCB(tensor, n=n, pb=pb):
                                return tensor.matmul(
                                    ps_big[:, pb:pb + 512], sel_view(n),
                                    scrG[0:16, 512:1024], start=True, stop=True)
                            ccb = P.last()

                            @P.step("scalar", deps=[cvs[n]])
                            def CEX(scalar, dhi=dhi, n=n, half=half):
                                nn = dhi * 16 + n
                                return scalar.activation(
                                    out=scrE[:, 1024 + half * 512:
                                             1024 + half * 512 + 512],
                                    in_=scrF[:, 512:1024], func=AF.Exp,
                                    scale=negA[:, nn:nn + 1])
                            cex = P.last()

                            @P.step("scalar", deps=[ccb, cvs[n]])
                            def CBH(scalar, n=n, ci=ci, dhi=dhi, half=half,
                                    pb=pb):
                                hc = ci * 256 + dhi * 16 + n
                                return scalar.activation(
                                    out=scrE[:, half * 512:half * 512 + 512],
                                    in_=ps_big[:, pb:pb + 512],
                                    func=AF.Identity,
                                    scale=hin[:, hc:hc + 1])
                            cbhs.append(P.last())

                            @P.step("vector", deps=[cex, cbhs[-1]], n_inst=2)
                            def CV(vector, n=n, half=half, fs=fs):
                                m1 = vector.tensor_tensor(
                                    out=scrF[:, fs:fs + 512],
                                    in0=scrE[:, fs:fs + 512],
                                    in1=scrE[:, half * 512:half * 512 + 512],
                                    op=ALU.mult)
                                if n == 0:
                                    m2 = vector.tensor_copy(
                                        out=scrD[:, 0:512],
                                        in_=scrF[:, fs:fs + 512])
                                else:
                                    m2 = vector.tensor_tensor(
                                        out=scrD[:, 0:512], in0=scrD[:, 0:512],
                                        in1=scrF[:, fs:fs + 512], op=ALU.add)
                                return [m1, m2]
                            cvs.append(P.last())
                        cps = [cvs[-1]]

                        @P.step("vector", deps=[cps[-1]])
                        def CGATE(vector, dhi=dhi):
                            return vector.tensor_tensor(
                                out=bufC[:, dhi * 512:(dhi + 1) * 512],
                                in0=scrD[:, 0:512],
                                in1=bufC[:, dhi * 512:(dhi + 1) * 512],
                                op=ALU.mult)
                        prev_cd = P.last()

                    P.phase = "mm_coproj"
                    omms2 = []
                    oadds2 = []
                    for dc in range(DC):
                        wb = wstr[:, 0:2048] if dc % 2 == 0 else bufB[:, 0:2048]
                        pp = ps_o if dc % 2 == 0 else ps_p
                        eng = ("sync", "gpsimd")[dc % 2]
                        wdeps = [omms2[dc - 2]] if dc >= 2 else [prev_cd]

                        @P.step(eng, deps=wdeps, n_inst=0, n_dma=1)
                        def OWLD2(eh, l=l, dc=dc, wb=wb):
                            return ("dma", eh.dma_start(out=wb,
                                                        in_=wvt(f"op{l}", dc)))
                        owld2 = P.last()

                        mdeps = [owld2] + ([oadds2[dc - 2]] if dc >= 2
                                           else [prev_cd])

                        @P.step("tensor", deps=mdeps)
                        def OMM2(tensor, wb=wb, pp=pp):
                            m = None
                            for dhi in range(DHI):
                                m = tensor.matmul(
                                    pp[:, :], wb[:, dhi * 128:(dhi + 1) * 128],
                                    bufC[:, dhi * 512:(dhi + 1) * 512],
                                    start=(dhi == 0), stop=(dhi == DHI - 1))
                            return m
                        omms2.append(P.last())

                        @P.step("vector", deps=[omms2[-1]])
                        def OADD2(vector, dc=dc, t0q=t0q, pp=pp):
                            return vector.tensor_tensor(
                                out=r_sb[:, dc * T + t0q: dc * T + t0q + 512],
                                in0=r_sb[:, dc * T + t0q: dc * T + t0q + 512],
                                in1=pp[:, :], op=ALU.add)
                        oadds2.append(P.last())
                    prev_omm2 = oadds2[-1]
                    cqprev = prev_omm2
                prev = cqprev

            P.phase = "mm_fin"
            lnf = ln_stream(r_sb, [prev],
                            ln_apply(r_sb, r_sb, gamma="nfw", beta="nfb"))
            radd = lnf[-1]
            for tt in range(4):
                for grp in range(2):
                    @P.step("sync", deps=[radd], n_inst=0, n_dma=4)
                    def RLD(sync, tt=tt, grp=grp):
                        ms = []
                        for k in range(4):
                            dc = grp * 4 + k
                            ms.append(("dma", sync.dma_start(
                                out=scrA[:, k * 512:(k + 1) * 512],
                                in_=park_r.ap()[:, dc * T + tt * 512:
                                                dc * T + tt * 512 + 512])))
                        return ms
                    rld = P.last()

                    @P.step("vector", deps=[rld], n_inst=4)
                    def RADDF(vector, tt=tt, grp=grp):
                        ms = []
                        for k in range(4):
                            dc = grp * 4 + k
                            ms.append(vector.tensor_tensor(
                                out=r_sb[:, dc * T + tt * 512:
                                         dc * T + tt * 512 + 512],
                                in0=r_sb[:, dc * T + tt * 512:
                                         dc * T + tt * 512 + 512],
                                in1=scrA[:, k * 512:(k + 1) * 512], op=ALU.add))
                        return ms
                    radd = P.last()
            return radd

        if BUILD_UPTO >= 3:
            SUBL3 = mamba_block(SUBL2)
        else:
            SUBL3 = SUBL2

        if BUILD_UPTO >= 4:
            P.phase = "ffn2"
            ln4_ids = ln_stream(r_sb, [SUBL3], ln_apply(r_sb, xn_sb))
            SUBL4 = ffn_block(ln4_ids[-1], pre_dep=SUBL3)
        else:
            SUBL4 = SUBL3

        FINAL = SUBL4

        P.phase = "out"

        @P.step("sync", deps=[FINAL], n_inst=0, n_dma=1)
        def OUT(sync):
            return ("dma", sync.dma_start(
                out=out_ext[:].rearrange("(dc p) t -> p dc t", p=128),
                in_=r_sb[:].rearrange("p (dc t) -> p dc t", t=T)))
        outid = P.last()

        @P.step("sync", deps=[outid])
        def FIN(sync):
            return sync.nop()

        with nc.Block() as block:
            P.emit(block, sems, nc)

    return nc



def kernel(**inputs):
    flat, woff = _weight_layout(inputs)
    smalls, soff = _smalls_layout(inputs)
    S = flat.size
    meta = {"S": S, "NSM": smalls.shape[1], "woff": woff, "soff": soff}

    x = inputs["L0_attn_a_x"]
    wslices = flat.reshape(8, S // 8)

    in_maps = []
    for c in range(8):
        b = c // 4
        jj = c % 4
        cA, cB = jj, 7 - jj
        xloc = np.concatenate([x[b, cA * CH:(cA + 1) * CH],
                               x[b, cB * CH:(cB + 1) * CH]], axis=0)
        in_maps.append({
            "xin": np.ascontiguousarray(xloc.T),
            "wsl": wslices[c:c + 1],
            "smalls": _core_smalls(smalls, soff, jj),
            "rope": _rope_dup((cA, cB)),
            "dyn": np.array([[jj, 0, 0, 0, 0, 0, 0, 0]], np.int32),
        })

    nc = build_program(meta)
    res = run_bass_kernel_spmd(nc, in_maps, core_ids=list(range(8)))

    out = np.zeros((B, TG, D), np.float32)
    for c in range(8):
        b = c // 4
        jj = c % 4
        cA, cB = jj, 7 - jj
        o = np.asarray(res.results[c]["out"])
        o = o.reshape(DC, 128, T).transpose(2, 0, 1).reshape(T, D)
        out[b, cA * CH:(cA + 1) * CH] = o[0:CH]
        out[b, cB * CH:(cB + 1) * CH] = o[CH:]
    return out
